# version 13
# baseline (speedup 1.0000x reference)
import numpy as np
import ml_dtypes
import concourse.bass as bass
import concourse.mybir as mybir
from concourse.bass_utils import run_bass_kernel_spmd
from concourse.bass_core_rust import IndirectOffsetOnAxis

F32 = mybir.dt.float32
BF16 = mybir.dt.bfloat16
I32 = mybir.dt.int32
AF = mybir.ActivationFunctionType
ALU = mybir.AluOpType
AX = mybir.AxisListType

D = 2048
DC = 16
H = 16
HD = 128
R = 16
NE = 8
MI = 1408
MIC = MI // 128
SI = 5632
SIC = SI // 128
SCALE = HD ** -0.5
EPS = 1e-5
BF = ml_dtypes.bfloat16


class Ctr:
    def __init__(self, sem, step=1):
        self.sem = sem
        self.step = step
        self.n = 0

    def inc(self, ins):
        ins.then_inc(self.sem, self.step)
        self.n += self.step
        return self.n

    def wait(self, eng, val=None):
        if val is None:
            val = self.n
        if val > 0:
            eng.wait_ge(self.sem, val)


def wge(eng, sem, val):
    if val > 0:
        eng.wait_ge(sem, val)


class VV:
    _PASS = ("wait_ge", "sem_inc", "drain", "nop", "engine_nop")

    def __init__(self, mk, v):
        self._mk = mk
        self._v = v

    def __getattr__(self, name):
        fn = getattr(self._v, name)
        if name in VV._PASS or name.startswith("_"):
            return fn

        def run(*a, **k):
            mk = self._mk
            if mk.dve.n:
                self._v.wait_ge(mk.dve.sem, mk.dve.n)
            ins = fn(*a, **k)
            mk.dve.inc(ins)
            return ins
        return run


class MK:
    def __init__(self, nc: bass.Bass, L: int, debug=()):
        assert L % 512 == 0
        self.nc = nc
        self.L = L
        self.T = L // 4
        self.TG = self.T // 512
        self.TC = self.T // 128
        self.KT = L // 128
        self.debug = set(debug)
        assert self.TG <= 4
        self.CAP = ((3 * self.T // 8 + 127) // 128) * 128
        self.NSL = NE * self.CAP
        self.TRASH = self.NSL
        self.PMW = self.NSL // 128 + 1
        self.SCH = []
        off = 0
        while off < self.CAP:
            n = min(512, self.CAP - off)
            self.SCH.append((off, n))
            off += n

        d = {}
        din = lambda name, shape, dt: nc.dram_tensor(name, shape, dt, kind="ExternalInput").ap()
        self.xt_in = din("xt_in", [D, L], F32)
        for p in "qkvo":
            d[f"w{p}T"] = din(f"w{p}T", [DC * 128, DC * 128], F32)
            d[f"b{p}T_a"] = din(f"b{p}T_a", [128, DC * R], F32)
            d[f"a{p}T"] = din(f"a{p}T", [R, D], F32)
        d["bq_a"] = din("bq_a", [128, DC], F32)
        d["bk_a"] = din("bk_a", [128, DC], F32)
        d["bo_a"] = din("bo_a", [128, DC], F32)
        d["bv_row"] = din("bv_row", [1, D], F32)
        d["gw_arr"] = din("gw_arr", [128, DC * 9], F32)
        d["wguT"] = din("wguT", [(2 * SI // 128) * 128, D], F32)
        d["wdownT"] = din("wdownT", [DC * 128, SIC * 128], F32)
        d["w13T"] = din("w13T", [(NE * 2 * MI // 128) * 128, D], F32)
        d["w2T"] = din("w2T", [NE * MI, D], F32)
        d["ident_b"] = din("ident_b", [128, 128], BF16)
        d["ident_f"] = din("ident_f", [128, 128], F32)
        d["ones_col_b"] = din("ones_col_b", [128, 1], BF16)
        d["ones_row_b"] = din("ones_row_b", [1, 128], BF16)
        d["ones_row_f"] = din("ones_row_f", [1, 128], F32)
        d["tie64"] = din("tie64", [128, 64], F32)
        d["sel8"] = din("sel8", [NE, NE * 128], F32)
        d["tri8"] = din("tri8", [NE, NE], F32)
        d["ones8"] = din("ones8", [NE, 1], F32)
        d["segb"] = din("segb", [NE, 1], F32)
        d["tokid"] = din("tokid", [128, self.TC], F32)
        d["trashf"] = din("trashf", [NE, self.T], F32)
        self.inp = d

        self.y = nc.dram_tensor("y", [self.T, D], F32, kind="ExternalOutput").ap()
        self.dbg = {}
        for name, shape, dt in [
            ("dbg_q", [D, self.T], BF16), ("dbg_k", [4 * D, self.T], BF16), ("dbg_v", [4 * D, self.T], BF16),
            ("dbg_o", [128, DC * self.T], BF16), ("dbg_x2", [D, self.T], F32),
            ("dbg_xn2", [128, DC * self.T], BF16),
            ("dbg_w", [NE, self.T], F32), ("dbg_sg", [1, self.T], F32),
            ("dbg_hsh", [self.TG * SI, 512], BF16),
            ("dbg_x3", [self.T, D], F32),
            ("dbg_slotmap", [128, self.PMW], F32), ("dbg_wg", [self.NSL + 128, 1], F32),
            ("dbg_contrib", [self.NSL + 128, D], F32),
        ]:
            if name in self.debug:
                self.dbg[name] = nc.dram_tensor(name, shape, dt, kind="ExternalOutput").ap()

        dsc = lambda name, shape, dt: nc.dram_tensor(name, shape, dt, kind="Internal").ap()
        self.kag_in = dsc("kag_in", [D, self.T], BF16)
        self.vag_in = dsc("vag_in", [D, self.T], BF16)
        self.kd = dsc("kd", [4 * D, self.T], BF16)
        self.vd = dsc("vd", [4 * D, self.T], BF16)
        self.qd = dsc("qd", [D, self.T], BF16)
        self.x2d = dsc("x2d", [D, self.T], F32)
        self.x3d = dsc("x3d", [self.T, D], F32)
        self.hshd = dsc("hshd", [self.TG * SI, 512], BF16)
        self.xn2tm = dsc("xn2tm", [self.T, D], BF16)
        self.slotpm = dsc("slotpm", [128 * self.PMW, 1], F32)
        self.wgflat = dsc("wgflat", [self.NSL + 128, 1], F32)
        self.contrib = dsc("contrib", [self.NSL + 128, D], F32)

    def build(self):
        nc = self.nc
        from contextlib import ExitStack
        with ExitStack() as ctx:
            e = ctx.enter_context
            self.psh = [e(nc.psum_tensor(f"ps{i}", [128, 512], F32)) for i in range(6)]
            self.psTh = e(nc.psum_tensor("psT", [128, 1024], BF16))
            self.ps7h = e(nc.psum_tensor("ps7", [128, 512], F32))
            self.ps = [p[:, :] for p in self.psh]
            self.psT = self.psTh[:, :]
            self.banks = self.ps + [self.psT.bitcast(F32), self.ps7h[:, :]]
            self.psT2 = self.ps7h[:, :].bitcast(BF16)

            self.ld = [Ctr(e(nc.semaphore(f"s_ld{i}")), 16) for i in range(2)]
            self.ldS = Ctr(e(nc.semaphore("s_ldS")), 16)
            self.st4 = [Ctr(e(nc.semaphore(f"s_st{i}")), 16) for i in range(4)]
            self.wl = [Ctr(e(nc.semaphore(f"s_wl{i}")), 16) for i in range(2)]
            self.wlS = Ctr(e(nc.semaphore("s_wlS")), 16)
            self.sc = Ctr(e(nc.semaphore("s_sc")), 16)
            self.scl = Ctr(e(nc.semaphore("s_scl")), 16)
            self.gl = [Ctr(e(nc.semaphore(f"s_gl{i}")), 16) for i in range(2)]
            self.cc = Ctr(e(nc.semaphore("s_cc")), 1)
            self.pe = Ctr(e(nc.semaphore("s_pe")))
            self.act = Ctr(e(nc.semaphore("s_act")))
            self.dve = Ctr(e(nc.semaphore("s_dve")))
            self.bar = e(nc.semaphore("s_bar"))
            self.nbar = 0

            sb = lambda name, shape, dt: e(nc.sbuf_tensor(name, shape, dt))
            self.c_bq = sb("c_bq", [128, DC], F32)
            self.c_bk = sb("c_bk", [128, DC], F32)
            self.c_bo = sb("c_bo", [128, DC], F32)
            self.c_bv = sb("c_bv", [1, D], BF16)
            self.c_BT = {p: sb(f"c_BT{p}", [128, DC * R], BF16) for p in "qkvo"}
            self.c_gw = sb("c_gw", [128, DC * 9], BF16)
            self.c_idb = sb("c_idb", [128, 128], BF16)
            self.c_idf = sb("c_idf", [128, 128], F32)
            self.c_ocb = sb("c_ocb", [128, 1], BF16)
            self.c_orb = sb("c_orb", [1, 128], BF16)
            self.c_orf = sb("c_orf", [1, 128], F32)
            self.c_tie = sb("c_tie", [128, 64], F32)
            self.c_sel = sb("c_sel", [NE, NE * 128], F32)
            self.sg_fm = sb("sg_fm", [1, self.T], F32)
            self.c_eps = sb("c_eps", [1, 1], F32)
            self.c_tri8 = sb("c_tri8", [NE, NE], F32)
            self.c_ones8 = sb("c_ones8", [NE, 1], F32)
            self.c_segb = sb("c_segb", [NE, 1], F32)
            self.c_tokid = sb("c_tokid", [128, self.TC], F32)
            self.slA_i = sb("slA_i", [128, self.TC], I32)
            self.slB_i = sb("slB_i", [128, self.TC], I32)
            self.c_trash = sb("c_trash", [NE, self.T], F32)

            self.block = e(nc.Block())
            self._emit()
        return nc

    _stats_n = 0

    def _alloc_stats(self, e):
        nc = self.nc
        MK._stats_n += 1
        sfx = f"_{MK._stats_n}"
        self.s1 = e(nc.sbuf_tensor("s1" + sfx, [1, self.T], F32))
        self.s2 = e(nc.sbuf_tensor("s2" + sfx, [1, self.T], F32))
        self.rstd = e(nc.sbuf_tensor("rstd" + sfx, [1, self.T], F32))
        self.Msb = e(nc.sbuf_tensor("Msb" + sfx, [128, self.T], BF16))
        self.Rsb = e(nc.sbuf_tensor("Rsb" + sfx, [128, self.T], BF16))

    def barrier(self):
        self.nbar += 1
        k = 5 * self.nbar
        blk = self.block

        @blk.tensor
        def _(t):
            t.drain()
            t.sem_inc(self.bar, 1)
            t.wait_ge(self.bar, k)

        @blk.scalar
        def _(s):
            s.drain()
            s.sem_inc(self.bar, 1)
            s.wait_ge(self.bar, k)

        @blk.vector
        def _(v):
            v = VV(self, v)
            v.drain()
            v.sem_inc(self.bar, 1)
            v.wait_ge(self.bar, k)

        @blk.sync
        def _(sp):
            for c in self.ld:
                c.wait(sp)
            self.ldS.wait(sp)
            for c in self.st4:
                c.wait(sp)
            sp.sem_inc(self.bar, 1)
            sp.wait_ge(self.bar, k)

        @blk.gpsimd
        def _(g):
            for c in self.wl:
                c.wait(g)
            self.wlS.wait(g)
            self.sc.wait(g)
            for c in self.gl:
                c.wait(g)
            g.sem_inc(self.bar, 1)
            g.wait_ge(self.bar, k)

    def block_load(self, g, dst_sb_ap, src, block, ctr):
        ctr.inc(g.dma_start(out=dst_sb_ap, in_=src[block * 128:(block + 1) * 128, :]))

    def _emit(self):
        from contextlib import ExitStack
        nc = self.nc
        self.p0_consts()
        self.barrier()
        with ExitStack() as s1:
            e = s1.enter_context
            xb = e(nc.sbuf_tensor("xb_g", [128, DC * self.T], BF16))
            xn = xb
            sq = e(nc.sbuf_tensor("sq_g", [128, 2 * self.T], BF16))
            t1b = e(nc.sbuf_tensor("t1b", [128, self.T], F32))
            wcol = e(nc.sbuf_tensor("wcol", [128, 4 * DC * 128], BF16))
            qcol = e(nc.sbuf_tensor("qcol", [128, 2 * DC * 128], BF16))
            ev = e(nc.sbuf_tensor("ev_p1", [128, 4 * 512], BF16))
            self._alloc_stats(e)
            self.c_AT = {p: e(nc.sbuf_tensor(f"c_AT{p}", [R, D], BF16)) for p in "qkv"}
            self.xBq = e(nc.sbuf_tensor("xBq", [R, self.T], BF16))
            self.xBg = {p: e(nc.sbuf_tensor(f"xBg{p}", [R, self.T], BF16)) for p in "kv"}
            blk = self.block

            @blk.gpsimd
            def _(g_):
                for p in "qkv":
                    self.wlS.inc(g_.dma_start(out=self.c_AT[p][:, :], in_=self.inp[f"a{p}T"]))
            self.p1_group(0, xb, xn, sq, t1b, wcol, qcol, ev)
            self.p1b_allgather()
            self.barrier()
        with ExitStack() as s2:
            e = s2.enter_context
            ofm = e(nc.sbuf_tensor("ofm", [128, DC * self.T], BF16))
            with ExitStack() as s2a:
                ee = s2a.enter_context
                kh = ee(nc.sbuf_tensor("kh", [128, 2 * self.L], BF16))
                vraw = ee(nc.sbuf_tensor("vraw", [128, self.L], BF16))
                vext = ee(nc.sbuf_tensor("vext", [128, 2 * (self.KT * 129)], BF16))
                qh = ee(nc.sbuf_tensor("qh", [128, 2 * self.T], BF16))
                pt = ee(nc.sbuf_tensor("pt", [128, 3 * 512], BF16))
                otmp = ee(nc.sbuf_tensor("otmp", [128, 128], BF16))
                rcp = ee(nc.sbuf_tensor("rcp", [128, 4], F32))
                self.p2_attention(ofm, kh, vraw, vext, qh, pt, otmp, rcp)
                self.barrier()
            with ExitStack() as s2b:
                ee = s2b.enter_context
                wcol = ee(nc.sbuf_tensor("wcol3", [128, 2 * DC * 128], BF16))
                xtile = ee(nc.sbuf_tensor("xtile3", [128, 2 * 512], F32))
                x2t = ee(nc.sbuf_tensor("x2t3", [128, 2 * 512], F32))
                self.c_ATo = ee(nc.sbuf_tensor("c_ATo", [R, D], BF16))
                self.xBo = ee(nc.sbuf_tensor("xBo", [R, self.T], BF16))
                self.p3_oproj(ofm, wcol, xtile, x2t)
                self.barrier()
        with ExitStack() as s3:
            e = s3.enter_context
            xn2 = e(nc.sbuf_tensor("xn2", [128, DC * self.T], BF16))
            self.w_fm = e(nc.sbuf_tensor("w_fm", [NE, self.T], F32))
            with ExitStack() as s3a:
                ee = s3a.enter_context
                sq = ee(nc.sbuf_tensor("sq2", [128, 2 * self.T], BF16))
                t1b = ee(nc.sbuf_tensor("t1b2", [128, self.T], F32))
                self._alloc_stats(ee)
                self.p4_ln2(xn2, sq, t1b)
                self.barrier()
            with ExitStack() as s3z:
                ee = s3z.enter_context
                xstg = ee(nc.sbuf_tensor("xstg", [128, 2 * D], BF16))
                self.p4b_xn2tm(xn2, xstg)
                self.barrier()
            with ExitStack() as s3b:
                ee = s3b.enter_context
                pr = ee(nc.sbuf_tensor("pr", [128, 8], F32))
                rsc = ee(nc.sbuf_tensor("rsc", [128, 24], F32))
                rs2 = ee(nc.sbuf_tensor("rs2", [128, 64], F32))
                rs3 = ee(nc.sbuf_tensor("rs3", [128, 64], F32))
                self.p5_router(xn2, pr, rsc, rs2, rs3)
                self.barrier()
            with ExitStack() as s3i:
                ee = s3i.enter_context
                self.p5b_index(ee)
                self.barrier()
            with ExitStack() as s3c:
                ee = s3c.enter_context
                wg = ee(nc.sbuf_tensor("wg6", [128, 2 * DC * 128], BF16))
                wu = ee(nc.sbuf_tensor("wu6", [128, 2 * DC * 128], BF16))
                sgb = ee(nc.sbuf_tensor("sgb6", [128, 2 * 512], BF16))
                tml = ee(nc.sbuf_tensor("tml6", [128, 512], F32))
                ht = ee(nc.sbuf_tensor("ht6", [128, 2 * 512], BF16))
                self.p6_up(xn2, wg, wu, sgb, tml, ht, None)
                self.barrier()
        with ExitStack() as s4b:
            e = s4b.enter_context
            self.p6b_sparse(e)
            self.barrier()
        with ExitStack() as s4:
            e = s4.enter_context
            hsb = e(nc.sbuf_tensor("hsb7", [128, SIC * 512], BF16))
            wd = e(nc.sbuf_tensor("wd7", [128, 2 * SIC * 128], BF16))
            sgbc = e(nc.sbuf_tensor("sgbc", [128, self.T], F32))
            x2t = e(nc.sbuf_tensor("x2t7", [128, 2 * 512], F32))
            x3t = e(nc.sbuf_tensor("x3t7", [128, 2 * 512], F32))
            tml = e(nc.sbuf_tensor("tml7", [128, 2 * 512], F32))
            x3s = e(nc.sbuf_tensor("x3s7", [128, 4 * D], F32))
            self.p7_shared_down(hsb, wd, sgbc, x2t, x3t, tml, x3s)
            self.barrier()
        with ExitStack() as s5:
            e = s5.enter_context
            cA = e(nc.sbuf_tensor("cA8", [128, 2 * D], F32))
            cB = e(nc.sbuf_tensor("cB8", [128, 2 * D], F32))
            x3r = e(nc.sbuf_tensor("x3r8", [128, 2 * D], F32))
            yt = e(nc.sbuf_tensor("yt8", [128, 2 * D], F32))
            self.p8_combine(cA, cB, x3r, yt)
        self.barrier()
        self.final_wait()

    def p0_consts(self):
        blk = self.block
        inp = self.inp

        @blk.vector
        def _(v):
            v = VV(self, v)
            v.memset(self.c_eps[0:1, 0:1], EPS)

        @blk.sync
        def _(sp):
            for dst, src in [
                (self.c_bq, inp["bq_a"]), (self.c_bk, inp["bk_a"]), (self.c_bo, inp["bo_a"]),
                (self.c_idb, inp["ident_b"]), (self.c_idf, inp["ident_f"]),
                (self.c_ocb, inp["ones_col_b"]), (self.c_orb, inp["ones_row_b"]),
                (self.c_orf, inp["ones_row_f"]), (self.c_tie, inp["tie64"]),
                (self.c_sel, inp["sel8"]), (self.c_tri8, inp["tri8"]),
                (self.c_ones8, inp["ones8"]), (self.c_segb, inp["segb"]),
                (self.c_tokid, inp["tokid"]), (self.c_trash, inp["trashf"]),
            ]:
                self.ldS.inc(sp.dma_start(out=dst[:, :], in_=src))

        @blk.gpsimd
        def _(g):
            self.wl[0].inc(g.dma_start(out=self.c_bv[:, :], in_=inp["bv_row"]))
            self.wl[0].inc(g.dma_start(out=self.c_gw[:, :], in_=inp["gw_arr"]))
            for p in "qkvo":
                self.wl[0].inc(g.dma_start(out=self.c_BT[p][:, :], in_=inp[f"b{p}T_a"]))

    def _ln_core(self, xb, sq, t1b, xn, src_sem, src_val):
        blk = self.block
        T, TG = self.T, self.TG
        pe0, act0, dve0 = self.pe.n, self.act.n, self.dve.n

        @blk.scalar
        def _(s):
            s.wait_ge(src_sem, src_val)
            for dc in range(DC):
                if dc >= 2:
                    wge(s, self.pe.sem, pe0 + (dc - 1))
                self.act.inc(s.activation(sq[:, (dc % 2) * T:(dc % 2) * T + T],
                                          xb[:, dc * T:(dc + 1) * T], AF.Square))

        @blk.tensor
        def _(t):
            for dc in range(DC):
                t.wait_ge(self.act.sem, act0 + dc + 1)
                first, last = dc == 0, dc == DC - 1
                ins = None
                for tg in range(TG):
                    t.matmul(self.banks[tg][0:1, 0:512], self.c_ocb[:, 0:1],
                             xb[:, dc * T + tg * 512: dc * T + tg * 512 + 512],
                             start=first, stop=last)
                    ins = t.matmul(self.banks[TG + tg][0:1, 0:512], self.c_ocb[:, 0:1],
                                   sq[:, (dc % 2) * T + tg * 512:(dc % 2) * T + tg * 512 + 512],
                                   start=first, stop=last)
                self.pe.inc(ins)

        @blk.vector
        def _(v):
            v = VV(self, v)
            v.wait_ge(self.pe.sem, pe0 + DC)
            for tg in range(TG):
                v.tensor_copy(self.s1[0:1, tg * 512:tg * 512 + 512], self.banks[tg][0:1, 0:512])
                v.tensor_copy(self.s2[0:1, tg * 512:tg * 512 + 512], self.banks[TG + tg][0:1, 0:512])
            v.tensor_scalar(self.s1[0:1, :], self.s1[0:1, :], 1.0 / D, None, ALU.mult)
            v.tensor_scalar(self.s2[0:1, :], self.s2[0:1, :], 1.0 / D, None, ALU.mult)
            v.tensor_tensor(self.rstd[0:1, :], self.s1[0:1, :], self.s1[0:1, :], ALU.mult)
            v.tensor_tensor(self.s2[0:1, :], self.s2[0:1, :], self.rstd[0:1, :],
                            ALU.subtract)

        @blk.scalar
        def _(s):
            s.wait_ge(self.dve.sem, dve0 + 2 * TG + 4)
            self.act.inc(s.activation(self.s2[0:1, :], self.s2[0:1, :], AF.Sqrt,
                                      bias=self.c_eps[0:1, 0:1]))

        @blk.vector
        def _(v):
            v = VV(self, v)
            v.wait_ge(self.act.sem, act0 + DC + 1)
            v.reciprocal(self.rstd[0:1, :], self.s2[0:1, :])

        @blk.tensor
        def _(t):
            t.wait_ge(self.dve.sem, dve0 + 2 * TG + 5)
            for tg in range(TG):
                if tg > 0:
                    t.wait_ge(self.dve.sem, dve0 + 2 * TG + 5 + 2 * tg)
                ins = t.matmul(self.ps[0], self.c_orf[0:1, :],
                               self.s1[0:1, tg * 512:tg * 512 + 512], start=True, stop=True)
                self.pe.inc(ins)
                ins = t.matmul(self.ps[1], self.c_orf[0:1, :],
                               self.rstd[0:1, tg * 512:tg * 512 + 512], start=True, stop=True)
                self.pe.inc(ins)

        @blk.vector
        def _(v):
            v = VV(self, v)
            for tg in range(TG):
                v.wait_ge(self.pe.sem, pe0 + DC + 2 * tg + 1)
                v.tensor_copy(self.Msb[:, tg * 512:tg * 512 + 512], self.ps[0])
                v.wait_ge(self.pe.sem, pe0 + DC + 2 * tg + 2)
                v.tensor_copy(self.Rsb[:, tg * 512:tg * 512 + 512], self.ps[1])
            for dc in range(DC):
                v.tensor_tensor(t1b[:, 0:T], xb[:, dc * T:(dc + 1) * T],
                                self.Msb[:, 0:T], ALU.subtract)
                v.tensor_tensor(xn[:, dc * T:(dc + 1) * T],
                                t1b[:, 0:T], self.Rsb[:, 0:T], ALU.mult)
        return self.dve.n

    def p1_group(self, g, xb, xn, sq, t1b, wcol, qcol, ev):
        blk = self.block
        T, TG, TC = self.T, self.TG, self.TC
        mine = True

        @blk.gpsimd
        def _(gp):
            self.wlS.inc(gp.dma_start(
                out=xb[:, 0:DC * T].rearrange("p (dc t) -> p dc t", t=T),
                in_=self.xt_in[:, 0:T].rearrange("(dc p) t -> p dc t", p=128)))
        wl_xb = self.wlS.n

        dve_norm = self._ln_core(xb, sq, t1b, xn, self.wlS.sem, wl_xb)

        pe_xb0 = self.pe.n
        act_xb0 = self.act.n
        projs = ["k", "v"] + (["q"] if mine else [])
        xBdst = {"k": self.xBg["k"], "v": self.xBg["v"], "q": self.xBq}

        @blk.tensor
        def _(t):
            t.wait_ge(self.dve.sem, dve_norm)
            t.wait_ge(self.wlS.sem, wl_xb)
            for i, p in enumerate(projs):
                for tg in range(TG):
                    wge(t, self.act.sem, act_xb0 + i * TG + tg)
                    ins = None
                    for dc in range(DC):
                        ins = t.matmul(self.ps[i][0:R, 0:512],
                                       self.c_BT[p][:, dc * R:(dc + 1) * R],
                                       xn[:, dc * T + tg * 512: dc * T + tg * 512 + 512],
                                       start=(dc == 0), stop=(dc == DC - 1))
                    self.pe.inc(ins)

        @blk.scalar
        def _(s):
            for i, p in enumerate(projs):
                for tg in range(TG):
                    s.wait_ge(self.pe.sem, pe_xb0 + i * TG + tg + 1)
                    self.act.inc(s.activation(xBdst[p][0:R, tg * 512:tg * 512 + 512],
                                              self.ps[i][0:R, 0:512], AF.Copy, scale=0.5))
        act_ev0 = self.act.n
        pe_proj0 = self.pe.n
        assert pe_proj0 == pe_xb0 + len(projs) * TG

        nproj = len(projs)
        per_oc = TG + TC + (TG if mine else 0)
        wl_b = [self.wl[0].n, self.wl[1].n]
        st_b = [c.n for c in self.st4]

        def gidx(oc, cls, j):
            off = {"k": 0, "v": TG, "q": TG + TC}[cls]
            return oc * per_oc + off + j

        @blk.gpsimd
        def _(gp):
            for oc in range(DC):
                par = oc % 2
                if oc >= 2:
                    wge(gp, self.pe.sem, pe_proj0 + (oc - 1) * per_oc)
                self.block_load(gp, wcol[:, (2 * par) * DC * 128:(2 * par + 1) * DC * 128],
                                self.inp["wkT"], oc, self.wl[par])
                self.block_load(gp, wcol[:, (2 * par + 1) * DC * 128:(2 * par + 2) * DC * 128],
                                self.inp["wvT"], oc, self.wl[par])
                if mine:
                    self.block_load(gp, qcol[:, par * DC * 128:(par + 1) * DC * 128],
                                    self.inp["wqT"], oc, self.wl[par])

        @blk.tensor
        def _(t):
            t.wait_ge(self.act.sem, act_ev0)
            for oc in range(DC):
                par = oc % 2
                t.wait_ge(self.wl[par].sem, wl_b[par] + 16 * nproj * (oc // 2 + 1))
                kw = wcol[:, (2 * par) * DC * 128:(2 * par) * DC * 128 + DC * 128]
                vw = wcol[:, (2 * par + 1) * DC * 128:(2 * par + 1) * DC * 128 + DC * 128]
                for tg in range(TG):
                    i = oc * TG + tg
                    bank = self.ps[i % 2]
                    if i >= 2:
                        oc2, tg2 = divmod(i - 2, TG)
                        wge(t, self.act.sem, act_ev0 + gidx(oc2, "k", tg2) + 1)
                    for dc in range(DC):
                        t.matmul(bank, kw[:, dc * 128:(dc + 1) * 128],
                                 xn[:, dc * T + tg * 512: dc * T + tg * 512 + 512],
                                 start=(dc == 0), stop=False)
                    ins = t.matmul(bank, self.c_AT["k"][0:R, oc * 128:(oc + 1) * 128],
                                   self.xBg["k"][0:R, tg * 512:tg * 512 + 512],
                                   start=False, stop=True)
                    self.pe.inc(ins)
                for tc in range(TC):
                    i = oc * TC + tc
                    bank = self.ps[2 + (i % 2)]
                    if i >= 2:
                        oc2, tc2 = divmod(i - 2, TC)
                        wge(t, self.act.sem, act_ev0 + gidx(oc2, "v", tc2) + 1)
                    for dc in range(DC):
                        t.matmul(bank[:, 0:128], xn[:, dc * T + tc * 128: dc * T + tc * 128 + 128],
                                 vw[:, dc * 128:(dc + 1) * 128], start=(dc == 0), stop=False)
                    t.matmul(bank[:, 0:128], self.xBg["v"][0:R, tc * 128:tc * 128 + 128],
                             self.c_AT["v"][0:R, oc * 128:(oc + 1) * 128], start=False, stop=False)
                    ins = t.matmul(bank[:, 0:128], self.c_orb[0:1, :],
                                   self.c_bv[0:1, oc * 128:(oc + 1) * 128], start=False, stop=True)
                    self.pe.inc(ins)
                if mine:
                    qw = qcol[:, par * DC * 128: par * DC * 128 + DC * 128]
                    for tg in range(TG):
                        i = oc * TG + tg
                        bank = self.ps[4 + (i % 2)]
                        if i >= 2:
                            oc2, tg2 = divmod(i - 2, TG)
                            wge(t, self.act.sem, act_ev0 + gidx(oc2, "q", tg2) + 1)
                        for dc in range(DC):
                            t.matmul(bank, qw[:, dc * 128:(dc + 1) * 128],
                                     xn[:, dc * T + tg * 512: dc * T + tg * 512 + 512],
                                     start=(dc == 0), stop=False)
                        ins = t.matmul(bank, self.c_AT["q"][0:R, oc * 128:(oc + 1) * 128],
                                       self.xBq[0:R, tg * 512:tg * 512 + 512],
                                       start=False, stop=True)
                        self.pe.inc(ins)

        @blk.scalar
        def _(s):
            for oc in range(DC):
                for tg in range(TG):
                    gi = gidx(oc, "k", tg)
                    s.wait_ge(self.pe.sem, pe_proj0 + gi + 1)
                    wge(s, self.st4[gi % 4].sem, st_b[gi % 4] + 16 * (gi // 4))
                    self.act.inc(s.activation(ev[:, (gi % 4) * 512:(gi % 4) * 512 + 512],
                                              self.ps[(oc * TG + tg) % 2], AF.Identity,
                                              bias=self.c_bk[:, oc:oc + 1]))
                for tc in range(TC):
                    gi = gidx(oc, "v", tc)
                    s.wait_ge(self.pe.sem, pe_proj0 + gi + 1)
                    wge(s, self.st4[gi % 4].sem, st_b[gi % 4] + 16 * (gi // 4))
                    self.act.inc(s.activation(ev[:, (gi % 4) * 512:(gi % 4) * 512 + 128],
                                              self.ps[2 + ((oc * TC + tc) % 2)][:, 0:128],
                                              AF.Copy))
                if mine:
                    for tg in range(TG):
                        gi = gidx(oc, "q", tg)
                        s.wait_ge(self.pe.sem, pe_proj0 + gi + 1)
                        wge(s, self.st4[gi % 4].sem, st_b[gi % 4] + 16 * (gi // 4))
                        self.act.inc(s.activation(ev[:, (gi % 4) * 512:(gi % 4) * 512 + 512],
                                                  self.ps[4 + ((oc * TG + tg) % 2)], AF.Identity,
                                                  bias=self.c_bq[:, oc:oc + 1]))

        @blk.sync
        def _(sp):
            for oc in range(DC):
                for tg in range(TG):
                    gi = gidx(oc, "k", tg)
                    sp.wait_ge(self.act.sem, act_ev0 + gi + 1)
                    self.st4[gi % 4].inc(sp.dma_start(
                        out=self.kag_in[oc * 128:(oc + 1) * 128,
                                        tg * 512: tg * 512 + 512],
                        in_=ev[:, (gi % 4) * 512:(gi % 4) * 512 + 512]))
                for tc in range(TC):
                    gi = gidx(oc, "v", tc)
                    sp.wait_ge(self.act.sem, act_ev0 + gi + 1)
                    self.st4[gi % 4].inc(sp.dma_start(
                        out=self.vag_in[oc * 128:(oc + 1) * 128, tc * 128:tc * 128 + 128],
                        in_=ev[:, (gi % 4) * 512:(gi % 4) * 512 + 128]))
                if mine:
                    for tg in range(TG):
                        gi = gidx(oc, "q", tg)
                        sp.wait_ge(self.act.sem, act_ev0 + gi + 1)
                        self.st4[gi % 4].inc(sp.dma_start(
                            out=self.qd[oc * 128:(oc + 1) * 128, tg * 512:tg * 512 + 512],
                            in_=ev[:, (gi % 4) * 512:(gi % 4) * 512 + 512]))

        if mine and "dbg_q" in self.dbg:
            @blk.sync
            def _(sp):
                for c in self.st4:
                    c.wait(sp)
                self.st4[0].inc(sp.dma_start(out=self.dbg["dbg_q"], in_=self.qd))

    NCH = 8

    def p1b_allgather(self):
        blk = self.block
        rg = [[0, 1, 2, 3], [4, 5, 6, 7]]
        NCH = self.NCH
        hb = H // NCH
        rows = hb * 128

        @blk.gpsimd
        def _(g):
            for c in self.st4:
                c.wait(g)
            for c in range(NCH):
                self.cc.inc(g.collective_compute(
                    "AllGather", ALU.bypass, replica_groups=rg,
                    ins=[self.kag_in[c * rows:(c + 1) * rows, :]],
                    outs=[self.kd[c * 4 * rows:(c + 1) * 4 * rows, :]]))
                self.cc.inc(g.collective_compute(
                    "AllGather", ALU.bypass, replica_groups=rg,
                    ins=[self.vag_in[c * rows:(c + 1) * rows, :]],
                    outs=[self.vd[c * 4 * rows:(c + 1) * 4 * rows, :]]))

        if "dbg_k" in self.dbg:
            @blk.sync
            def _(sp):
                sp.wait_ge(self.cc.sem, self.cc.n)
                self.st4[0].inc(sp.dma_start(out=self.dbg["dbg_k"], in_=self.kd))
                self.st4[0].inc(sp.dma_start(out=self.dbg["dbg_v"], in_=self.vd))

    def p2_attention(self, ofm, kh, vraw, vext, qh, pt, otmp, rcp):
        blk = self.block
        T, L, KT = self.T, self.L, self.KT

        @blk.vector
        def _(v):
            v = VV(self, v)
            for b in range(2):
                ve = vext[:, b * (KT * 129):(b + 1) * (KT * 129)]
                v.memset(
                    ve.rearrange("p (kt j) -> p kt j", j=129)[:, :, 128:129], 1.0)
        dve_pre = self.dve.n

        cc_base = self.cc.n - 2 * self.NCH

        def kd_row(h, r):
            hb = H // self.NCH
            c, hh = divmod(h, hb)
            return c * 4 * hb * 128 + r * hb * 128 + hh * 128

        @blk.sync
        def _(sp):
            sp.wait_ge(self.cc.sem, cc_base + 2)
            self.ld[0].inc(sp.dma_start(out=qh[:, 0:T], in_=self.qd[0:128, :]))
            for r in range(4):
                self.ld[0].inc(sp.dma_start(out=kh[:, r * T:(r + 1) * T],
                                            in_=self.kd[kd_row(0, r): kd_row(0, r) + 128, :]))
                self.ldS.inc(sp.dma_start(out=vraw[:, r * T:(r + 1) * T],
                                          in_=self.vd[kd_row(0, r): kd_row(0, r) + 128, :]))
        self._cc_base = cc_base
        self._kd_row = kd_row

        prev = dict(pe_end=self.pe.n, dve_end=self.dve.n, inter_done=dve_pre,
                    ld_qk=self.ld[0].n, ld_v=self.ldS.n)
        for h in range(H):
            prev = self._p2_head(h, ofm, kh, vraw, vext, qh, pt, otmp, rcp, prev)

        if "dbg_o" in self.dbg:
            @blk.sync
            def _(sp):
                sp.wait_ge(self.dve.sem, self.dve.n)
                self.st4[0].inc(sp.dma_start(out=self.dbg["dbg_o"], in_=ofm[:, 0:DC * T]))

    def _p2_head(self, h, ofm, kh, vraw, vext, qh, pt, otmp, rcp, prev):
        blk = self.block
        T, L, KT = self.T, self.L, self.KT
        QG = T // 512
        par = h % 2
        pe0, act0, dve0 = self.pe.n, self.act.n, self.dve.n
        PEQG = 2 * KT + 4
        DVQG = 12

        @blk.vector
        def _(v):
            v = VV(self, v)
            v.wait_ge(self.ldS.sem, prev["ld_v"])
            if h >= 1:
                v.wait_ge(self.pe.sem, prev["pe_end"])
            ve = vext[:, par * (KT * 129):par * (KT * 129) + KT * 129]
            v.tensor_copy(
                ve.rearrange("p (kt j) -> p kt j", j=129)[:, :, 0:128],
                vraw[:, :].rearrange("p (kt j) -> p kt j", j=128))
        inter_done = dve0 + 1

        ld_qk_next, ld_v_next = prev["ld_qk"], prev["ld_v"]
        if h + 1 < H:
            @blk.sync
            def _(sp):
                npar = (h + 1) % 2
                if h >= 1:
                    sp.wait_ge(self.pe.sem, prev["pe_end"])
                hb = H // self.NCH
                sp.wait_ge(self.cc.sem, self._cc_base + 2 * ((h + 1) // hb + 1))
                self.ld[npar].inc(sp.dma_start(out=qh[:, npar * T:npar * T + T],
                                               in_=self.qd[(h + 1) * 128:(h + 2) * 128, :]))
                for r in range(4):
                    row = self._kd_row(h + 1, r)
                    self.ld[npar].inc(sp.dma_start(
                        out=kh[:, npar * L + r * T: npar * L + (r + 1) * T],
                        in_=self.kd[row: row + 128, :]))
                sp.wait_ge(self.dve.sem, inter_done)
                for r in range(4):
                    row = self._kd_row(h + 1, r)
                    self.ldS.inc(sp.dma_start(
                        out=vraw[:, r * T:(r + 1) * T],
                        in_=self.vd[row: row + 128, :]))
            ld_qk_next = self.ld[(h + 1) % 2].n
            ld_v_next = self.ldS.n

        @blk.tensor
        def _(t):
            t.wait_ge(self.ld[par].sem, prev["ld_qk"])
            t.wait_ge(self.dve.sem, inter_done)
            kb = kh[:, par * L:par * L + L]
            qb = qh[:, par * T:par * T + T]
            ve = vext[:, par * (KT * 129):par * (KT * 129) + KT * 129]
            def emit_scores(qg, kt):
                g = qg * KT + kt
                sb_ = [self.ps[0], self.ps[1], self.banks[7]][g % 3]
                wge(t, self.act.sem, act0 + g - 2)
                ins = t.matmul(sb_, kb[:, kt * 128:(kt + 1) * 128],
                               qb[:, qg * 512:qg * 512 + 512], start=True, stop=True)
                self.pe.inc(ins)

            for qg in range(QG):
                if qg == 0:
                    t.wait_ge(self.dve.sem, prev["dve_end"])
                else:
                    t.wait_ge(self.dve.sem, dve0 + 1 + DVQG * qg)
                emit_scores(qg, 0)
                emit_scores(qg, 1)
                for kt in range(KT):
                    if kt + 2 < KT:
                        emit_scores(qg, kt + 2)
                    t.wait_ge(self.act.sem, act0 + qg * KT + kt + 1)
                    pb = pt[:, ((qg * KT + kt) % 3) * 512:((qg * KT + kt) % 3) * 512 + 512]
                    ins = None
                    for qs in range(4):
                        ins = t.matmul(self.ps[2 + qs][:, 0:129],
                                       pb[:, qs * 128:(qs + 1) * 128],
                                       ve[:, kt * 129:(kt + 1) * 129],
                                       start=(kt == 0), stop=(kt == KT - 1))
                    self.pe.inc(ins)
                for qs in range(4):
                    t.wait_ge(self.dve.sem, dve0 + 1 + DVQG * qg + 3 * qs + 2)
                    ins = t.transpose(self.psT[:, 0:128], otmp[:, :], self.c_idb[:, :])
                    self.pe.inc(ins)
        pe_end = self.pe.n

        @blk.scalar
        def _(s):
            for qg in range(QG):
                for kt in range(KT):
                    off = 1 if kt == 0 else (2 if kt == 1 else 2 * kt - 1)
                    s.wait_ge(self.pe.sem, pe0 + qg * PEQG + off)
                    g = qg * KT + kt
                    sb_ = [self.ps[0], self.ps[1], self.banks[7]][g % 3]
                    pb = pt[:, (g % 3) * 512:(g % 3) * 512 + 512]
                    self.act.inc(s.activation(pb, sb_, AF.Exp, scale=SCALE))

        @blk.vector
        def _(v):
            v = VV(self, v)
            for qg in range(QG):
                for qs in range(4):
                    v.wait_ge(self.pe.sem, pe0 + qg * PEQG + 2 * KT)
                    ob = self.ps[2 + qs]
                    v.reciprocal(rcp[:, qs:qs + 1], ob[:, 128:129])
                    if qg > 0 or qs > 0:
                        pqg, pqs = (qg, qs - 1) if qs > 0 else (qg - 1, 3)
                        v.wait_ge(self.pe.sem, pe0 + pqg * PEQG + 2 * KT + pqs + 1)
                    v.tensor_scalar(otmp[:, :], ob[:, 0:128],
                                    rcp[:, qs:qs + 1], None, ALU.mult)
                    v.wait_ge(self.pe.sem, pe0 + qg * PEQG + 2 * KT + qs + 1)
                    v.tensor_copy(
                        ofm[:, h * T + qg * 512 + qs * 128: h * T + qg * 512 + qs * 128 + 128],
                        self.psT[:, 0:128])
        dve_end = self.dve.n

        return dict(pe_end=pe_end, dve_end=dve_end, inter_done=inter_done,
                    ld_qk=ld_qk_next, ld_v=ld_v_next)

    def p3_oproj(self, ofm, wcol, xtile, x2t):
        blk = self.block
        T, TG = self.T, self.TG

        @blk.gpsimd
        def _(gp):
            self.wlS.inc(gp.dma_start(out=self.c_ATo[:, :], in_=self.inp["aoT"]))
        wl_at = self.wlS.n

        pe0, act0, dve0 = self.pe.n, self.act.n, self.dve.n
        ld_b = [self.ld[0].n, self.ld[1].n]
        st_b = [self.st4[0].n, self.st4[1].n]
        wl_b = [self.wl[0].n, self.wl[1].n]

        @blk.tensor
        def _(t):
            for tg in range(TG):
                wge(t, self.act.sem, act0 + tg)
                ins = None
                for dc in range(DC):
                    ins = t.matmul(self.ps[2][0:R, 0:512],
                                   self.c_BT["o"][:, dc * R:(dc + 1) * R],
                                   ofm[:, dc * T + tg * 512: dc * T + tg * 512 + 512],
                                   start=(dc == 0), stop=(dc == DC - 1))
                self.pe.inc(ins)

        @blk.scalar
        def _(s):
            for tg in range(TG):
                s.wait_ge(self.pe.sem, pe0 + tg + 1)
                self.act.inc(s.activation(self.xBo[0:R, tg * 512:tg * 512 + 512],
                                          self.ps[2][0:R, 0:512], AF.Copy, scale=0.5))
        act_xbo = self.act.n

        @blk.gpsimd
        def _(gp):
            for oc in range(DC):
                par = oc % 2
                if oc >= 2:
                    wge(gp, self.pe.sem, pe0 + TG + (oc - 1) * TG)
                self.block_load(gp, wcol[:, par * DC * 128:(par + 1) * DC * 128],
                                self.inp["woT"], oc, self.wl[par])

        @blk.sync
        def _(sp):
            NI = DC * TG
            for i in range(NI + 2):
                if i >= 2:
                    j = i - 2
                    sp.wait_ge(self.dve.sem, dve0 + j + 1)
                    self.st4[j % 2].inc(sp.dma_start(
                        out=self.x2d[(j // TG) * 128:(j // TG + 1) * 128,
                                     (j % TG) * 512:(j % TG) * 512 + 512],
                        in_=x2t[:, (j % 2) * 512:(j % 2) * 512 + 512]))
                if i < NI:
                    par2 = i % 2
                    self.ld[par2].inc(sp.dma_start(
                        out=xtile[:, par2 * 512:par2 * 512 + 512],
                        in_=self.xt_in[(i // TG) * 128:(i // TG + 1) * 128,
                                       (i % TG) * 512:(i % TG) * 512 + 512]))

        @blk.tensor
        def _(t):
            t.wait_ge(self.act.sem, act_xbo)
            t.wait_ge(self.wlS.sem, wl_at)
            for oc in range(DC):
                par = oc % 2
                t.wait_ge(self.wl[par].sem, wl_b[par] + 16 * (oc // 2 + 1))
                ow = wcol[:, par * DC * 128:par * DC * 128 + DC * 128]
                for qg in range(TG):
                    i = oc * TG + qg
                    bank = self.ps[i % 2]
                    if i >= 2:
                        wge(t, self.dve.sem, dve0 + i - 1)
                    for dc in range(DC):
                        t.matmul(bank, ow[:, dc * 128:(dc + 1) * 128],
                                 ofm[:, dc * T + qg * 512: dc * T + qg * 512 + 512],
                                 start=(dc == 0), stop=False)
                    ins = t.matmul(bank, self.c_ATo[0:R, oc * 128:(oc + 1) * 128],
                                   self.xBo[0:R, qg * 512:qg * 512 + 512],
                                   start=False, stop=True)
                    self.pe.inc(ins)
        pe_op0 = pe0 + TG

        @blk.vector
        def _(v):
            v = VV(self, v)
            for oc in range(DC):
                for qg in range(TG):
                    i = oc * TG + qg
                    par2 = i % 2
                    v.wait_ge(self.pe.sem, pe_op0 + i + 1)
                    v.wait_ge(self.ld[par2].sem, ld_b[par2] + 16 * (i // 2 + 1))
                    wge(v, self.st4[par2].sem, st_b[par2] + 16 * (i // 2))
                    bank = self.ps[i % 2]
                    v.scalar_tensor_tensor(
                        x2t[:, par2 * 512:par2 * 512 + 512], bank,
                        self.c_bo[:, oc:oc + 1], xtile[:, par2 * 512:par2 * 512 + 512],
                        ALU.add, ALU.add)

        if "dbg_x2" in self.dbg:
            @blk.sync
            def _(sp):
                for c in self.st4:
                    c.wait(sp)
                self.st4[0].inc(sp.dma_start(out=self.dbg["dbg_x2"], in_=self.x2d))

    def p4_ln2(self, xn2, sq, t1b):
        blk = self.block

        @blk.gpsimd
        def _(gp):
            self.wlS.inc(gp.dma_start(
                out=xn2[:, 0:DC * self.T].rearrange("p (dc t) -> p dc t", t=self.T),
                in_=self.x2d.rearrange("(dc p) t -> p dc t", p=128)))
        wl_x = self.wlS.n

        self._ln_core(xn2, sq, t1b, xn2, self.wlS.sem, wl_x)

        if "dbg_xn2" in self.dbg:
            @blk.sync
            def _(sp):
                sp.wait_ge(self.dve.sem, self.dve.n)
                self.st4[0].inc(sp.dma_start(out=self.dbg["dbg_xn2"], in_=xn2[:, 0:DC * self.T]))

    def p5_router(self, xn2, pr, rsc, rs2, rs3):
        blk = self.block
        T, TC = self.T, self.TC
        for tc in range(TC):
            pe_tc0, act_tc0, dve_tc0 = self.pe.n, self.act.n, self.dve.n

            @blk.tensor
            def _(t):
                wge(t, self.dve.sem, dve_tc0)
                wge(t, self.act.sem, act_tc0)
                ins = None
                for dc in range(DC):
                    ins = t.matmul(self.ps[0][:, 0:9],
                                   xn2[:, dc * T + tc * 128: dc * T + tc * 128 + 128],
                                   self.c_gw[:, dc * 9:(dc + 1) * 9],
                                   start=(dc == 0), stop=(dc == DC - 1))
                self.pe.inc(ins)

            @blk.vector
            def _(v):
                v = VV(self, v)
                v.wait_ge(self.pe.sem, pe_tc0 + 1)
                v.reduce_max(rsc[:, 0:1], self.ps[0][:, 0:8], axis=AX.X)
                v.tensor_scalar(rsc[:, 1:2], rsc[:, 0:1], -1.0, None, ALU.mult)

            @blk.scalar
            def _(s):
                s.wait_ge(self.dve.sem, dve_tc0 + 2)
                self.act.inc(s.activation(pr[:, 0:8], self.ps[0][:, 0:8], AF.Exp,
                                          bias=rsc[:, 1:2]))
                self.act.inc(s.activation(rsc[:, 2:3], self.ps[0][:, 8:9], AF.Sigmoid))

            @blk.vector
            def _(v):
                v = VV(self, v)
                v.wait_ge(self.act.sem, act_tc0 + 1)
                v.reduce_sum(rsc[:, 3:4], pr[:, 0:8], axis=AX.X)
                v.reciprocal(rsc[:, 4:5], rsc[:, 3:4])
                v.tensor_scalar(pr[:, 0:8], pr[:, 0:8], rsc[:, 4:5], None, ALU.mult)
                prA = pr[:, 0:8].unsqueeze(2).broadcast_to([128, 8, 8])
                prB = pr[:, 0:8].unsqueeze(1).broadcast_to([128, 8, 8])
                v.tensor_tensor(rs2[:, 0:64], prA, prB, ALU.is_lt)
                v.tensor_tensor(rs3[:, 0:64], prA, prB, ALU.is_equal)
                v.tensor_tensor(rs3[:, 0:64], rs3[:, 0:64], self.c_tie[:, 0:64], ALU.mult)
                v.tensor_tensor(rs2[:, 0:64], rs2[:, 0:64], rs3[:, 0:64], ALU.add)
                v.tensor_reduce(rsc[:, 8:16], rs2[:, 0:64].rearrange("p (e j) -> p e j", j=8),
                                AX.X, ALU.add)
                v.tensor_scalar(rsc[:, 16:24], rsc[:, 8:16], 2.0, None, ALU.is_lt)
                v.tensor_tensor(pr[:, 0:8], pr[:, 0:8], rsc[:, 16:24], ALU.mult)

            @blk.tensor
            def _(t):
                t.wait_ge(self.dve.sem, dve_tc0 + 12)
                ins = t.transpose(self.ps[1][0:8, 0:128], pr[:, 0:8], self.c_idf[:, :])
                self.pe.inc(ins)
                t.wait_ge(self.act.sem, act_tc0 + 2)
                ins = t.transpose(self.ps[2][0:1, 0:128], rsc[:, 2:3], self.c_idf[:, :])
                self.pe.inc(ins)

            @blk.vector
            def _(v):
                v = VV(self, v)
                v.wait_ge(self.pe.sem, pe_tc0 + 3)
                v.tensor_copy(self.w_fm[0:NE, tc * 128:(tc + 1) * 128], self.ps[1][0:8, 0:128])
                v.tensor_copy(self.sg_fm[0:1, tc * 128:(tc + 1) * 128],
                              self.ps[2][0:1, 0:128])

        if "dbg_w" in self.dbg:
            @blk.sync
            def _(sp):
                sp.wait_ge(self.dve.sem, self.dve.n)
                self.st4[0].inc(sp.dma_start(out=self.dbg["dbg_w"], in_=self.w_fm[0:NE, 0:T]))
                self.st4[0].inc(sp.dma_start(out=self.dbg["dbg_sg"], in_=self.sg_fm[0:1, 0:T]))

    def p6_up(self, xn2, wg, wu, sgb, tml, ht, web):
        blk = self.block
        T, TG = self.T, self.TG
        segs = [("sh", SIC, None)]
        prev_dve_end = self.dve.n
        for name, nic, eidx in segs:
            pe0, act0, dve0 = self.pe.n, self.act.n, self.dve.n
            wl_b = [self.wl[0].n, self.wl[1].n]
            st_b = [self.st4[0].n, self.st4[1].n]
            expert = eidx is not None
            pe_pre = TG if expert else 0
            dve_pre = TG if expert else 0
            dve_per = 2 if expert else 1

            @blk.gpsimd
            def _(gp):
                gp.wait_ge(self.pe.sem, pe0)
                for ic in range(nic):
                    par = ic % 2
                    if ic >= 2:
                        wge(gp, self.pe.sem, pe0 + pe_pre + 2 * TG * (ic - 1))
                    if expert:
                        b0 = eidx * 2 * MIC + ic
                        b1 = eidx * 2 * MIC + MIC + ic
                        src = self.inp["w13T"]
                    else:
                        b0 = ic
                        b1 = SIC + ic
                        src = self.inp["wguT"]
                    self.block_load(gp, wg[:, par * DC * 128:(par + 1) * DC * 128],
                                    src, b0, self.wl[par])
                    self.block_load(gp, wu[:, par * DC * 128:(par + 1) * DC * 128],
                                    src, b1, self.wl[par])

            if expert:
                @blk.tensor
                def _(t):
                    t.wait_ge(self.dve.sem, prev_dve_end)
                    for tg in range(TG):
                        if tg > 0:
                            t.wait_ge(self.dve.sem, dve0 + tg)
                        ins = t.matmul(self.ps[4], self.c_sel[0:NE, eidx * 128:(eidx + 1) * 128],
                                       self.w_fm[0:NE, tg * 512:tg * 512 + 512],
                                       start=True, stop=True)
                        self.pe.inc(ins)

                @blk.vector
                def _(v):
                    v = VV(self, v)
                    for tg in range(TG):
                        v.wait_ge(self.pe.sem, pe0 + tg + 1)
                        v.tensor_copy(web[:, tg * 512:tg * 512 + 512], self.ps[4])

            pe1 = pe0 + pe_pre

            @blk.tensor
            def _(t):
                if not expert:
                    t.wait_ge(self.dve.sem, prev_dve_end)
                for ic in range(nic):
                    par = ic % 2
                    t.wait_ge(self.wl[par].sem, wl_b[par] + 32 * (ic // 2 + 1))
                    for tg in range(TG):
                        it = ic * TG + tg
                        par2 = it % 2
                        gbank = self.ps[0 + 2 * par2]
                        ubank = self.ps[1 + 2 * par2]
                        if it >= 2:
                            wge(t, self.act.sem, act0 + (it - 1))
                            wge(t, self.dve.sem, dve0 + dve_pre + dve_per * (it - 1))
                        ins = None
                        for dc in range(DC):
                            ins = t.matmul(gbank, wg[:, par * DC * 128 + dc * 128: par * DC * 128 + (dc + 1) * 128],
                                           xn2[:, dc * T + tg * 512: dc * T + tg * 512 + 512],
                                           start=(dc == 0), stop=(dc == DC - 1))
                        self.pe.inc(ins)
                        ins = None
                        for dc in range(DC):
                            ins = t.matmul(ubank, wu[:, par * DC * 128 + dc * 128: par * DC * 128 + (dc + 1) * 128],
                                           xn2[:, dc * T + tg * 512: dc * T + tg * 512 + 512],
                                           start=(dc == 0), stop=(dc == DC - 1))
                        self.pe.inc(ins)

            @blk.scalar
            def _(s):
                for ic in range(nic):
                    for tg in range(TG):
                        it = ic * TG + tg
                        par2 = it % 2
                        s.wait_ge(self.pe.sem, pe1 + 2 * it + 1)
                        if it >= 2:
                            wge(s, self.dve.sem, dve0 + dve_pre + dve_per * (it - 1))
                        gbank = self.ps[0 + 2 * par2]
                        self.act.inc(s.activation(sgb[:, par2 * 512:par2 * 512 + 512],
                                                  gbank, AF.Silu))

            @blk.vector
            def _(v):
                v = VV(self, v)
                for ic in range(nic):
                    for tg in range(TG):
                        it = ic * TG + tg
                        par2 = it % 2
                        v.wait_ge(self.pe.sem, pe1 + 2 * it + 2)
                        v.wait_ge(self.act.sem, act0 + it + 1)
                        wge(v, self.st4[par2].sem, st_b[par2] + 16 * (it // 2))
                        ubank = self.ps[1 + 2 * par2]
                        if expert:
                            v.tensor_tensor(tml[:, 0:512],
                                            sgb[:, par2 * 512:par2 * 512 + 512],
                                            ubank, ALU.mult)
                            v.tensor_tensor(ht[:, par2 * 512:par2 * 512 + 512],
                                            tml[:, 0:512],
                                            web[:, tg * 512:tg * 512 + 512], ALU.mult)
                        else:
                            v.tensor_tensor(ht[:, par2 * 512:par2 * 512 + 512],
                                            sgb[:, par2 * 512:par2 * 512 + 512],
                                            ubank, ALU.mult)

            @blk.sync
            def _(sp):
                for ic in range(nic):
                    for tg in range(TG):
                        it = ic * TG + tg
                        par2 = it % 2
                        sp.wait_ge(self.dve.sem, dve0 + dve_pre + dve_per * (it + 1))
                        if expert:
                            rows0 = tg * NE * MI + eidx * MI + ic * 128
                            dst = self.hexd[rows0:rows0 + 128, 0:512]
                        else:
                            rows0 = tg * SI + ic * 128
                            dst = self.hshd[rows0:rows0 + 128, 0:512]
                        self.st4[par2].inc(sp.dma_start(out=dst,
                                                        in_=ht[:, par2 * 512:par2 * 512 + 512]))
            prev_dve_end = self.dve.n

        if "dbg_hsh" in self.dbg:
            @blk.sync
            def _(sp):
                for c in self.st4:
                    c.wait(sp)
                self.st4[0].inc(sp.dma_start(out=self.dbg["dbg_hsh"], in_=self.hshd))


    def p4b_xn2tm(self, xn2, xstg):
        blk = self.block
        T, TC = self.T, self.TC
        pe0, dve0 = self.pe.n, self.dve.n
        st_b = [self.st4[2].n, self.st4[3].n]

        @blk.tensor
        def _(t):
            for tc in range(TC):
                for dc in range(DC):
                    i = tc * DC + dc
                    wge(t, self.dve.sem, dve0 + i - 1)
                    tb = self.psT if i % 2 == 0 else self.psT2
                    ins = t.transpose(tb[:, 0:128],
                                      xn2[:, dc * T + tc * 128: dc * T + tc * 128 + 128],
                                      self.c_idb[:, :])
                    self.pe.inc(ins)

        @blk.vector
        def _(v):
            v = VV(self, v)
            for tc in range(TC):
                par = tc % 2
                for dc in range(DC):
                    i = tc * DC + dc
                    v.wait_ge(self.pe.sem, pe0 + i + 1)
                    if dc == 0:
                        wge(v, self.st4[2 + par].sem, st_b[par] + 16 * (tc // 2))
                    srcb = self.psT if i % 2 == 0 else self.psT2
                    v.tensor_copy(xstg[:, par * D + dc * 128: par * D + (dc + 1) * 128],
                                  srcb[:, 0:128])

        @blk.sync
        def _(sp):
            for tc in range(TC):
                par = tc % 2
                sp.wait_ge(self.dve.sem, dve0 + (tc + 1) * DC)
                self.st4[2 + par].inc(sp.dma_start(
                    out=self.xn2tm[tc * 128:(tc + 1) * 128, :],
                    in_=xstg[:, par * D:(par + 1) * D]))
        self.xn2tm_st = [self.st4[2].n, self.st4[3].n]

    def p5b_index(self, ee):
        nc = self.nc
        blk = self.block
        T, TC, CAP, NSL, PMW = self.T, self.TC, self.CAP, self.NSL, self.PMW
        NCHK = T // 512
        mfm = ee(nc.sbuf_tensor("i_mfm", [NE, T], F32))
        sfm = ee(nc.sbuf_tensor("i_sfm", [NE, T], F32))
        gfm = ee(nc.sbuf_tensor("i_gfm", [NE, T], F32))
        cum = ee(nc.sbuf_tensor("i_cum", [NE, T], F32))
        m1 = ee(nc.sbuf_tensor("i_m1", [NE, T], F32))
        m2 = ee(nc.sbuf_tensor("i_m2", [NE, T], F32))
        amk = ee(nc.sbuf_tensor("i_amk", [NE, T], F32))
        bmk = ee(nc.sbuf_tensor("i_bmk", [NE, T], F32))
        rowA = ee(nc.sbuf_tensor("i_rowA", [1, T], F32))
        rowB = ee(nc.sbuf_tensor("i_rowB", [1, T], F32))
        wrA = ee(nc.sbuf_tensor("i_wrA", [1, T], F32))
        wrB = ee(nc.sbuf_tensor("i_wrB", [1, T], F32))
        saf = ee(nc.sbuf_tensor("i_saf", [128, TC], F32))
        sbf = ee(nc.sbuf_tensor("i_sbf", [128, TC], F32))
        paf = ee(nc.sbuf_tensor("i_paf", [128, TC], F32))
        pbf = ee(nc.sbuf_tensor("i_pbf", [128, TC], F32))
        paf_i = ee(nc.sbuf_tensor("i_paf_i", [128, TC], I32))
        pbf_i = ee(nc.sbuf_tensor("i_pbf_i", [128, TC], I32))
        waf = ee(nc.sbuf_tensor("i_waf", [128, TC], F32))
        wbf = ee(nc.sbuf_tensor("i_wbf", [128, TC], F32))
        zrow = ee(nc.sbuf_tensor("i_zrow", [128, 512], F32))
        pe0, dve0 = self.pe.n, self.dve.n
        wl0 = self.wl[0].n

        @blk.vector
        def _(v):
            v = VV(self, v)
            v.memset(zrow[:, :], 0.0)
            v.tensor_scalar(mfm[0:NE, :], self.w_fm[0:NE, :], 0.0, None, ALU.is_gt)
            v.tensor_tensor_scan(sfm[0:NE, :], mfm[0:NE, :], mfm[0:NE, :], 0.0,
                                 ALU.add, ALU.bypass)
            v.tensor_tensor(sfm[0:NE, :], sfm[0:NE, :], mfm[0:NE, :], ALU.subtract)
        dve_v1 = self.dve.n

        @blk.tensor
        def _(t):
            t.wait_ge(self.dve.sem, dve_v1)
            for ch in range(NCHK):
                if ch >= 2:
                    t.wait_ge(self.dve.sem, dve_v1 + ch - 1)
                ins = t.matmul(self.ps[4 + ch % 2][0:NE, 0:512], self.c_tri8[0:NE, 0:NE],
                               mfm[0:NE, ch * 512:(ch + 1) * 512], start=True, stop=True)
                self.pe.inc(ins)

        @blk.vector
        def _(v):
            v = VV(self, v)
            for ch in range(NCHK):
                v.wait_ge(self.pe.sem, pe0 + ch + 1)
                v.tensor_copy(cum[0:NE, ch * 512:(ch + 1) * 512],
                              self.ps[4 + ch % 2][0:NE, 0:512])
        pe_cum = self.pe.n

        @blk.vector
        def _(v):
            v = VV(self, v)
            v.tensor_scalar(m1[0:NE, :], sfm[0:NE, :], float(CAP), None, ALU.is_lt)
            v.tensor_tensor(m1[0:NE, :], m1[0:NE, :], mfm[0:NE, :], ALU.mult)
            v.tensor_scalar(gfm[0:NE, :], sfm[0:NE, :], self.c_segb[0:NE, 0:1], None, ALU.add)
            v.tensor_tensor(gfm[0:NE, :], gfm[0:NE, :], m1[0:NE, :], ALU.mult)
            v.tensor_tensor(m2[0:NE, :], self.c_trash[0:NE, :], m1[0:NE, :], ALU.mult)
            v.tensor_tensor(m2[0:NE, :], self.c_trash[0:NE, :], m2[0:NE, :], ALU.subtract)
            v.tensor_tensor(gfm[0:NE, :], gfm[0:NE, :], m2[0:NE, :], ALU.add)
            v.tensor_scalar(amk[0:NE, :], cum[0:NE, :], 0.0, None, ALU.is_equal)
            v.tensor_tensor(amk[0:NE, :], amk[0:NE, :], mfm[0:NE, :], ALU.mult)
            v.tensor_scalar(bmk[0:NE, :], cum[0:NE, :], 1.0, None, ALU.is_equal)
            v.tensor_tensor(bmk[0:NE, :], bmk[0:NE, :], mfm[0:NE, :], ALU.mult)
            v.tensor_tensor(m1[0:NE, :], amk[0:NE, :], gfm[0:NE, :], ALU.mult)
            v.tensor_tensor(m2[0:NE, :], bmk[0:NE, :], gfm[0:NE, :], ALU.mult)
            v.tensor_tensor(amk[0:NE, :], amk[0:NE, :], self.w_fm[0:NE, :], ALU.mult)
            v.tensor_tensor(bmk[0:NE, :], bmk[0:NE, :], self.w_fm[0:NE, :], ALU.mult)
        dve_v2 = self.dve.n

        @blk.tensor
        def _(t):
            t.wait_ge(self.dve.sem, dve_v2)
            for pp, (sA, sB) in enumerate([(m1, m2), (amk, bmk)]):
                for ch in range(NCHK):
                    it = pp * NCHK + ch
                    if it >= 2:
                        t.wait_ge(self.dve.sem, dve_v2 + 2 * (it - 1))
                    ins = t.matmul(self.ps[4 + ch % 2][0:1, 0:512], self.c_ones8[0:NE, 0:1],
                                   sA[0:NE, ch * 512:(ch + 1) * 512], start=True, stop=True)
                    self.pe.inc(ins)
                    ins = t.matmul(self.banks[6][0:1, 0:512], self.c_ones8[0:NE, 0:1],
                                   sB[0:NE, ch * 512:(ch + 1) * 512], start=True, stop=True)
                    self.pe.inc(ins)

        @blk.vector
        def _(v):
            v = VV(self, v)
            for pp, (dA, dB) in enumerate([(rowA, rowB), (wrA, wrB)]):
                for ch in range(NCHK):
                    it = pp * NCHK + ch
                    v.wait_ge(self.pe.sem, pe_cum + 2 * it + 2)
                    v.tensor_copy(dA[0:1, ch * 512:(ch + 1) * 512], self.ps[4 + ch % 2][0:1, 0:512])
                    v.tensor_copy(dB[0:1, ch * 512:(ch + 1) * 512], self.banks[6][0:1, 0:512])
        pmA, pmB, pmt = amk, bmk, m1
        it1 = m2[0:NE, :].bitcast(I32)
        it2 = gfm[0:NE, :].bitcast(I32)

        @blk.vector
        def _(v):
            v = VV(self, v)
            for src, dst in ((rowA, pmA), (rowB, pmB)):
                v.tensor_copy(it1[0:1, :], src[0:1, :])
                v.tensor_scalar(it2[0:1, :], it1[0:1, :], 127, None, ALU.bitwise_and)
                v.tensor_scalar(it1[0:1, :], it1[0:1, :], 7, None, ALU.arith_shift_right)
                v.tensor_copy(dst[0:1, :], it2[0:1, :])
                v.tensor_scalar(dst[0:1, :], dst[0:1, :], float(PMW), None, ALU.mult)
                v.tensor_copy(pmt[0:1, :], it1[0:1, :])
                v.tensor_tensor(dst[0:1, :], dst[0:1, :], pmt[0:1, :], ALU.add)
        pe_sab = self.pe.n
        dve_sab = self.dve.n

        @blk.tensor
        def _(t):
            t.wait_ge(self.dve.sem, dve_sab)
            for tc in range(TC):
                for k, src in enumerate([rowA, rowB, wrA, wrB, pmA, pmB]):
                    i = tc * 6 + k
                    wge(t, self.dve.sem, dve_sab + i - 1)
                    ins = t.transpose(self.ps[4 + i % 2][0:128, 0:1],
                                      src[0:1, tc * 128:(tc + 1) * 128],
                                      self.c_idf[0:1, 0:1])
                    self.pe.inc(ins)

        @blk.vector
        def _(v):
            v = VV(self, v)
            for tc in range(TC):
                for k, dst in enumerate([saf, sbf, waf, wbf, paf, pbf]):
                    i = tc * 6 + k
                    v.wait_ge(self.pe.sem, pe_sab + i + 1)
                    v.tensor_copy(dst[:, tc:tc + 1], self.ps[4 + i % 2][0:128, 0:1])
            v.tensor_copy(self.slA_i[:, 0:TC], saf[:, 0:TC])
            v.tensor_copy(self.slB_i[:, 0:TC], sbf[:, 0:TC])
            v.tensor_copy(paf_i[:, 0:TC], paf[:, 0:TC])
            v.tensor_copy(pbf_i[:, 0:TC], pbf[:, 0:TC])
        dve_all = self.dve.n

        @blk.gpsimd
        def _(g):
            g.wait_ge(self.dve.sem, dve_all)
            self.sc.inc(g.dma_start(
                out=self.slotpm.rearrange("(p c) x -> p (c x)", p=128),
                in_=zrow[:, 0:PMW]))
            self.sc.inc(g.dma_start(
                out=self.wgflat.rearrange("(p c) x -> p (c x)", p=128),
                in_=zrow[:, 0:PMW]))
            for q in range(D // 512):
                self.sc.inc(g.dma_start(
                    out=self.contrib[self.NSL:self.NSL + 128, q * 512:(q + 1) * 512],
                    in_=zrow[:, 0:512]))
            g.wait_ge(self.sc.sem, self.sc.n)
            for tc in range(TC):
                for out, idx, val in ((self.slotpm, paf_i, self.c_tokid),
                                      (self.slotpm, pbf_i, self.c_tokid),
                                      (self.wgflat, self.slA_i, waf),
                                      (self.wgflat, self.slB_i, wbf)):
                    self.sc.inc(g.indirect_dma_start(
                        out=out,
                        out_offset=IndirectOffsetOnAxis(idx[:, tc:tc + 1], 0),
                        in_=val[:, tc:tc + 1], in_offset=None))
        self.sc_scatter_end = self.sc.n

    def p6b_sparse(self, e):
        nc = self.nc
        blk = self.block
        T, CAP, NSL, PMW = self.T, self.CAP, self.NSL, self.PMW
        JT = CAP // 128
        SCH = self.SCH
        spm_f = e(nc.sbuf_tensor("b_spm_f", [128, PMW], F32))
        spm_i = e(nc.sbuf_tensor("b_spm_i", [128, PMW], I32))
        xstg = e(nc.sbuf_tensor("b_xstg", [128, 2 * D], BF16))
        xg = [e(nc.sbuf_tensor(f"b_xg{i}", [128, DC * CAP], BF16)) for i in range(2)]
        hg = [e(nc.sbuf_tensor(f"b_hg{i}", [128, MIC * CAP], BF16)) for i in range(2)]
        w2a = e(nc.sbuf_tensor("b_w2a", [128, MIC * D], BF16))
        wgb = e(nc.sbuf_tensor("b_wgb", [128, CAP], F32))
        wgr = e(nc.sbuf_tensor("b_wgr", [1, CAP], F32))
        wgc = e(nc.sbuf_tensor("b_wgc", [128, 2 * D], BF16))
        wuc = e(nc.sbuf_tensor("b_wuc", [128, 2 * D], BF16))
        sgb = e(nc.sbuf_tensor("b_sgb", [128, 2 * 512], BF16))
        tml = e(nc.sbuf_tensor("b_tml", [128, 512], F32))
        ev = e(nc.sbuf_tensor("b_ev", [128, 2 * 512], F32))

        @blk.sync
        def _(sp):
            sp.wait_ge(self.sc.sem, self.sc_scatter_end)
            self.scl.inc(sp.dma_start(
                out=spm_f[:, :],
                in_=self.slotpm.rearrange("(p c) x -> p (c x)", p=128)))
        ld_spm = self.scl.n

        @blk.vector
        def _(v):
            v = VV(self, v)
            v.wait_ge(self.scl.sem, ld_spm)
            v.tensor_copy(spm_i[:, :], spm_f[:, :])
        dve_spm = self.dve.n

        init = dict(pe_tr_end=self.pe.n, dve_up_end=self.dve.n, pe_up_end=self.pe.n,
                    pe_dn_end=self.pe.n, dve_dn_end=self.dve.n)
        hist = [init, init]
        for ex in range(NE):
            r = self._p6b_expert(ex, spm_i, spm_f, xstg, xg[ex % 2], hg[ex % 2], w2a,
                                 wgb, wgr, wgc, wuc, sgb, tml, ev, dve_spm,
                                 hist[-1], hist[-2])
            hist.append(r)

        if "dbg_contrib" in self.dbg:
            @blk.sync
            def _(sp):
                for c in self.st4:
                    c.wait(sp)
                self.st4[0].inc(sp.dma_start(out=self.dbg["dbg_contrib"], in_=self.contrib))

    def _p6b_expert(self, ex, spm_i, spm_f, xstg, xg, hg, w2a, wgb, wgr, wgc, wuc,
                    sgb, tml, ev, dve_spm, prev, prev2):
        blk = self.block
        CAP, SCH = self.CAP, self.SCH
        JT = CAP // 128
        NS = len(SCH)
        NOG = D // 512
        pe0, act0, dve0 = self.pe.n, self.act.n, self.dve.n
        gl_b = [self.gl[0].n, self.gl[1].n]
        wl_b = [self.wl[0].n, self.wl[1].n]
        st_b2 = [self.st4[2].n, self.st4[3].n]

        @blk.gpsimd
        def _(g):
            g.wait_ge(self.dve.sem, dve_spm)
            if ex == 0:
                wge(g, self.st4[2].sem, self.xn2tm_st[0])
                wge(g, self.st4[3].sem, self.xn2tm_st[1])
            for j in range(JT):
                par = j % 2
                if j >= 2:
                    wge(g, self.pe.sem, pe0 + 16 * (j - 1))
                elif ex >= 1:
                    wge(g, self.pe.sem, prev["pe_tr_end"])
                self.gl[par].inc(g.indirect_dma_start(
                    out=xstg[:, par * D:(par + 1) * D], out_offset=None,
                    in_=self.xn2tm,
                    in_offset=IndirectOffsetOnAxis(spm_i[:, ex * JT + j: ex * JT + j + 1], 0)))

        @blk.tensor
        def _(t):
            for j in range(JT):
                par = j % 2
                t.wait_ge(self.gl[par].sem, gl_b[par] + 16 * (j // 2 + 1))
                for dc in range(DC):
                    i = j * DC + dc
                    wge(t, self.dve.sem, dve0 + i - 1)
                    tb = self.psT if i % 2 == 0 else self.psT2
                    ins = t.transpose(tb[:, 0:128],
                                      xstg[:, par * D + dc * 128: par * D + (dc + 1) * 128],
                                      self.c_idb[:, :])
                    self.pe.inc(ins)
        pe_tr_end = self.pe.n

        @blk.vector
        def _(v):
            v = VV(self, v)
            if ex >= 2:
                v.wait_ge(self.pe.sem, prev2["pe_up_end"])
            for j in range(JT):
                for dc in range(DC):
                    i = j * DC + dc
                    v.wait_ge(self.pe.sem, pe0 + i + 1)
                    srcb = self.psT if i % 2 == 0 else self.psT2
                    v.tensor_copy(xg[:, dc * CAP + j * 128: dc * CAP + (j + 1) * 128],
                                  srcb[:, 0:128])
        dve_tr = self.dve.n

        @blk.sync
        def _(sp):
            if ex == 0:
                wge(sp, self.dve.sem, dve_spm)
            else:
                wge(sp, self.pe.sem, pe0)
            self.scl.inc(sp.dma_start(out=wgr[0:1, 0:CAP],
                                      in_=self.wgflat[ex * CAP:(ex + 1) * CAP, :]
                                      .rearrange("a x -> x (a)")))
        ldS_w = self.scl.n

        @blk.tensor
        def _(t):
            t.wait_ge(self.scl.sem, ldS_w)
            for si, (sc, sn) in enumerate(SCH):
                wge(t, self.dve.sem, dve_tr + si)
                ins = t.matmul(self.ps[4][0:128, 0:sn], self.c_orf[0:1, :],
                               wgr[0:1, sc:sc + sn], start=True, stop=True)
                self.pe.inc(ins)

        @blk.vector
        def _(v):
            v = VV(self, v)
            for si, (sc, sn) in enumerate(SCH):
                v.wait_ge(self.pe.sem, pe_tr_end + si + 1)
                v.tensor_copy(wgb[:, sc:sc + sn], self.ps[4][0:128, 0:sn])
        pe_wgb = self.pe.n
        dve_wgb = self.dve.n

        @blk.gpsimd
        def _(g):
            for icp in range(MIC):
                par = icp % 2
                if icp >= 2:
                    wge(g, self.pe.sem, pe_wgb + 2 * NS * (icp - 1))
                elif ex >= 1:
                    wge(g, self.pe.sem, prev["pe_up_end"])
                b0 = ex * 2 * MIC + icp
                b1 = ex * 2 * MIC + MIC + icp
                self.block_load(g, wgc[:, par * D:(par + 1) * D], self.inp["w13T"], b0, self.wl[par])
                self.block_load(g, wuc[:, par * D:(par + 1) * D], self.inp["w13T"], b1, self.wl[par])

        @blk.tensor
        def _(t):
            t.wait_ge(self.dve.sem, dve_wgb)
            for icp in range(MIC):
                par = icp % 2
                t.wait_ge(self.wl[par].sem, wl_b[par] + 32 * (icp // 2 + 1))
                for si, (sc, sn) in enumerate(SCH):
                    it = icp * NS + si
                    par2 = it % 2
                    gbank = self.ps[0 + 2 * par2]
                    ubank = self.ps[1 + 2 * par2]
                    if it >= 2:
                        wge(t, self.act.sem, act0 + (it - 1))
                        wge(t, self.dve.sem, dve_wgb + 2 * (it - 1))
                    ins = None
                    for dc in range(DC):
                        ins = t.matmul(gbank[:, 0:sn],
                                       wgc[:, par * D + dc * 128: par * D + (dc + 1) * 128],
                                       xg[:, dc * CAP + sc: dc * CAP + sc + sn],
                                       start=(dc == 0), stop=(dc == DC - 1))
                    self.pe.inc(ins)
                    ins = None
                    for dc in range(DC):
                        ins = t.matmul(ubank[:, 0:sn],
                                       wuc[:, par * D + dc * 128: par * D + (dc + 1) * 128],
                                       xg[:, dc * CAP + sc: dc * CAP + sc + sn],
                                       start=(dc == 0), stop=(dc == DC - 1))
                    self.pe.inc(ins)
        pe_up_end = self.pe.n

        @blk.scalar
        def _(s):
            if ex >= 1:
                s.wait_ge(self.dve.sem, prev["dve_up_end"])
            for icp in range(MIC):
                for si, (sc, sn) in enumerate(SCH):
                    it = icp * NS + si
                    par2 = it % 2
                    s.wait_ge(self.pe.sem, pe_wgb + 2 * it + 1)
                    if it >= 2:
                        wge(s, self.dve.sem, dve_wgb + 2 * (it - 1))
                    self.act.inc(s.activation(sgb[:, par2 * 512:par2 * 512 + sn],
                                              self.ps[0 + 2 * par2][:, 0:sn], AF.Silu))

        @blk.vector
        def _(v):
            v = VV(self, v)
            if ex >= 2:
                v.wait_ge(self.pe.sem, prev2["pe_dn_end"])
            for icp in range(MIC):
                for si, (sc, sn) in enumerate(SCH):
                    it = icp * NS + si
                    par2 = it % 2
                    v.wait_ge(self.pe.sem, pe_wgb + 2 * it + 2)
                    v.wait_ge(self.act.sem, act0 + it + 1)
                    v.tensor_tensor(tml[:, 0:sn], sgb[:, par2 * 512:par2 * 512 + sn],
                                    self.ps[1 + 2 * par2][:, 0:sn], ALU.mult)
                    v.tensor_tensor(hg[:, icp * CAP + sc: icp * CAP + sc + sn],
                                    tml[:, 0:sn], wgb[:, sc:sc + sn], ALU.mult)
        dve_up_end = self.dve.n

        @blk.gpsimd
        def _(g):
            if ex >= 1:
                wge(g, self.pe.sem, prev["pe_dn_end"])
            for icp in range(MIC):
                self.block_load(g, w2a[:, icp * D:(icp + 1) * D],
                                self.inp["w2T"], ex * MIC + icp, self.wlS)
        wlS_dn = self.wlS.n

        @blk.tensor
        def _(t):
            t.wait_ge(self.wlS.sem, wlS_dn)
            t.wait_ge(self.dve.sem, dve_up_end)
            for j in range(JT):
                for og in range(NOG):
                    i = j * NOG + og
                    bank = self.ps[5] if i % 2 == 0 else self.banks[7]
                    wge(t, self.dve.sem, dve_up_end + i - 1)
                    ins = None
                    for icp in range(MIC):
                        ins = t.matmul(bank[:, 0:512],
                                       hg[:, icp * CAP + j * 128: icp * CAP + (j + 1) * 128],
                                       w2a[:, icp * D + og * 512: icp * D + (og + 1) * 512],
                                       start=(icp == 0), stop=(icp == MIC - 1))
                    self.pe.inc(ins)
        pe_dn_end = self.pe.n

        @blk.vector
        def _(v):
            v = VV(self, v)
            for j in range(JT):
                for og in range(NOG):
                    i = j * NOG + og
                    par2 = i % 2
                    v.wait_ge(self.pe.sem, pe_up_end + i + 1)
                    wge(v, self.st4[2 + par2].sem, st_b2[par2] + 16 * (i // 2))
                    bank = self.ps[5] if i % 2 == 0 else self.banks[7]
                    v.tensor_copy(ev[:, par2 * 512:par2 * 512 + 512], bank[:, 0:512])
        dve_dn_end = self.dve.n

        @blk.sync
        def _(sp):
            for j in range(JT):
                for og in range(NOG):
                    i = j * NOG + og
                    par2 = i % 2
                    sp.wait_ge(self.dve.sem, dve_up_end + i + 1)
                    self.st4[2 + par2].inc(sp.dma_start(
                        out=self.contrib[ex * CAP + j * 128: ex * CAP + (j + 1) * 128,
                                         og * 512:(og + 1) * 512],
                        in_=ev[:, par2 * 512:par2 * 512 + 512]))

        return dict(pe_tr_end=pe_tr_end, dve_up_end=dve_up_end, pe_up_end=pe_up_end,
                    pe_dn_end=pe_dn_end, dve_dn_end=dve_dn_end)

    def p8_combine(self, cA, cB, x3r, yt):
        blk = self.block
        TC = self.TC
        pe0, dve0 = self.pe.n, self.dve.n
        ld_b = [self.ld[0].n, self.ld[1].n]
        wl_b = [self.wl[0].n, self.wl[1].n]
        st_b = [self.st4[0].n, self.st4[1].n]
        stC = [self.st4[2].n, self.st4[3].n]

        @blk.gpsimd
        def _(g):
            wge(g, self.st4[2].sem, stC[0])
            wge(g, self.st4[3].sem, stC[1])
            for tc in range(TC):
                par = tc % 2
                if tc >= 2:
                    wge(g, self.dve.sem, dve0 + 2 * (tc - 1))
                self.wl[par].inc(g.indirect_dma_start(
                    out=cA[:, par * D:(par + 1) * D], out_offset=None,
                    in_=self.contrib,
                    in_offset=IndirectOffsetOnAxis(self.slA_i[:, tc:tc + 1], 0)))
                self.wl[par].inc(g.indirect_dma_start(
                    out=cB[:, par * D:(par + 1) * D], out_offset=None,
                    in_=self.contrib,
                    in_offset=IndirectOffsetOnAxis(self.slB_i[:, tc:tc + 1], 0)))

        @blk.sync
        def _(sp):
            for tc in range(TC):
                par = tc % 2
                if tc >= 2:
                    wge(sp, self.dve.sem, dve0 + 2 * (tc - 1))
                self.ld[par].inc(sp.dma_start(
                    out=x3r[:, par * D:(par + 1) * D],
                    in_=self.x3d[tc * 128:(tc + 1) * 128, :]))
                if tc >= 2:
                    j = tc - 2
                    sp.wait_ge(self.dve.sem, dve0 + 2 * (j + 1))
                    self.st4[j % 2].inc(sp.dma_start(
                        out=self.y[j * 128:(j + 1) * 128, :],
                        in_=yt[:, (j % 2) * D:(j % 2 + 1) * D]))
            for j in (TC - 2, TC - 1):
                sp.wait_ge(self.dve.sem, dve0 + 2 * (j + 1))
                self.st4[j % 2].inc(sp.dma_start(
                    out=self.y[j * 128:(j + 1) * 128, :],
                    in_=yt[:, (j % 2) * D:(j % 2 + 1) * D]))

        @blk.vector
        def _(v):
            v = VV(self, v)
            for tc in range(TC):
                par = tc % 2
                v.wait_ge(self.wl[par].sem, wl_b[par] + 32 * (tc // 2 + 1))
                v.wait_ge(self.ld[par].sem, ld_b[par] + 16 * (tc // 2 + 1))
                wge(v, self.st4[par].sem, st_b[par] + 16 * (tc // 2))
                v.tensor_tensor(yt[:, par * D:(par + 1) * D], x3r[:, par * D:(par + 1) * D],
                                cA[:, par * D:(par + 1) * D], ALU.add)
                v.tensor_tensor(yt[:, par * D:(par + 1) * D], yt[:, par * D:(par + 1) * D],
                                cB[:, par * D:(par + 1) * D], ALU.add)

    def p7_shared_down(self, hsb, wd, sgbc, x2t, x3t, tml, x3s):
        blk = self.block
        T, TG = self.T, self.TG
        pe0, dve0 = self.pe.n, self.dve.n

        @blk.tensor
        def _(t):
            for tg in range(TG):
                if tg > 0:
                    t.wait_ge(self.dve.sem, dve0 + tg)
                ins = t.matmul(self.ps[4], self.c_orf[0:1, :],
                               self.sg_fm[0:1, tg * 512:tg * 512 + 512], start=True, stop=True)
                self.pe.inc(ins)

        @blk.vector
        def _(v):
            v = VV(self, v)
            for tg in range(TG):
                v.wait_ge(self.pe.sem, pe0 + tg + 1)
                v.tensor_copy(sgbc[:, tg * 512:tg * 512 + 512], self.ps[4])

        for tg in range(TG):
            tg_pe0, tg_dve0 = self.pe.n, self.dve.n
            ld_b = [self.ld[0].n, self.ld[1].n]
            st_b = [self.st4[0].n, self.st4[1].n]
            wl_b = [self.wl[0].n, self.wl[1].n]

            @blk.sync
            def _(sp):
                if tg > 0:
                    sp.wait_ge(self.pe.sem, tg_pe0)
                self.ldS.inc(sp.dma_start(
                    out=hsb[:, 0:SIC * 512].rearrange("p (ic t) -> p ic t", t=512),
                    in_=self.hshd.rearrange("(tgx ic p) t -> tgx p ic t", p=128, ic=SIC)[tg]))
                for oc in range(DC):
                    par2 = oc % 2
                    if oc >= 2:
                        wge(sp, self.dve.sem, tg_dve0 + 6 * (oc - 2) + 2)
                    self.ld[par2].inc(sp.dma_start(
                        out=x2t[:, par2 * 512:par2 * 512 + 512],
                        in_=self.x2d[oc * 128:(oc + 1) * 128, tg * 512:tg * 512 + 512]))
                for qs in range(4):
                    sp.wait_ge(self.dve.sem, tg_dve0 + 6 * DC)
                    self.st4[qs % 2].inc(sp.dma_start(
                        out=self.x3d[tg * 512 + qs * 128: tg * 512 + (qs + 1) * 128, :],
                        in_=x3s[:, qs * D:(qs + 1) * D]))
            ld_h = self.ldS.n

            @blk.gpsimd
            def _(gp):
                wge(gp, self.pe.sem, tg_pe0)
                for oc in range(DC):
                    par = oc % 2
                    if oc >= 2:
                        wge(gp, self.pe.sem, tg_pe0 + 5 * (oc - 2) + 1)
                    self.block_load(gp, wd[:, par * SIC * 128:(par + 1) * SIC * 128],
                                    self.inp["wdownT"], oc, self.wl[par])

            @blk.tensor
            def _(t):
                t.wait_ge(self.ldS.sem, ld_h)
                for oc in range(DC):
                    par = oc % 2
                    t.wait_ge(self.wl[par].sem, wl_b[par] + 16 * (oc // 2 + 1))
                    bank = self.ps[oc % 2]
                    if oc >= 2:
                        wge(t, self.dve.sem, tg_dve0 + 6 * (oc - 2) + 1)
                    ins = None
                    for ic in range(SIC):
                        ins = t.matmul(bank,
                                       wd[:, par * SIC * 128 + ic * 128: par * SIC * 128 + (ic + 1) * 128],
                                       hsb[:, ic * 512:(ic + 1) * 512],
                                       start=(ic == 0), stop=(ic == SIC - 1))
                    self.pe.inc(ins)
                    for qs in range(4):
                        tbank = self.psT[:, 0:128] if qs % 2 == 0 else self.psT[:, 128:256]
                        wge(t, self.dve.sem, tg_dve0 + 6 * oc + 2 + max(0, qs - 1))
                        ins = t.transpose(tbank.bitcast(F32)[:, 0:128] if False else
                                          (self.ps[5][:, 0:128] if qs % 2 == 0 else self.banks[7][:, 0:128]),
                                          x3t[:, (oc % 2) * 512 + qs * 128: (oc % 2) * 512 + (qs + 1) * 128],
                                          self.c_idf[:, :])
                        self.pe.inc(ins)

            @blk.vector
            def _(v):
                v = VV(self, v)
                for oc in range(DC):
                    par2 = oc % 2
                    v.wait_ge(self.pe.sem, tg_pe0 + 5 * oc + 1)
                    v.wait_ge(self.ld[par2].sem, ld_b[par2] + 16 * (oc // 2 + 1))
                    if tg > 0 and oc == 0:
                        for q2 in range(2):
                            wge(v, self.st4[q2].sem, st_b[q2])
                    bank = self.ps[oc % 2]
                    v.tensor_tensor(tml[:, par2 * 512:par2 * 512 + 512], bank,
                                    sgbc[:, tg * 512:tg * 512 + 512], ALU.mult)
                    v.tensor_tensor(x3t[:, par2 * 512:par2 * 512 + 512],
                                    tml[:, par2 * 512:par2 * 512 + 512],
                                    x2t[:, par2 * 512:par2 * 512 + 512], ALU.add)
                    for qs in range(4):
                        v.wait_ge(self.pe.sem, tg_pe0 + 5 * oc + 2 + qs)
                        src = self.ps[5][:, 0:128] if qs % 2 == 0 else self.banks[7][:, 0:128]
                        v.tensor_copy(x3s[:, qs * D + oc * 128: qs * D + (oc + 1) * 128], src)

        if "dbg_x3" in self.dbg:
            @blk.sync
            def _(sp):
                for c in self.st4:
                    c.wait(sp)
                self.st4[0].inc(sp.dma_start(out=self.dbg["dbg_x3"], in_=self.x3d))

    def final_wait(self):
        blk = self.block

        @blk.sync
        def _(sp):
            for c in self.st4:
                c.wait(sp)
            for c in self.ld:
                c.wait(sp)
            self.ldS.wait(sp)
            self.scl.wait(sp)

        @blk.gpsimd
        def _(g):
            for c in self.wl:
                c.wait(g)
            self.wlS.wait(g)
            self.sc.wait(g)
            for c in self.gl:
                c.wait(g)



def build_mk(L=8192, debug=()):
    nc = bass.Bass("TRN2", target_bir_lowering=False)
    mk = MK(nc, L, debug=debug)
    mk.build()
    return nc, mk


def host_prep(inputs, L=8192):
    f = np.float32
    T_ = L // 4
    TC_ = T_ // 128
    CAP_ = ((3 * T_ // 8 + 127) // 128) * 128
    d = {}
    def blockize(WT):
        rows, cols = WT.shape
        rc, nb = rows // 128, cols // 128
        w4 = WT.reshape(rc, 128, nb, 128)
        return np.ascontiguousarray(w4.transpose(2, 1, 0, 3).reshape(nb * 128, rc * 128))

    for p in "qkvo":
        W = np.asarray(inputs[f"L0_attn_W{p}"], f)
        d[f"w{p}T"] = blockize(np.ascontiguousarray(W.T))
        Bm = np.asarray(inputs[f"L0_attn_B{p}"], f)
        BT = np.ascontiguousarray(Bm.T)
        d[f"b{p}T_a"] = np.ascontiguousarray(
            BT.reshape(DC, 128, R).transpose(1, 0, 2).reshape(128, DC * R))
        Am = np.asarray(inputs[f"L0_attn_A{p}"], f)
        d[f"a{p}T"] = np.ascontiguousarray(Am.T)
    for p, key in [("q", "L0_attn_bq"), ("k", "L0_attn_bk"), ("o", "L0_attn_bo")]:
        b = np.asarray(inputs[key], f)
        d[f"b{p}_a"] = np.ascontiguousarray(b.reshape(DC, 128).T)
    d["bv_row"] = np.ascontiguousarray(np.asarray(inputs["L0_attn_bv"], f).reshape(1, D))
    gw9 = np.concatenate([np.asarray(inputs["L0_ffn_gate_w"], f),
                          np.asarray(inputs["L0_ffn_shared_gate_w"], f)], axis=0)
    gwT = gw9.T
    d["gw_arr"] = np.ascontiguousarray(gwT.reshape(DC, 128, 9).transpose(1, 0, 2).reshape(128, DC * 9))
    d["wguT"] = blockize(np.asarray(inputs["L0_ffn_shared_gate_up_w"], f).T)
    d["wdownT"] = blockize(np.asarray(inputs["L0_ffn_shared_down_w"], f).T)
    w13 = np.asarray(inputs["L0_ffn_w13"], f)
    d["w13T"] = blockize(np.ascontiguousarray(w13.transpose(2, 0, 1).reshape(D, NE * 2 * MI)))
    w2 = np.asarray(inputs["L0_ffn_w2"], f)
    d["w2T"] = np.ascontiguousarray(w2.transpose(0, 2, 1).reshape(NE * MI, D))
    d["ident_b"] = np.eye(128, dtype=BF)
    d["ident_f"] = np.eye(128, dtype=f)
    d["ones_col_b"] = np.ones((128, 1), dtype=BF)
    d["ones_row_b"] = np.ones((1, 128), dtype=BF)
    d["ones_row_f"] = np.ones((1, 128), dtype=f)
    tie = np.zeros((8, 8), f)
    for e_ in range(8):
        for j in range(8):
            tie[e_, j] = 1.0 if j < e_ else 0.0
    d["tie64"] = np.ascontiguousarray(np.tile(tie.reshape(1, 64), (128, 1)))
    sel = np.zeros((NE, NE * 128), f)
    for e_ in range(NE):
        sel[e_, e_ * 128:(e_ + 1) * 128] = 1.0
    d["sel8"] = sel
    tri = np.zeros((NE, NE), f)
    for j in range(NE):
        for e_ in range(NE):
            tri[j, e_] = 1.0 if j < e_ else 0.0
    d["tri8"] = tri
    d["ones8"] = np.ones((NE, 1), f)
    d["segb"] = (np.arange(NE, dtype=f) * CAP_).reshape(NE, 1)
    d["tokid"] = (np.arange(TC_, dtype=f)[None, :] * 128
                  + np.arange(128, dtype=f)[:, None]).astype(f)
    NSL_ = NE * CAP_
    tr2 = np.empty((NE, T_), f)
    for e_ in range(NE):
        tr2[e_, :] = NSL_ + (np.arange(T_) // 128) * NE + e_
    d["trashf"] = tr2
    return d


def make_xt(x, core, L=8192):
    T = L // 4
    b, q = core // 4, core % 4
    xb_ = np.asarray(x[b], np.float32)
    order = [q] + [i for i in range(4) if i != q]
    parts = [xb_[p * T:(p + 1) * T] for p in order]
    xr = np.concatenate(parts, axis=0)
    return np.ascontiguousarray(xr.T)


def kernel(**inputs):
    L = 8192
    T = L // 4
    nc, mk = build_mk(L=L)
    shared = host_prep(inputs, L=L)
    x = np.asarray(inputs["L0_attn_x"], np.float32)
    in_maps = []
    for c in range(8):
        m = dict(shared)
        m["xt_in"] = make_xt(x, c, L=L)
        in_maps.append(m)
    res = run_bass_kernel_spmd(nc, in_maps, core_ids=list(range(8)))
    out = np.empty((2, L, D), np.float32)
    for c in range(8):
        b, q = c // 4, c % 4
        out[b, q * T:(q + 1) * T, :] = res.results[c]["y"]
    return out


# version 14
# speedup vs baseline: 1.0900x; 1.0900x over previous
import numpy as np
import ml_dtypes
import concourse.bass as bass
import concourse.mybir as mybir
from concourse.bass_utils import run_bass_kernel_spmd
from concourse.bass_core_rust import IndirectOffsetOnAxis

F32 = mybir.dt.float32
BF16 = mybir.dt.bfloat16
I32 = mybir.dt.int32
AF = mybir.ActivationFunctionType
ALU = mybir.AluOpType
AX = mybir.AxisListType

D = 2048
DC = 16
H = 16
HD = 128
R = 16
NE = 8
MI = 1408
MIC = MI // 128
SI = 5632
SIC = SI // 128
SCALE = HD ** -0.5
EPS = 1e-5
BF = ml_dtypes.bfloat16


class Ctr:
    def __init__(self, sem, step=1):
        self.sem = sem
        self.step = step
        self.n = 0

    def inc(self, ins):
        ins.then_inc(self.sem, self.step)
        self.n += self.step
        return self.n

    def wait(self, eng, val=None):
        if val is None:
            val = self.n
        if val > 0:
            eng.wait_ge(self.sem, val)


def wge(eng, sem, val):
    if val > 0:
        eng.wait_ge(sem, val)


class VV:
    _PASS = ("wait_ge", "sem_inc", "drain", "nop", "engine_nop")

    def __init__(self, mk, v):
        self._mk = mk
        self._v = v

    def __getattr__(self, name):
        fn = getattr(self._v, name)
        if name in VV._PASS or name.startswith("_"):
            return fn

        def run(*a, **k):
            mk = self._mk
            if mk.dve.n:
                self._v.wait_ge(mk.dve.sem, mk.dve.n)
            ins = fn(*a, **k)
            mk.dve.inc(ins)
            return ins
        return run


class MK:
    def __init__(self, nc: bass.Bass, L: int, debug=()):
        assert L % 512 == 0
        self.nc = nc
        self.L = L
        self.T = L // 4
        self.TG = self.T // 512
        self.TC = self.T // 128
        self.KT = L // 128
        self.debug = set(debug)
        assert self.TG <= 4
        self.CAP = ((3 * self.T // 8 + 127) // 128) * 128
        self.NSL = NE * self.CAP
        self.TRASH = self.NSL
        self.PMW = self.NSL // 128 + 1
        self.SCH = []
        off = 0
        while off < self.CAP:
            n = min(512, self.CAP - off)
            self.SCH.append((off, n))
            off += n

        d = {}
        din = lambda name, shape, dt: nc.dram_tensor(name, shape, dt, kind="ExternalInput").ap()
        self.xt_in = din("xt_in", [D, L], F32)
        for p in "qkvo":
            d[f"w{p}T"] = din(f"w{p}T", [DC * 128, DC * 128], F32)
            d[f"b{p}T_a"] = din(f"b{p}T_a", [128, DC * R], F32)
            d[f"a{p}T"] = din(f"a{p}T", [R, D], F32)
        d["bq_a"] = din("bq_a", [128, DC], F32)
        d["bk_a"] = din("bk_a", [128, DC], F32)
        d["bo_a"] = din("bo_a", [128, DC], F32)
        d["bv_row"] = din("bv_row", [1, D], F32)
        d["gw_arr"] = din("gw_arr", [128, DC * 9], F32)
        d["wguT"] = din("wguT", [(2 * SI // 128) * 128, D], F32)
        d["wdownT"] = din("wdownT", [DC * 128, SIC * 128], F32)
        d["w13T"] = din("w13T", [(NE * 2 * MI // 128) * 128, D], F32)
        d["w2T"] = din("w2T", [NE * MI, D], F32)
        d["ident_b"] = din("ident_b", [128, 128], BF16)
        d["ident_f"] = din("ident_f", [128, 128], F32)
        d["ones_col_b"] = din("ones_col_b", [128, 1], BF16)
        d["ones_row_b"] = din("ones_row_b", [1, 128], BF16)
        d["ones_row_f"] = din("ones_row_f", [1, 128], F32)
        d["tie64"] = din("tie64", [128, 64], F32)
        d["sel8"] = din("sel8", [NE, NE * 128], F32)
        d["tri8"] = din("tri8", [NE, NE], F32)
        d["ones8"] = din("ones8", [NE, 1], F32)
        d["segb"] = din("segb", [NE, 1], F32)
        d["tokid"] = din("tokid", [128, self.TC], F32)
        d["trashf"] = din("trashf", [NE, self.T], F32)
        self.inp = d

        self.y = nc.dram_tensor("y", [self.T, D], F32, kind="ExternalOutput").ap()
        self.dbg = {}
        for name, shape, dt in [
            ("dbg_q", [D, self.T], BF16), ("dbg_k", [4 * D, self.T], BF16), ("dbg_v", [4 * D, self.T], BF16),
            ("dbg_o", [128, DC * self.T], BF16), ("dbg_x2", [D, self.T], F32),
            ("dbg_xn2", [128, DC * self.T], BF16),
            ("dbg_w", [NE, self.T], F32), ("dbg_sg", [1, self.T], F32),
            ("dbg_hsh", [self.TG * SI, 512], BF16),
            ("dbg_x3", [self.T, D], F32),
            ("dbg_slotmap", [128, self.PMW], F32), ("dbg_wg", [self.NSL + 128, 1], F32),
            ("dbg_contrib", [self.NSL + 128, D], F32),
        ]:
            if name in self.debug:
                self.dbg[name] = nc.dram_tensor(name, shape, dt, kind="ExternalOutput").ap()

        dsc = lambda name, shape, dt: nc.dram_tensor(name, shape, dt, kind="Internal").ap()
        self.kag_in = dsc("kag_in", [D, self.T], BF16)
        self.vag_in = dsc("vag_in", [D, self.T], BF16)
        self.kd = dsc("kd", [4 * D, self.T], BF16)
        self.vd = dsc("vd", [4 * D, self.T], BF16)
        self.qd = dsc("qd", [D, self.T], BF16)
        self.x2d = dsc("x2d", [D, self.T], F32)
        self.x3d = dsc("x3d", [self.T, D], F32)
        self.hshd = dsc("hshd", [self.TG * SI, 512], BF16)
        self.xn2tm = dsc("xn2tm", [self.T, D], BF16)
        self.slotpm = dsc("slotpm", [128 * self.PMW, 1], F32)
        self.wgflat = dsc("wgflat", [self.NSL + 128, 1], F32)
        self.contrib = dsc("contrib", [self.NSL + 128, D], F32)

    def build(self):
        nc = self.nc
        from contextlib import ExitStack
        with ExitStack() as ctx:
            e = ctx.enter_context
            self.psh = [e(nc.psum_tensor(f"ps{i}", [128, 512], F32)) for i in range(6)]
            self.psTh = e(nc.psum_tensor("psT", [128, 1024], BF16))
            self.ps7h = e(nc.psum_tensor("ps7", [128, 512], F32))
            self.ps = [p[:, :] for p in self.psh]
            self.psT = self.psTh[:, :]
            self.banks = self.ps + [self.psT.bitcast(F32), self.ps7h[:, :]]
            self.psT2 = self.ps7h[:, :].bitcast(BF16)

            self.ld = [Ctr(e(nc.semaphore(f"s_ld{i}")), 16) for i in range(2)]
            self.ldS = Ctr(e(nc.semaphore("s_ldS")), 16)
            self.st4 = [Ctr(e(nc.semaphore(f"s_st{i}")), 16) for i in range(4)]
            self.wl = [Ctr(e(nc.semaphore(f"s_wl{i}")), 16) for i in range(2)]
            self.wlS = Ctr(e(nc.semaphore("s_wlS")), 16)
            self.sc = Ctr(e(nc.semaphore("s_sc")), 16)
            self.scl = Ctr(e(nc.semaphore("s_scl")), 16)
            self.gl = [Ctr(e(nc.semaphore(f"s_gl{i}")), 16) for i in range(2)]
            self.cc = Ctr(e(nc.semaphore("s_cc")), 1)
            self.pe = Ctr(e(nc.semaphore("s_pe")))
            self.act = Ctr(e(nc.semaphore("s_act")))
            self.dve = Ctr(e(nc.semaphore("s_dve")))
            self.bar = e(nc.semaphore("s_bar"))
            self.nbar = 0

            sb = lambda name, shape, dt: e(nc.sbuf_tensor(name, shape, dt))
            self.c_bq = sb("c_bq", [128, DC], F32)
            self.c_bk = sb("c_bk", [128, DC], F32)
            self.c_bo = sb("c_bo", [128, DC], F32)
            self.c_bv = sb("c_bv", [1, D], BF16)
            self.c_BT = {p: sb(f"c_BT{p}", [128, DC * R], BF16) for p in "qkvo"}
            self.c_gw = sb("c_gw", [128, DC * 9], BF16)
            self.c_idb = sb("c_idb", [128, 128], BF16)
            self.c_idf = sb("c_idf", [128, 128], F32)
            self.c_ocb = sb("c_ocb", [128, 1], BF16)
            self.c_orb = sb("c_orb", [1, 128], BF16)
            self.c_orf = sb("c_orf", [1, 128], F32)
            self.c_tie = sb("c_tie", [128, 64], F32)
            self.c_sel = sb("c_sel", [NE, NE * 128], F32)
            self.sg_fm = sb("sg_fm", [1, self.T], F32)
            self.c_eps = sb("c_eps", [1, 1], F32)
            self.c_tri8 = sb("c_tri8", [NE, NE], F32)
            self.c_ones8 = sb("c_ones8", [NE, 1], F32)
            self.c_segb = sb("c_segb", [NE, 1], F32)
            self.c_tokid = sb("c_tokid", [128, self.TC], F32)
            self.slA_i = sb("slA_i", [128, self.TC], I32)
            self.slB_i = sb("slB_i", [128, self.TC], I32)
            self.c_trash = sb("c_trash", [NE, self.T], F32)

            self.block = e(nc.Block())
            self._emit()
        return nc

    _stats_n = 0

    def _alloc_stats(self, e):
        nc = self.nc
        MK._stats_n += 1
        sfx = f"_{MK._stats_n}"
        self.s1 = e(nc.sbuf_tensor("s1" + sfx, [1, self.T], F32))
        self.s2 = e(nc.sbuf_tensor("s2" + sfx, [1, self.T], F32))
        self.rstd = e(nc.sbuf_tensor("rstd" + sfx, [1, self.T], F32))
        self.Msb = e(nc.sbuf_tensor("Msb" + sfx, [128, self.T], BF16))
        self.Rsb = e(nc.sbuf_tensor("Rsb" + sfx, [128, self.T], BF16))

    def barrier(self):
        self.nbar += 1
        k = 5 * self.nbar
        blk = self.block

        @blk.tensor
        def _(t):
            t.drain()
            t.sem_inc(self.bar, 1)
            t.wait_ge(self.bar, k)

        @blk.scalar
        def _(s):
            s.drain()
            s.sem_inc(self.bar, 1)
            s.wait_ge(self.bar, k)

        @blk.vector
        def _(v):
            v = VV(self, v)
            v.drain()
            v.sem_inc(self.bar, 1)
            v.wait_ge(self.bar, k)

        @blk.sync
        def _(sp):
            for c in self.ld:
                c.wait(sp)
            self.ldS.wait(sp)
            for c in self.st4:
                c.wait(sp)
            sp.sem_inc(self.bar, 1)
            sp.wait_ge(self.bar, k)

        @blk.gpsimd
        def _(g):
            for c in self.wl:
                c.wait(g)
            self.wlS.wait(g)
            self.sc.wait(g)
            for c in self.gl:
                c.wait(g)
            g.sem_inc(self.bar, 1)
            g.wait_ge(self.bar, k)

    def block_load(self, g, dst_sb_ap, src, block, ctr):
        ctr.inc(g.dma_start(out=dst_sb_ap, in_=src[block * 128:(block + 1) * 128, :]))

    def _emit(self):
        from contextlib import ExitStack
        nc = self.nc
        self.p0_consts()
        self.barrier()
        with ExitStack() as s1:
            e = s1.enter_context
            xb = e(nc.sbuf_tensor("xb_g", [128, DC * self.T], BF16))
            xn = xb
            sq = e(nc.sbuf_tensor("sq_g", [128, 2 * self.T], BF16))
            t1b = e(nc.sbuf_tensor("t1b", [128, self.T], F32))
            wcol = e(nc.sbuf_tensor("wcol", [128, 4 * DC * 128], BF16))
            qcol = e(nc.sbuf_tensor("qcol", [128, 2 * DC * 128], BF16))
            ev = e(nc.sbuf_tensor("ev_p1", [128, 4 * 512], BF16))
            self._alloc_stats(e)
            self.c_AT = {p: e(nc.sbuf_tensor(f"c_AT{p}", [R, D], BF16)) for p in "qkv"}
            self.xBq = e(nc.sbuf_tensor("xBq", [R, self.T], BF16))
            self.xBg = {p: e(nc.sbuf_tensor(f"xBg{p}", [R, self.T], BF16)) for p in "kv"}
            blk = self.block

            @blk.gpsimd
            def _(g_):
                for p in "qkv":
                    self.wlS.inc(g_.dma_start(out=self.c_AT[p][:, :], in_=self.inp[f"a{p}T"]))
            self.p1_group(0, xb, xn, sq, t1b, wcol, qcol, ev)
            self.p1b_allgather()
            self.barrier()
        with ExitStack() as s2:
            e = s2.enter_context
            ofm = e(nc.sbuf_tensor("ofm", [128, DC * self.T], BF16))
            with ExitStack() as s2a:
                ee = s2a.enter_context
                kh = ee(nc.sbuf_tensor("kh", [128, 2 * self.L], BF16))
                vraw = ee(nc.sbuf_tensor("vraw", [128, self.L], BF16))
                vext = ee(nc.sbuf_tensor("vext", [128, 2 * (self.KT * 129)], BF16))
                qh = ee(nc.sbuf_tensor("qh", [128, 2 * self.T], BF16))
                pt = ee(nc.sbuf_tensor("pt", [128, 4 * 512], BF16))
                otmp = ee(nc.sbuf_tensor("otmp", [128, 128], BF16))
                rcp = ee(nc.sbuf_tensor("rcp", [128, 4], F32))
                self.p2_attention(ofm, kh, vraw, vext, qh, pt, otmp, rcp)
                self.barrier()
            with ExitStack() as s2b:
                ee = s2b.enter_context
                wcol = ee(nc.sbuf_tensor("wcol3", [128, 2 * DC * 128], BF16))
                xtile = ee(nc.sbuf_tensor("xtile3", [128, 2 * 512], F32))
                x2t = ee(nc.sbuf_tensor("x2t3", [128, 2 * 512], F32))
                self.c_ATo = ee(nc.sbuf_tensor("c_ATo", [R, D], BF16))
                self.xBo = ee(nc.sbuf_tensor("xBo", [R, self.T], BF16))
                self.p3_oproj(ofm, wcol, xtile, x2t)
                self.barrier()
        with ExitStack() as s3:
            e = s3.enter_context
            xn2 = e(nc.sbuf_tensor("xn2", [128, DC * self.T], BF16))
            self.w_fm = e(nc.sbuf_tensor("w_fm", [NE, self.T], F32))
            with ExitStack() as s3a:
                ee = s3a.enter_context
                sq = ee(nc.sbuf_tensor("sq2", [128, 2 * self.T], BF16))
                t1b = ee(nc.sbuf_tensor("t1b2", [128, self.T], F32))
                self._alloc_stats(ee)
                self.p4_ln2(xn2, sq, t1b)
                self.barrier()
            with ExitStack() as s3z:
                ee = s3z.enter_context
                xstg = ee(nc.sbuf_tensor("xstg", [128, 2 * D], BF16))
                self.p4b_xn2tm(xn2, xstg)
                self.barrier()
            with ExitStack() as s3b:
                ee = s3b.enter_context
                pr = ee(nc.sbuf_tensor("pr", [128, 8], F32))
                rsc = ee(nc.sbuf_tensor("rsc", [128, 24], F32))
                rs2 = ee(nc.sbuf_tensor("rs2", [128, 64], F32))
                rs3 = ee(nc.sbuf_tensor("rs3", [128, 64], F32))
                self.p5_router(xn2, pr, rsc, rs2, rs3)
                self.barrier()
            with ExitStack() as s3i:
                ee = s3i.enter_context
                self.p5b_index(ee)
                self.barrier()
            with ExitStack() as s3c:
                ee = s3c.enter_context
                wg = ee(nc.sbuf_tensor("wg6", [128, 2 * DC * 128], BF16))
                wu = ee(nc.sbuf_tensor("wu6", [128, 2 * DC * 128], BF16))
                sgb = ee(nc.sbuf_tensor("sgb6", [128, 2 * 512], BF16))
                tml = ee(nc.sbuf_tensor("tml6", [128, 512], F32))
                ht = ee(nc.sbuf_tensor("ht6", [128, 2 * 512], BF16))
                self.p6_up(xn2, wg, wu, sgb, tml, ht, None)
                self.barrier()
        with ExitStack() as s4b:
            e = s4b.enter_context
            self.p6b_sparse(e)
            self.barrier()
        with ExitStack() as s4:
            e = s4.enter_context
            hsb = e(nc.sbuf_tensor("hsb7", [128, SIC * 512], BF16))
            wd = e(nc.sbuf_tensor("wd7", [128, 2 * SIC * 128], BF16))
            sgbc = e(nc.sbuf_tensor("sgbc", [128, self.T], F32))
            x2t = e(nc.sbuf_tensor("x2t7", [128, 2 * 512], F32))
            x3t = e(nc.sbuf_tensor("x3t7", [128, 2 * 512], F32))
            tml = e(nc.sbuf_tensor("tml7", [128, 2 * 512], F32))
            x3s = e(nc.sbuf_tensor("x3s7", [128, 4 * D], F32))
            self.p7_shared_down(hsb, wd, sgbc, x2t, x3t, tml, x3s)
            self.barrier()
        with ExitStack() as s5:
            e = s5.enter_context
            cA = e(nc.sbuf_tensor("cA8", [128, 2 * D], F32))
            cB = e(nc.sbuf_tensor("cB8", [128, 2 * D], F32))
            x3r = e(nc.sbuf_tensor("x3r8", [128, 2 * D], F32))
            yt = e(nc.sbuf_tensor("yt8", [128, 2 * D], F32))
            self.p8_combine(cA, cB, x3r, yt)
        self.barrier()
        self.final_wait()

    def p0_consts(self):
        blk = self.block
        inp = self.inp

        @blk.vector
        def _(v):
            v = VV(self, v)
            v.memset(self.c_eps[0:1, 0:1], EPS)

        @blk.sync
        def _(sp):
            for dst, src in [
                (self.c_bq, inp["bq_a"]), (self.c_bk, inp["bk_a"]), (self.c_bo, inp["bo_a"]),
                (self.c_idb, inp["ident_b"]), (self.c_idf, inp["ident_f"]),
                (self.c_ocb, inp["ones_col_b"]), (self.c_orb, inp["ones_row_b"]),
                (self.c_orf, inp["ones_row_f"]), (self.c_tie, inp["tie64"]),
                (self.c_sel, inp["sel8"]), (self.c_tri8, inp["tri8"]),
                (self.c_ones8, inp["ones8"]), (self.c_segb, inp["segb"]),
                (self.c_tokid, inp["tokid"]), (self.c_trash, inp["trashf"]),
            ]:
                self.ldS.inc(sp.dma_start(out=dst[:, :], in_=src))

        @blk.gpsimd
        def _(g):
            self.wl[0].inc(g.dma_start(out=self.c_bv[:, :], in_=inp["bv_row"]))
            self.wl[0].inc(g.dma_start(out=self.c_gw[:, :], in_=inp["gw_arr"]))
            for p in "qkvo":
                self.wl[0].inc(g.dma_start(out=self.c_BT[p][:, :], in_=inp[f"b{p}T_a"]))

    def _ln_core(self, xb, sq, t1b, xn, src_sem, src_val):
        blk = self.block
        T, TG = self.T, self.TG
        pe0, act0, dve0 = self.pe.n, self.act.n, self.dve.n

        @blk.scalar
        def _(s):
            s.wait_ge(src_sem, src_val)
            for dc in range(DC):
                if dc >= 2:
                    wge(s, self.pe.sem, pe0 + (dc - 1))
                self.act.inc(s.activation(sq[:, (dc % 2) * T:(dc % 2) * T + T],
                                          xb[:, dc * T:(dc + 1) * T], AF.Square))

        @blk.tensor
        def _(t):
            for dc in range(DC):
                t.wait_ge(self.act.sem, act0 + dc + 1)
                first, last = dc == 0, dc == DC - 1
                ins = None
                for tg in range(TG):
                    t.matmul(self.banks[tg][0:1, 0:512], self.c_ocb[:, 0:1],
                             xb[:, dc * T + tg * 512: dc * T + tg * 512 + 512],
                             start=first, stop=last)
                    ins = t.matmul(self.banks[TG + tg][0:1, 0:512], self.c_ocb[:, 0:1],
                                   sq[:, (dc % 2) * T + tg * 512:(dc % 2) * T + tg * 512 + 512],
                                   start=first, stop=last)
                self.pe.inc(ins)

        @blk.vector
        def _(v):
            v = VV(self, v)
            v.wait_ge(self.pe.sem, pe0 + DC)
            for tg in range(TG):
                v.tensor_copy(self.s1[0:1, tg * 512:tg * 512 + 512], self.banks[tg][0:1, 0:512])
                v.tensor_copy(self.s2[0:1, tg * 512:tg * 512 + 512], self.banks[TG + tg][0:1, 0:512])
            v.tensor_scalar(self.s1[0:1, :], self.s1[0:1, :], 1.0 / D, None, ALU.mult)
            v.tensor_scalar(self.s2[0:1, :], self.s2[0:1, :], 1.0 / D, None, ALU.mult)
            v.tensor_tensor(self.rstd[0:1, :], self.s1[0:1, :], self.s1[0:1, :], ALU.mult)
            v.tensor_tensor(self.s2[0:1, :], self.s2[0:1, :], self.rstd[0:1, :],
                            ALU.subtract)

        @blk.scalar
        def _(s):
            s.wait_ge(self.dve.sem, dve0 + 2 * TG + 4)
            self.act.inc(s.activation(self.s2[0:1, :], self.s2[0:1, :], AF.Sqrt,
                                      bias=self.c_eps[0:1, 0:1]))

        @blk.vector
        def _(v):
            v = VV(self, v)
            v.wait_ge(self.act.sem, act0 + DC + 1)
            v.reciprocal(self.rstd[0:1, :], self.s2[0:1, :])

        @blk.tensor
        def _(t):
            t.wait_ge(self.dve.sem, dve0 + 2 * TG + 5)
            for tg in range(TG):
                if tg > 0:
                    t.wait_ge(self.dve.sem, dve0 + 2 * TG + 5 + 2 * tg)
                ins = t.matmul(self.ps[0], self.c_orf[0:1, :],
                               self.s1[0:1, tg * 512:tg * 512 + 512], start=True, stop=True)
                self.pe.inc(ins)
                ins = t.matmul(self.ps[1], self.c_orf[0:1, :],
                               self.rstd[0:1, tg * 512:tg * 512 + 512], start=True, stop=True)
                self.pe.inc(ins)

        @blk.vector
        def _(v):
            v = VV(self, v)
            for tg in range(TG):
                v.wait_ge(self.pe.sem, pe0 + DC + 2 * tg + 1)
                v.tensor_copy(self.Msb[:, tg * 512:tg * 512 + 512], self.ps[0])
                v.wait_ge(self.pe.sem, pe0 + DC + 2 * tg + 2)
                v.tensor_copy(self.Rsb[:, tg * 512:tg * 512 + 512], self.ps[1])
            for dc in range(DC):
                v.tensor_tensor(t1b[:, 0:T], xb[:, dc * T:(dc + 1) * T],
                                self.Msb[:, 0:T], ALU.subtract)
                v.tensor_tensor(xn[:, dc * T:(dc + 1) * T],
                                t1b[:, 0:T], self.Rsb[:, 0:T], ALU.mult)
        return self.dve.n

    def p1_group(self, g, xb, xn, sq, t1b, wcol, qcol, ev):
        blk = self.block
        T, TG, TC = self.T, self.TG, self.TC
        mine = True

        @blk.gpsimd
        def _(gp):
            self.wlS.inc(gp.dma_start(
                out=xb[:, 0:DC * T].rearrange("p (dc t) -> p dc t", t=T),
                in_=self.xt_in[:, 0:T].rearrange("(dc p) t -> p dc t", p=128)))
        wl_xb = self.wlS.n

        dve_norm = self._ln_core(xb, sq, t1b, xn, self.wlS.sem, wl_xb)

        pe_xb0 = self.pe.n
        act_xb0 = self.act.n
        projs = ["k", "v"] + (["q"] if mine else [])
        xBdst = {"k": self.xBg["k"], "v": self.xBg["v"], "q": self.xBq}

        @blk.tensor
        def _(t):
            t.wait_ge(self.dve.sem, dve_norm)
            t.wait_ge(self.wlS.sem, wl_xb)
            for i, p in enumerate(projs):
                for tg in range(TG):
                    wge(t, self.act.sem, act_xb0 + i * TG + tg)
                    ins = None
                    for dc in range(DC):
                        ins = t.matmul(self.ps[i][0:R, 0:512],
                                       self.c_BT[p][:, dc * R:(dc + 1) * R],
                                       xn[:, dc * T + tg * 512: dc * T + tg * 512 + 512],
                                       start=(dc == 0), stop=(dc == DC - 1))
                    self.pe.inc(ins)

        @blk.scalar
        def _(s):
            for i, p in enumerate(projs):
                for tg in range(TG):
                    s.wait_ge(self.pe.sem, pe_xb0 + i * TG + tg + 1)
                    self.act.inc(s.activation(xBdst[p][0:R, tg * 512:tg * 512 + 512],
                                              self.ps[i][0:R, 0:512], AF.Copy, scale=0.5))
        act_ev0 = self.act.n
        pe_proj0 = self.pe.n
        assert pe_proj0 == pe_xb0 + len(projs) * TG

        nproj = len(projs)
        per_oc = TG + TC + (TG if mine else 0)
        wl_b = [self.wl[0].n, self.wl[1].n]
        st_b = [c.n for c in self.st4]

        def gidx(oc, cls, j):
            off = {"k": 0, "v": TG, "q": TG + TC}[cls]
            return oc * per_oc + off + j

        @blk.gpsimd
        def _(gp):
            for oc in range(DC):
                par = oc % 2
                if oc >= 2:
                    wge(gp, self.pe.sem, pe_proj0 + (oc - 1) * per_oc)
                self.block_load(gp, wcol[:, (2 * par) * DC * 128:(2 * par + 1) * DC * 128],
                                self.inp["wkT"], oc, self.wl[par])
                self.block_load(gp, wcol[:, (2 * par + 1) * DC * 128:(2 * par + 2) * DC * 128],
                                self.inp["wvT"], oc, self.wl[par])
                if mine:
                    self.block_load(gp, qcol[:, par * DC * 128:(par + 1) * DC * 128],
                                    self.inp["wqT"], oc, self.wl[par])

        @blk.tensor
        def _(t):
            t.wait_ge(self.act.sem, act_ev0)
            for oc in range(DC):
                par = oc % 2
                t.wait_ge(self.wl[par].sem, wl_b[par] + 16 * nproj * (oc // 2 + 1))
                kw = wcol[:, (2 * par) * DC * 128:(2 * par) * DC * 128 + DC * 128]
                vw = wcol[:, (2 * par + 1) * DC * 128:(2 * par + 1) * DC * 128 + DC * 128]
                for tg in range(TG):
                    i = oc * TG + tg
                    bank = self.ps[i % 2]
                    if i >= 2:
                        oc2, tg2 = divmod(i - 2, TG)
                        wge(t, self.act.sem, act_ev0 + gidx(oc2, "k", tg2) + 1)
                    for dc in range(DC):
                        t.matmul(bank, kw[:, dc * 128:(dc + 1) * 128],
                                 xn[:, dc * T + tg * 512: dc * T + tg * 512 + 512],
                                 start=(dc == 0), stop=False)
                    ins = t.matmul(bank, self.c_AT["k"][0:R, oc * 128:(oc + 1) * 128],
                                   self.xBg["k"][0:R, tg * 512:tg * 512 + 512],
                                   start=False, stop=True)
                    self.pe.inc(ins)
                for tc in range(TC):
                    i = oc * TC + tc
                    bank = self.ps[2 + (i % 2)]
                    if i >= 2:
                        oc2, tc2 = divmod(i - 2, TC)
                        wge(t, self.act.sem, act_ev0 + gidx(oc2, "v", tc2) + 1)
                    for dc in range(DC):
                        t.matmul(bank[:, 0:128], xn[:, dc * T + tc * 128: dc * T + tc * 128 + 128],
                                 vw[:, dc * 128:(dc + 1) * 128], start=(dc == 0), stop=False)
                    t.matmul(bank[:, 0:128], self.xBg["v"][0:R, tc * 128:tc * 128 + 128],
                             self.c_AT["v"][0:R, oc * 128:(oc + 1) * 128], start=False, stop=False)
                    ins = t.matmul(bank[:, 0:128], self.c_orb[0:1, :],
                                   self.c_bv[0:1, oc * 128:(oc + 1) * 128], start=False, stop=True)
                    self.pe.inc(ins)
                if mine:
                    qw = qcol[:, par * DC * 128: par * DC * 128 + DC * 128]
                    for tg in range(TG):
                        i = oc * TG + tg
                        bank = self.ps[4 + (i % 2)]
                        if i >= 2:
                            oc2, tg2 = divmod(i - 2, TG)
                            wge(t, self.act.sem, act_ev0 + gidx(oc2, "q", tg2) + 1)
                        for dc in range(DC):
                            t.matmul(bank, qw[:, dc * 128:(dc + 1) * 128],
                                     xn[:, dc * T + tg * 512: dc * T + tg * 512 + 512],
                                     start=(dc == 0), stop=False)
                        ins = t.matmul(bank, self.c_AT["q"][0:R, oc * 128:(oc + 1) * 128],
                                       self.xBq[0:R, tg * 512:tg * 512 + 512],
                                       start=False, stop=True)
                        self.pe.inc(ins)

        @blk.scalar
        def _(s):
            for oc in range(DC):
                for tg in range(TG):
                    gi = gidx(oc, "k", tg)
                    s.wait_ge(self.pe.sem, pe_proj0 + gi + 1)
                    wge(s, self.st4[gi % 4].sem, st_b[gi % 4] + 16 * (gi // 4))
                    self.act.inc(s.activation(ev[:, (gi % 4) * 512:(gi % 4) * 512 + 512],
                                              self.ps[(oc * TG + tg) % 2], AF.Identity,
                                              bias=self.c_bk[:, oc:oc + 1]))
                for tc in range(TC):
                    gi = gidx(oc, "v", tc)
                    s.wait_ge(self.pe.sem, pe_proj0 + gi + 1)
                    wge(s, self.st4[gi % 4].sem, st_b[gi % 4] + 16 * (gi // 4))
                    self.act.inc(s.activation(ev[:, (gi % 4) * 512:(gi % 4) * 512 + 128],
                                              self.ps[2 + ((oc * TC + tc) % 2)][:, 0:128],
                                              AF.Copy))
                if mine:
                    for tg in range(TG):
                        gi = gidx(oc, "q", tg)
                        s.wait_ge(self.pe.sem, pe_proj0 + gi + 1)
                        wge(s, self.st4[gi % 4].sem, st_b[gi % 4] + 16 * (gi // 4))
                        self.act.inc(s.activation(ev[:, (gi % 4) * 512:(gi % 4) * 512 + 512],
                                                  self.ps[4 + ((oc * TG + tg) % 2)], AF.Identity,
                                                  bias=self.c_bq[:, oc:oc + 1]))

        @blk.sync
        def _(sp):
            for oc in range(DC):
                for tg in range(TG):
                    gi = gidx(oc, "k", tg)
                    sp.wait_ge(self.act.sem, act_ev0 + gi + 1)
                    self.st4[gi % 4].inc(sp.dma_start(
                        out=self.kag_in[oc * 128:(oc + 1) * 128,
                                        tg * 512: tg * 512 + 512],
                        in_=ev[:, (gi % 4) * 512:(gi % 4) * 512 + 512]))
                for tc in range(TC):
                    gi = gidx(oc, "v", tc)
                    sp.wait_ge(self.act.sem, act_ev0 + gi + 1)
                    self.st4[gi % 4].inc(sp.dma_start(
                        out=self.vag_in[oc * 128:(oc + 1) * 128, tc * 128:tc * 128 + 128],
                        in_=ev[:, (gi % 4) * 512:(gi % 4) * 512 + 128]))
                if mine:
                    for tg in range(TG):
                        gi = gidx(oc, "q", tg)
                        sp.wait_ge(self.act.sem, act_ev0 + gi + 1)
                        self.st4[gi % 4].inc(sp.dma_start(
                            out=self.qd[oc * 128:(oc + 1) * 128, tg * 512:tg * 512 + 512],
                            in_=ev[:, (gi % 4) * 512:(gi % 4) * 512 + 512]))

        if mine and "dbg_q" in self.dbg:
            @blk.sync
            def _(sp):
                for c in self.st4:
                    c.wait(sp)
                self.st4[0].inc(sp.dma_start(out=self.dbg["dbg_q"], in_=self.qd))

    NCH = 8

    def p1b_allgather(self):
        blk = self.block
        rg = [[0, 1, 2, 3], [4, 5, 6, 7]]
        NCH = self.NCH
        hb = H // NCH
        rows = hb * 128

        @blk.gpsimd
        def _(g):
            for c in self.st4:
                c.wait(g)
            for c in range(NCH):
                self.cc.inc(g.collective_compute(
                    "AllGather", ALU.bypass, replica_groups=rg,
                    ins=[self.kag_in[c * rows:(c + 1) * rows, :]],
                    outs=[self.kd[c * 4 * rows:(c + 1) * 4 * rows, :]]))
                self.cc.inc(g.collective_compute(
                    "AllGather", ALU.bypass, replica_groups=rg,
                    ins=[self.vag_in[c * rows:(c + 1) * rows, :]],
                    outs=[self.vd[c * 4 * rows:(c + 1) * 4 * rows, :]]))

        if "dbg_k" in self.dbg:
            @blk.sync
            def _(sp):
                sp.wait_ge(self.cc.sem, self.cc.n)
                self.st4[0].inc(sp.dma_start(out=self.dbg["dbg_k"], in_=self.kd))
                self.st4[0].inc(sp.dma_start(out=self.dbg["dbg_v"], in_=self.vd))

    def p2_attention(self, ofm, kh, vraw, vext, qh, pt, otmp, rcp):
        blk = self.block
        T, L, KT = self.T, self.L, self.KT

        @blk.vector
        def _(v):
            v = VV(self, v)
            for b in range(2):
                ve = vext[:, b * (KT * 129):(b + 1) * (KT * 129)]
                v.memset(
                    ve.rearrange("p (kt j) -> p kt j", j=129)[:, :, 128:129], 1.0)
        dve_pre = self.dve.n

        cc_base = self.cc.n - 2 * self.NCH

        def kd_row(h, r):
            hb = H // self.NCH
            c, hh = divmod(h, hb)
            return c * 4 * hb * 128 + r * hb * 128 + hh * 128

        @blk.sync
        def _(sp):
            sp.wait_ge(self.cc.sem, cc_base + 2)
            self.ld[0].inc(sp.dma_start(out=qh[:, 0:T], in_=self.qd[0:128, :]))
            for r in range(4):
                self.ld[0].inc(sp.dma_start(out=kh[:, r * T:(r + 1) * T],
                                            in_=self.kd[kd_row(0, r): kd_row(0, r) + 128, :]))
                self.ldS.inc(sp.dma_start(out=vraw[:, r * T:(r + 1) * T],
                                          in_=self.vd[kd_row(0, r): kd_row(0, r) + 128, :]))
        self._cc_base = cc_base
        self._kd_row = kd_row

        prev = dict(pe_end=self.pe.n, dve_end=self.dve.n, inter_done=dve_pre,
                    ld_qk=self.ld[0].n, ld_v=self.ldS.n)
        for h in range(H):
            prev = self._p2_head(h, ofm, kh, vraw, vext, qh, pt, otmp, rcp, prev)

        if "dbg_o" in self.dbg:
            @blk.sync
            def _(sp):
                sp.wait_ge(self.dve.sem, self.dve.n)
                self.st4[0].inc(sp.dma_start(out=self.dbg["dbg_o"], in_=ofm[:, 0:DC * T]))

    def _p2_head(self, h, ofm, kh, vraw, vext, qh, pt, otmp, rcp, prev):
        blk = self.block
        T, L, KT = self.T, self.L, self.KT
        QG = T // 512
        par = h % 2
        pe0, act0, dve0 = self.pe.n, self.act.n, self.dve.n
        PEQG = 2 * KT + 4
        DVQG = 12

        @blk.vector
        def _(v):
            v = VV(self, v)
            v.wait_ge(self.ldS.sem, prev["ld_v"])
            if h >= 1:
                v.wait_ge(self.pe.sem, prev["pe_end"])
            ve = vext[:, par * (KT * 129):par * (KT * 129) + KT * 129]
            v.tensor_copy(
                ve.rearrange("p (kt j) -> p kt j", j=129)[:, :, 0:128],
                vraw[:, :].rearrange("p (kt j) -> p kt j", j=128))
        inter_done = dve0 + 1

        ld_qk_next, ld_v_next = prev["ld_qk"], prev["ld_v"]
        if h + 1 < H:
            @blk.sync
            def _(sp):
                npar = (h + 1) % 2
                if h >= 1:
                    sp.wait_ge(self.pe.sem, prev["pe_end"])
                hb = H // self.NCH
                sp.wait_ge(self.cc.sem, self._cc_base + 2 * ((h + 1) // hb + 1))
                self.ld[npar].inc(sp.dma_start(out=qh[:, npar * T:npar * T + T],
                                               in_=self.qd[(h + 1) * 128:(h + 2) * 128, :]))
                for r in range(4):
                    row = self._kd_row(h + 1, r)
                    self.ld[npar].inc(sp.dma_start(
                        out=kh[:, npar * L + r * T: npar * L + (r + 1) * T],
                        in_=self.kd[row: row + 128, :]))
                sp.wait_ge(self.dve.sem, inter_done)
                for r in range(4):
                    row = self._kd_row(h + 1, r)
                    self.ldS.inc(sp.dma_start(
                        out=vraw[:, r * T:(r + 1) * T],
                        in_=self.vd[row: row + 128, :]))
            ld_qk_next = self.ld[(h + 1) % 2].n
            ld_v_next = self.ldS.n

        @blk.tensor
        def _(t):
            t.wait_ge(self.ld[par].sem, prev["ld_qk"])
            t.wait_ge(self.dve.sem, inter_done)
            kb = kh[:, par * L:par * L + L]
            qb = qh[:, par * T:par * T + T]
            ve = vext[:, par * (KT * 129):par * (KT * 129) + KT * 129]
            def emit_scores(qg, kt):
                g = qg * KT + kt
                sb_ = [self.ps[0], self.ps[1], self.banks[7], self.banks[6]][g % 4]
                wge(t, self.act.sem, act0 + g - 3)
                ins = t.matmul(sb_, kb[:, kt * 128:(kt + 1) * 128],
                               qb[:, qg * 512:qg * 512 + 512], start=True, stop=True)
                self.pe.inc(ins)

            for qg in range(QG):
                if qg == 0:
                    t.wait_ge(self.dve.sem, prev["dve_end"])
                else:
                    t.wait_ge(self.dve.sem, dve0 + 1 + DVQG * qg)
                emit_scores(qg, 0)
                emit_scores(qg, 1)
                emit_scores(qg, 2)
                for kt in range(KT):
                    if kt + 3 < KT:
                        emit_scores(qg, kt + 3)
                    t.wait_ge(self.act.sem, act0 + qg * KT + kt + 1)
                    pb = pt[:, ((qg * KT + kt) % 4) * 512:((qg * KT + kt) % 4) * 512 + 512]
                    ins = None
                    for qs in range(4):
                        ins = t.matmul(self.ps[2 + qs][:, 0:129],
                                       pb[:, qs * 128:(qs + 1) * 128],
                                       ve[:, kt * 129:(kt + 1) * 129],
                                       start=(kt == 0), stop=(kt == KT - 1))
                    self.pe.inc(ins)
                for qs in range(4):
                    t.wait_ge(self.dve.sem, dve0 + 1 + DVQG * qg + 3 * qs + 2)
                    ins = t.transpose(self.psT[:, 0:128], otmp[:, :], self.c_idb[:, :])
                    self.pe.inc(ins)
        pe_end = self.pe.n

        @blk.scalar
        def _(s):
            for qg in range(QG):
                for kt in range(KT):
                    off = kt + 1 if kt < 3 else 2 * kt - 2
                    s.wait_ge(self.pe.sem, pe0 + qg * PEQG + off)
                    g = qg * KT + kt
                    sb_ = [self.ps[0], self.ps[1], self.banks[7], self.banks[6]][g % 4]
                    pb = pt[:, (g % 4) * 512:(g % 4) * 512 + 512]
                    self.act.inc(s.activation(pb, sb_, AF.Exp, scale=SCALE))

        @blk.vector
        def _(v):
            v = VV(self, v)
            for qg in range(QG):
                for qs in range(4):
                    v.wait_ge(self.pe.sem, pe0 + qg * PEQG + 2 * KT)
                    ob = self.ps[2 + qs]
                    v.reciprocal(rcp[:, qs:qs + 1], ob[:, 128:129])
                    if qg > 0 or qs > 0:
                        pqg, pqs = (qg, qs - 1) if qs > 0 else (qg - 1, 3)
                        v.wait_ge(self.pe.sem, pe0 + pqg * PEQG + 2 * KT + pqs + 1)
                    v.tensor_scalar(otmp[:, :], ob[:, 0:128],
                                    rcp[:, qs:qs + 1], None, ALU.mult)
                    v.wait_ge(self.pe.sem, pe0 + qg * PEQG + 2 * KT + qs + 1)
                    v.tensor_copy(
                        ofm[:, h * T + qg * 512 + qs * 128: h * T + qg * 512 + qs * 128 + 128],
                        self.psT[:, 0:128])
        dve_end = self.dve.n

        return dict(pe_end=pe_end, dve_end=dve_end, inter_done=inter_done,
                    ld_qk=ld_qk_next, ld_v=ld_v_next)

    def p3_oproj(self, ofm, wcol, xtile, x2t):
        blk = self.block
        T, TG = self.T, self.TG

        @blk.gpsimd
        def _(gp):
            self.wlS.inc(gp.dma_start(out=self.c_ATo[:, :], in_=self.inp["aoT"]))
        wl_at = self.wlS.n

        pe0, act0, dve0 = self.pe.n, self.act.n, self.dve.n
        ld_b = [self.ld[0].n, self.ld[1].n]
        st_b = [self.st4[0].n, self.st4[1].n]
        wl_b = [self.wl[0].n, self.wl[1].n]

        @blk.tensor
        def _(t):
            for tg in range(TG):
                wge(t, self.act.sem, act0 + tg)
                ins = None
                for dc in range(DC):
                    ins = t.matmul(self.ps[2][0:R, 0:512],
                                   self.c_BT["o"][:, dc * R:(dc + 1) * R],
                                   ofm[:, dc * T + tg * 512: dc * T + tg * 512 + 512],
                                   start=(dc == 0), stop=(dc == DC - 1))
                self.pe.inc(ins)

        @blk.scalar
        def _(s):
            for tg in range(TG):
                s.wait_ge(self.pe.sem, pe0 + tg + 1)
                self.act.inc(s.activation(self.xBo[0:R, tg * 512:tg * 512 + 512],
                                          self.ps[2][0:R, 0:512], AF.Copy, scale=0.5))
        act_xbo = self.act.n

        @blk.gpsimd
        def _(gp):
            for oc in range(DC):
                par = oc % 2
                if oc >= 2:
                    wge(gp, self.pe.sem, pe0 + TG + (oc - 1) * TG)
                self.block_load(gp, wcol[:, par * DC * 128:(par + 1) * DC * 128],
                                self.inp["woT"], oc, self.wl[par])

        @blk.sync
        def _(sp):
            NI = DC * TG
            for i in range(NI + 2):
                if i >= 2:
                    j = i - 2
                    sp.wait_ge(self.dve.sem, dve0 + j + 1)
                    self.st4[j % 2].inc(sp.dma_start(
                        out=self.x2d[(j // TG) * 128:(j // TG + 1) * 128,
                                     (j % TG) * 512:(j % TG) * 512 + 512],
                        in_=x2t[:, (j % 2) * 512:(j % 2) * 512 + 512]))
                if i < NI:
                    par2 = i % 2
                    self.ld[par2].inc(sp.dma_start(
                        out=xtile[:, par2 * 512:par2 * 512 + 512],
                        in_=self.xt_in[(i // TG) * 128:(i // TG + 1) * 128,
                                       (i % TG) * 512:(i % TG) * 512 + 512]))

        @blk.tensor
        def _(t):
            t.wait_ge(self.act.sem, act_xbo)
            t.wait_ge(self.wlS.sem, wl_at)
            for oc in range(DC):
                par = oc % 2
                t.wait_ge(self.wl[par].sem, wl_b[par] + 16 * (oc // 2 + 1))
                ow = wcol[:, par * DC * 128:par * DC * 128 + DC * 128]
                for qg in range(TG):
                    i = oc * TG + qg
                    bank = self.ps[i % 2]
                    if i >= 2:
                        wge(t, self.dve.sem, dve0 + i - 1)
                    for dc in range(DC):
                        t.matmul(bank, ow[:, dc * 128:(dc + 1) * 128],
                                 ofm[:, dc * T + qg * 512: dc * T + qg * 512 + 512],
                                 start=(dc == 0), stop=False)
                    ins = t.matmul(bank, self.c_ATo[0:R, oc * 128:(oc + 1) * 128],
                                   self.xBo[0:R, qg * 512:qg * 512 + 512],
                                   start=False, stop=True)
                    self.pe.inc(ins)
        pe_op0 = pe0 + TG

        @blk.vector
        def _(v):
            v = VV(self, v)
            for oc in range(DC):
                for qg in range(TG):
                    i = oc * TG + qg
                    par2 = i % 2
                    v.wait_ge(self.pe.sem, pe_op0 + i + 1)
                    v.wait_ge(self.ld[par2].sem, ld_b[par2] + 16 * (i // 2 + 1))
                    wge(v, self.st4[par2].sem, st_b[par2] + 16 * (i // 2))
                    bank = self.ps[i % 2]
                    v.scalar_tensor_tensor(
                        x2t[:, par2 * 512:par2 * 512 + 512], bank,
                        self.c_bo[:, oc:oc + 1], xtile[:, par2 * 512:par2 * 512 + 512],
                        ALU.add, ALU.add)

        if "dbg_x2" in self.dbg:
            @blk.sync
            def _(sp):
                for c in self.st4:
                    c.wait(sp)
                self.st4[0].inc(sp.dma_start(out=self.dbg["dbg_x2"], in_=self.x2d))

    def p4_ln2(self, xn2, sq, t1b):
        blk = self.block

        @blk.gpsimd
        def _(gp):
            self.wlS.inc(gp.dma_start(
                out=xn2[:, 0:DC * self.T].rearrange("p (dc t) -> p dc t", t=self.T),
                in_=self.x2d.rearrange("(dc p) t -> p dc t", p=128)))
        wl_x = self.wlS.n

        self._ln_core(xn2, sq, t1b, xn2, self.wlS.sem, wl_x)

        if "dbg_xn2" in self.dbg:
            @blk.sync
            def _(sp):
                sp.wait_ge(self.dve.sem, self.dve.n)
                self.st4[0].inc(sp.dma_start(out=self.dbg["dbg_xn2"], in_=xn2[:, 0:DC * self.T]))

    def p5_router(self, xn2, pr, rsc, rs2, rs3):
        blk = self.block
        T, TC = self.T, self.TC
        for tc in range(TC):
            pe_tc0, act_tc0, dve_tc0 = self.pe.n, self.act.n, self.dve.n

            @blk.tensor
            def _(t):
                wge(t, self.dve.sem, dve_tc0)
                wge(t, self.act.sem, act_tc0)
                ins = None
                for dc in range(DC):
                    ins = t.matmul(self.ps[0][:, 0:9],
                                   xn2[:, dc * T + tc * 128: dc * T + tc * 128 + 128],
                                   self.c_gw[:, dc * 9:(dc + 1) * 9],
                                   start=(dc == 0), stop=(dc == DC - 1))
                self.pe.inc(ins)

            @blk.vector
            def _(v):
                v = VV(self, v)
                v.wait_ge(self.pe.sem, pe_tc0 + 1)
                v.reduce_max(rsc[:, 0:1], self.ps[0][:, 0:8], axis=AX.X)
                v.tensor_scalar(rsc[:, 1:2], rsc[:, 0:1], -1.0, None, ALU.mult)

            @blk.scalar
            def _(s):
                s.wait_ge(self.dve.sem, dve_tc0 + 2)
                self.act.inc(s.activation(pr[:, 0:8], self.ps[0][:, 0:8], AF.Exp,
                                          bias=rsc[:, 1:2]))
                self.act.inc(s.activation(rsc[:, 2:3], self.ps[0][:, 8:9], AF.Sigmoid))

            @blk.vector
            def _(v):
                v = VV(self, v)
                v.wait_ge(self.act.sem, act_tc0 + 1)
                v.reduce_sum(rsc[:, 3:4], pr[:, 0:8], axis=AX.X)
                v.reciprocal(rsc[:, 4:5], rsc[:, 3:4])
                v.tensor_scalar(pr[:, 0:8], pr[:, 0:8], rsc[:, 4:5], None, ALU.mult)
                prA = pr[:, 0:8].unsqueeze(2).broadcast_to([128, 8, 8])
                prB = pr[:, 0:8].unsqueeze(1).broadcast_to([128, 8, 8])
                v.tensor_tensor(rs2[:, 0:64], prA, prB, ALU.is_lt)
                v.tensor_tensor(rs3[:, 0:64], prA, prB, ALU.is_equal)
                v.tensor_tensor(rs3[:, 0:64], rs3[:, 0:64], self.c_tie[:, 0:64], ALU.mult)
                v.tensor_tensor(rs2[:, 0:64], rs2[:, 0:64], rs3[:, 0:64], ALU.add)
                v.tensor_reduce(rsc[:, 8:16], rs2[:, 0:64].rearrange("p (e j) -> p e j", j=8),
                                AX.X, ALU.add)
                v.tensor_scalar(rsc[:, 16:24], rsc[:, 8:16], 2.0, None, ALU.is_lt)
                v.tensor_tensor(pr[:, 0:8], pr[:, 0:8], rsc[:, 16:24], ALU.mult)

            @blk.tensor
            def _(t):
                t.wait_ge(self.dve.sem, dve_tc0 + 12)
                ins = t.transpose(self.ps[1][0:8, 0:128], pr[:, 0:8], self.c_idf[:, :])
                self.pe.inc(ins)
                t.wait_ge(self.act.sem, act_tc0 + 2)
                ins = t.transpose(self.ps[2][0:1, 0:128], rsc[:, 2:3], self.c_idf[:, :])
                self.pe.inc(ins)

            @blk.vector
            def _(v):
                v = VV(self, v)
                v.wait_ge(self.pe.sem, pe_tc0 + 3)
                v.tensor_copy(self.w_fm[0:NE, tc * 128:(tc + 1) * 128], self.ps[1][0:8, 0:128])
                v.tensor_copy(self.sg_fm[0:1, tc * 128:(tc + 1) * 128],
                              self.ps[2][0:1, 0:128])

        if "dbg_w" in self.dbg:
            @blk.sync
            def _(sp):
                sp.wait_ge(self.dve.sem, self.dve.n)
                self.st4[0].inc(sp.dma_start(out=self.dbg["dbg_w"], in_=self.w_fm[0:NE, 0:T]))
                self.st4[0].inc(sp.dma_start(out=self.dbg["dbg_sg"], in_=self.sg_fm[0:1, 0:T]))

    def p6_up(self, xn2, wg, wu, sgb, tml, ht, web):
        blk = self.block
        T, TG = self.T, self.TG
        segs = [("sh", SIC, None)]
        prev_dve_end = self.dve.n
        for name, nic, eidx in segs:
            pe0, act0, dve0 = self.pe.n, self.act.n, self.dve.n
            wl_b = [self.wl[0].n, self.wl[1].n]
            st_b = [self.st4[0].n, self.st4[1].n]
            expert = eidx is not None
            pe_pre = TG if expert else 0
            dve_pre = TG if expert else 0
            dve_per = 2 if expert else 1

            @blk.gpsimd
            def _(gp):
                gp.wait_ge(self.pe.sem, pe0)
                for ic in range(nic):
                    par = ic % 2
                    if ic >= 2:
                        wge(gp, self.pe.sem, pe0 + pe_pre + 2 * TG * (ic - 1))
                    if expert:
                        b0 = eidx * 2 * MIC + ic
                        b1 = eidx * 2 * MIC + MIC + ic
                        src = self.inp["w13T"]
                    else:
                        b0 = ic
                        b1 = SIC + ic
                        src = self.inp["wguT"]
                    self.block_load(gp, wg[:, par * DC * 128:(par + 1) * DC * 128],
                                    src, b0, self.wl[par])
                    self.block_load(gp, wu[:, par * DC * 128:(par + 1) * DC * 128],
                                    src, b1, self.wl[par])

            if expert:
                @blk.tensor
                def _(t):
                    t.wait_ge(self.dve.sem, prev_dve_end)
                    for tg in range(TG):
                        if tg > 0:
                            t.wait_ge(self.dve.sem, dve0 + tg)
                        ins = t.matmul(self.ps[4], self.c_sel[0:NE, eidx * 128:(eidx + 1) * 128],
                                       self.w_fm[0:NE, tg * 512:tg * 512 + 512],
                                       start=True, stop=True)
                        self.pe.inc(ins)

                @blk.vector
                def _(v):
                    v = VV(self, v)
                    for tg in range(TG):
                        v.wait_ge(self.pe.sem, pe0 + tg + 1)
                        v.tensor_copy(web[:, tg * 512:tg * 512 + 512], self.ps[4])

            pe1 = pe0 + pe_pre

            @blk.tensor
            def _(t):
                if not expert:
                    t.wait_ge(self.dve.sem, prev_dve_end)
                for ic in range(nic):
                    par = ic % 2
                    t.wait_ge(self.wl[par].sem, wl_b[par] + 32 * (ic // 2 + 1))
                    for tg in range(TG):
                        it = ic * TG + tg
                        par2 = it % 2
                        gbank = self.ps[0 + 2 * par2]
                        ubank = self.ps[1 + 2 * par2]
                        if it >= 2:
                            wge(t, self.act.sem, act0 + (it - 1))
                            wge(t, self.dve.sem, dve0 + dve_pre + dve_per * (it - 1))
                        ins = None
                        for dc in range(DC):
                            ins = t.matmul(gbank, wg[:, par * DC * 128 + dc * 128: par * DC * 128 + (dc + 1) * 128],
                                           xn2[:, dc * T + tg * 512: dc * T + tg * 512 + 512],
                                           start=(dc == 0), stop=(dc == DC - 1))
                        self.pe.inc(ins)
                        ins = None
                        for dc in range(DC):
                            ins = t.matmul(ubank, wu[:, par * DC * 128 + dc * 128: par * DC * 128 + (dc + 1) * 128],
                                           xn2[:, dc * T + tg * 512: dc * T + tg * 512 + 512],
                                           start=(dc == 0), stop=(dc == DC - 1))
                        self.pe.inc(ins)

            @blk.scalar
            def _(s):
                for ic in range(nic):
                    for tg in range(TG):
                        it = ic * TG + tg
                        par2 = it % 2
                        s.wait_ge(self.pe.sem, pe1 + 2 * it + 1)
                        if it >= 2:
                            wge(s, self.dve.sem, dve0 + dve_pre + dve_per * (it - 1))
                        gbank = self.ps[0 + 2 * par2]
                        self.act.inc(s.activation(sgb[:, par2 * 512:par2 * 512 + 512],
                                                  gbank, AF.Silu))

            @blk.vector
            def _(v):
                v = VV(self, v)
                for ic in range(nic):
                    for tg in range(TG):
                        it = ic * TG + tg
                        par2 = it % 2
                        v.wait_ge(self.pe.sem, pe1 + 2 * it + 2)
                        v.wait_ge(self.act.sem, act0 + it + 1)
                        wge(v, self.st4[par2].sem, st_b[par2] + 16 * (it // 2))
                        ubank = self.ps[1 + 2 * par2]
                        if expert:
                            v.tensor_tensor(tml[:, 0:512],
                                            sgb[:, par2 * 512:par2 * 512 + 512],
                                            ubank, ALU.mult)
                            v.tensor_tensor(ht[:, par2 * 512:par2 * 512 + 512],
                                            tml[:, 0:512],
                                            web[:, tg * 512:tg * 512 + 512], ALU.mult)
                        else:
                            v.tensor_tensor(ht[:, par2 * 512:par2 * 512 + 512],
                                            sgb[:, par2 * 512:par2 * 512 + 512],
                                            ubank, ALU.mult)

            @blk.sync
            def _(sp):
                for ic in range(nic):
                    for tg in range(TG):
                        it = ic * TG + tg
                        par2 = it % 2
                        sp.wait_ge(self.dve.sem, dve0 + dve_pre + dve_per * (it + 1))
                        if expert:
                            rows0 = tg * NE * MI + eidx * MI + ic * 128
                            dst = self.hexd[rows0:rows0 + 128, 0:512]
                        else:
                            rows0 = tg * SI + ic * 128
                            dst = self.hshd[rows0:rows0 + 128, 0:512]
                        self.st4[par2].inc(sp.dma_start(out=dst,
                                                        in_=ht[:, par2 * 512:par2 * 512 + 512]))
            prev_dve_end = self.dve.n

        if "dbg_hsh" in self.dbg:
            @blk.sync
            def _(sp):
                for c in self.st4:
                    c.wait(sp)
                self.st4[0].inc(sp.dma_start(out=self.dbg["dbg_hsh"], in_=self.hshd))


    def p4b_xn2tm(self, xn2, xstg):
        blk = self.block
        T, TC = self.T, self.TC
        pe0, dve0 = self.pe.n, self.dve.n
        st_b = [self.st4[2].n, self.st4[3].n]

        @blk.tensor
        def _(t):
            for tc in range(TC):
                for dc in range(DC):
                    i = tc * DC + dc
                    wge(t, self.dve.sem, dve0 + i - 1)
                    tb = self.psT if i % 2 == 0 else self.psT2
                    ins = t.transpose(tb[:, 0:128],
                                      xn2[:, dc * T + tc * 128: dc * T + tc * 128 + 128],
                                      self.c_idb[:, :])
                    self.pe.inc(ins)

        @blk.vector
        def _(v):
            v = VV(self, v)
            for tc in range(TC):
                par = tc % 2
                for dc in range(DC):
                    i = tc * DC + dc
                    v.wait_ge(self.pe.sem, pe0 + i + 1)
                    if dc == 0:
                        wge(v, self.st4[2 + par].sem, st_b[par] + 16 * (tc // 2))
                    srcb = self.psT if i % 2 == 0 else self.psT2
                    v.tensor_copy(xstg[:, par * D + dc * 128: par * D + (dc + 1) * 128],
                                  srcb[:, 0:128])

        @blk.sync
        def _(sp):
            for tc in range(TC):
                par = tc % 2
                sp.wait_ge(self.dve.sem, dve0 + (tc + 1) * DC)
                self.st4[2 + par].inc(sp.dma_start(
                    out=self.xn2tm[tc * 128:(tc + 1) * 128, :],
                    in_=xstg[:, par * D:(par + 1) * D]))
        self.xn2tm_st = [self.st4[2].n, self.st4[3].n]

    def p5b_index(self, ee):
        nc = self.nc
        blk = self.block
        T, TC, CAP, NSL, PMW = self.T, self.TC, self.CAP, self.NSL, self.PMW
        NCHK = T // 512
        mfm = ee(nc.sbuf_tensor("i_mfm", [NE, T], F32))
        sfm = ee(nc.sbuf_tensor("i_sfm", [NE, T], F32))
        gfm = ee(nc.sbuf_tensor("i_gfm", [NE, T], F32))
        cum = ee(nc.sbuf_tensor("i_cum", [NE, T], F32))
        m1 = ee(nc.sbuf_tensor("i_m1", [NE, T], F32))
        m2 = ee(nc.sbuf_tensor("i_m2", [NE, T], F32))
        amk = ee(nc.sbuf_tensor("i_amk", [NE, T], F32))
        bmk = ee(nc.sbuf_tensor("i_bmk", [NE, T], F32))
        rowA = ee(nc.sbuf_tensor("i_rowA", [1, T], F32))
        rowB = ee(nc.sbuf_tensor("i_rowB", [1, T], F32))
        wrA = ee(nc.sbuf_tensor("i_wrA", [1, T], F32))
        wrB = ee(nc.sbuf_tensor("i_wrB", [1, T], F32))
        saf = ee(nc.sbuf_tensor("i_saf", [128, TC], F32))
        sbf = ee(nc.sbuf_tensor("i_sbf", [128, TC], F32))
        paf = ee(nc.sbuf_tensor("i_paf", [128, TC], F32))
        pbf = ee(nc.sbuf_tensor("i_pbf", [128, TC], F32))
        paf_i = ee(nc.sbuf_tensor("i_paf_i", [128, TC], I32))
        pbf_i = ee(nc.sbuf_tensor("i_pbf_i", [128, TC], I32))
        waf = ee(nc.sbuf_tensor("i_waf", [128, TC], F32))
        wbf = ee(nc.sbuf_tensor("i_wbf", [128, TC], F32))
        zrow = ee(nc.sbuf_tensor("i_zrow", [128, 512], F32))
        pe0, dve0 = self.pe.n, self.dve.n
        wl0 = self.wl[0].n

        @blk.vector
        def _(v):
            v = VV(self, v)
            v.memset(zrow[:, :], 0.0)
            v.tensor_scalar(mfm[0:NE, :], self.w_fm[0:NE, :], 0.0, None, ALU.is_gt)
            v.tensor_tensor_scan(sfm[0:NE, :], mfm[0:NE, :], mfm[0:NE, :], 0.0,
                                 ALU.add, ALU.bypass)
            v.tensor_tensor(sfm[0:NE, :], sfm[0:NE, :], mfm[0:NE, :], ALU.subtract)
        dve_v1 = self.dve.n

        @blk.tensor
        def _(t):
            t.wait_ge(self.dve.sem, dve_v1)
            for ch in range(NCHK):
                if ch >= 2:
                    t.wait_ge(self.dve.sem, dve_v1 + ch - 1)
                ins = t.matmul(self.ps[4 + ch % 2][0:NE, 0:512], self.c_tri8[0:NE, 0:NE],
                               mfm[0:NE, ch * 512:(ch + 1) * 512], start=True, stop=True)
                self.pe.inc(ins)

        @blk.vector
        def _(v):
            v = VV(self, v)
            for ch in range(NCHK):
                v.wait_ge(self.pe.sem, pe0 + ch + 1)
                v.tensor_copy(cum[0:NE, ch * 512:(ch + 1) * 512],
                              self.ps[4 + ch % 2][0:NE, 0:512])
        pe_cum = self.pe.n

        @blk.vector
        def _(v):
            v = VV(self, v)
            v.tensor_scalar(m1[0:NE, :], sfm[0:NE, :], float(CAP), None, ALU.is_lt)
            v.tensor_tensor(m1[0:NE, :], m1[0:NE, :], mfm[0:NE, :], ALU.mult)
            v.tensor_scalar(gfm[0:NE, :], sfm[0:NE, :], self.c_segb[0:NE, 0:1], None, ALU.add)
            v.tensor_tensor(gfm[0:NE, :], gfm[0:NE, :], m1[0:NE, :], ALU.mult)
            v.tensor_tensor(m2[0:NE, :], self.c_trash[0:NE, :], m1[0:NE, :], ALU.mult)
            v.tensor_tensor(m2[0:NE, :], self.c_trash[0:NE, :], m2[0:NE, :], ALU.subtract)
            v.tensor_tensor(gfm[0:NE, :], gfm[0:NE, :], m2[0:NE, :], ALU.add)
            v.tensor_scalar(amk[0:NE, :], cum[0:NE, :], 0.0, None, ALU.is_equal)
            v.tensor_tensor(amk[0:NE, :], amk[0:NE, :], mfm[0:NE, :], ALU.mult)
            v.tensor_scalar(bmk[0:NE, :], cum[0:NE, :], 1.0, None, ALU.is_equal)
            v.tensor_tensor(bmk[0:NE, :], bmk[0:NE, :], mfm[0:NE, :], ALU.mult)
            v.tensor_tensor(m1[0:NE, :], amk[0:NE, :], gfm[0:NE, :], ALU.mult)
            v.tensor_tensor(m2[0:NE, :], bmk[0:NE, :], gfm[0:NE, :], ALU.mult)
            v.tensor_tensor(amk[0:NE, :], amk[0:NE, :], self.w_fm[0:NE, :], ALU.mult)
            v.tensor_tensor(bmk[0:NE, :], bmk[0:NE, :], self.w_fm[0:NE, :], ALU.mult)
        dve_v2 = self.dve.n

        @blk.tensor
        def _(t):
            t.wait_ge(self.dve.sem, dve_v2)
            for pp, (sA, sB) in enumerate([(m1, m2), (amk, bmk)]):
                for ch in range(NCHK):
                    it = pp * NCHK + ch
                    if it >= 2:
                        t.wait_ge(self.dve.sem, dve_v2 + 2 * (it - 1))
                    ins = t.matmul(self.ps[4 + ch % 2][0:1, 0:512], self.c_ones8[0:NE, 0:1],
                                   sA[0:NE, ch * 512:(ch + 1) * 512], start=True, stop=True)
                    self.pe.inc(ins)
                    ins = t.matmul(self.banks[6][0:1, 0:512], self.c_ones8[0:NE, 0:1],
                                   sB[0:NE, ch * 512:(ch + 1) * 512], start=True, stop=True)
                    self.pe.inc(ins)

        @blk.vector
        def _(v):
            v = VV(self, v)
            for pp, (dA, dB) in enumerate([(rowA, rowB), (wrA, wrB)]):
                for ch in range(NCHK):
                    it = pp * NCHK + ch
                    v.wait_ge(self.pe.sem, pe_cum + 2 * it + 2)
                    v.tensor_copy(dA[0:1, ch * 512:(ch + 1) * 512], self.ps[4 + ch % 2][0:1, 0:512])
                    v.tensor_copy(dB[0:1, ch * 512:(ch + 1) * 512], self.banks[6][0:1, 0:512])
        pmA, pmB, pmt = amk, bmk, m1
        it1 = m2[0:NE, :].bitcast(I32)
        it2 = gfm[0:NE, :].bitcast(I32)

        @blk.vector
        def _(v):
            v = VV(self, v)
            for src, dst in ((rowA, pmA), (rowB, pmB)):
                v.tensor_copy(it1[0:1, :], src[0:1, :])
                v.tensor_scalar(it2[0:1, :], it1[0:1, :], 127, None, ALU.bitwise_and)
                v.tensor_scalar(it1[0:1, :], it1[0:1, :], 7, None, ALU.arith_shift_right)
                v.tensor_copy(dst[0:1, :], it2[0:1, :])
                v.tensor_scalar(dst[0:1, :], dst[0:1, :], float(PMW), None, ALU.mult)
                v.tensor_copy(pmt[0:1, :], it1[0:1, :])
                v.tensor_tensor(dst[0:1, :], dst[0:1, :], pmt[0:1, :], ALU.add)
        pe_sab = self.pe.n
        dve_sab = self.dve.n

        @blk.tensor
        def _(t):
            t.wait_ge(self.dve.sem, dve_sab)
            for tc in range(TC):
                for k, src in enumerate([rowA, rowB, wrA, wrB, pmA, pmB]):
                    i = tc * 6 + k
                    wge(t, self.dve.sem, dve_sab + i - 1)
                    ins = t.transpose(self.ps[4 + i % 2][0:128, 0:1],
                                      src[0:1, tc * 128:(tc + 1) * 128],
                                      self.c_idf[0:1, 0:1])
                    self.pe.inc(ins)

        @blk.vector
        def _(v):
            v = VV(self, v)
            for tc in range(TC):
                for k, dst in enumerate([saf, sbf, waf, wbf, paf, pbf]):
                    i = tc * 6 + k
                    v.wait_ge(self.pe.sem, pe_sab + i + 1)
                    v.tensor_copy(dst[:, tc:tc + 1], self.ps[4 + i % 2][0:128, 0:1])
            v.tensor_copy(self.slA_i[:, 0:TC], saf[:, 0:TC])
            v.tensor_copy(self.slB_i[:, 0:TC], sbf[:, 0:TC])
            v.tensor_copy(paf_i[:, 0:TC], paf[:, 0:TC])
            v.tensor_copy(pbf_i[:, 0:TC], pbf[:, 0:TC])
        dve_all = self.dve.n

        @blk.gpsimd
        def _(g):
            g.wait_ge(self.dve.sem, dve_all)
            self.sc.inc(g.dma_start(
                out=self.slotpm.rearrange("(p c) x -> p (c x)", p=128),
                in_=zrow[:, 0:PMW]))
            self.sc.inc(g.dma_start(
                out=self.wgflat.rearrange("(p c) x -> p (c x)", p=128),
                in_=zrow[:, 0:PMW]))
            for q in range(D // 512):
                self.sc.inc(g.dma_start(
                    out=self.contrib[self.NSL:self.NSL + 128, q * 512:(q + 1) * 512],
                    in_=zrow[:, 0:512]))
            g.wait_ge(self.sc.sem, self.sc.n)
            for tc in range(TC):
                for out, idx, val in ((self.slotpm, paf_i, self.c_tokid),
                                      (self.slotpm, pbf_i, self.c_tokid),
                                      (self.wgflat, self.slA_i, waf),
                                      (self.wgflat, self.slB_i, wbf)):
                    self.sc.inc(g.indirect_dma_start(
                        out=out,
                        out_offset=IndirectOffsetOnAxis(idx[:, tc:tc + 1], 0),
                        in_=val[:, tc:tc + 1], in_offset=None))
        self.sc_scatter_end = self.sc.n

    def p6b_sparse(self, e):
        nc = self.nc
        blk = self.block
        T, CAP, NSL, PMW = self.T, self.CAP, self.NSL, self.PMW
        JT = CAP // 128
        SCH = self.SCH
        spm_f = e(nc.sbuf_tensor("b_spm_f", [128, PMW], F32))
        spm_i = e(nc.sbuf_tensor("b_spm_i", [128, PMW], I32))
        xstg = e(nc.sbuf_tensor("b_xstg", [128, 2 * D], BF16))
        xg = [e(nc.sbuf_tensor(f"b_xg{i}", [128, DC * CAP], BF16)) for i in range(2)]
        hg = [e(nc.sbuf_tensor(f"b_hg{i}", [128, MIC * CAP], BF16)) for i in range(2)]
        w2a = e(nc.sbuf_tensor("b_w2a", [128, MIC * D], BF16))
        wgb = e(nc.sbuf_tensor("b_wgb", [128, CAP], F32))
        wgr = e(nc.sbuf_tensor("b_wgr", [1, CAP], F32))
        wgc = e(nc.sbuf_tensor("b_wgc", [128, 2 * D], BF16))
        wuc = e(nc.sbuf_tensor("b_wuc", [128, 2 * D], BF16))
        sgb = e(nc.sbuf_tensor("b_sgb", [128, 2 * 512], BF16))
        tml = e(nc.sbuf_tensor("b_tml", [128, 512], F32))
        ev = e(nc.sbuf_tensor("b_ev", [128, 2 * 512], F32))

        @blk.sync
        def _(sp):
            sp.wait_ge(self.sc.sem, self.sc_scatter_end)
            self.scl.inc(sp.dma_start(
                out=spm_f[:, :],
                in_=self.slotpm.rearrange("(p c) x -> p (c x)", p=128)))
        ld_spm = self.scl.n

        @blk.vector
        def _(v):
            v = VV(self, v)
            v.wait_ge(self.scl.sem, ld_spm)
            v.tensor_copy(spm_i[:, :], spm_f[:, :])
        dve_spm = self.dve.n

        init = dict(pe_tr_end=self.pe.n, dve_up_end=self.dve.n, pe_up_end=self.pe.n,
                    pe_dn_end=self.pe.n, dve_dn_end=self.dve.n)
        hist = [init, init]
        for ex in range(NE):
            r = self._p6b_expert(ex, spm_i, spm_f, xstg, xg[ex % 2], hg[ex % 2], w2a,
                                 wgb, wgr, wgc, wuc, sgb, tml, ev, dve_spm,
                                 hist[-1], hist[-2])
            hist.append(r)

        if "dbg_contrib" in self.dbg:
            @blk.sync
            def _(sp):
                for c in self.st4:
                    c.wait(sp)
                self.st4[0].inc(sp.dma_start(out=self.dbg["dbg_contrib"], in_=self.contrib))

    def _p6b_expert(self, ex, spm_i, spm_f, xstg, xg, hg, w2a, wgb, wgr, wgc, wuc,
                    sgb, tml, ev, dve_spm, prev, prev2):
        blk = self.block
        CAP, SCH = self.CAP, self.SCH
        JT = CAP // 128
        NS = len(SCH)
        NOG = D // 512
        pe0, act0, dve0 = self.pe.n, self.act.n, self.dve.n
        gl_b = [self.gl[0].n, self.gl[1].n]
        wl_b = [self.wl[0].n, self.wl[1].n]
        st_b2 = [self.st4[2].n, self.st4[3].n]

        @blk.gpsimd
        def _(g):
            g.wait_ge(self.dve.sem, dve_spm)
            if ex == 0:
                wge(g, self.st4[2].sem, self.xn2tm_st[0])
                wge(g, self.st4[3].sem, self.xn2tm_st[1])
            for j in range(JT):
                par = j % 2
                if j >= 2:
                    wge(g, self.pe.sem, pe0 + 16 * (j - 1))
                elif ex >= 1:
                    wge(g, self.pe.sem, prev["pe_tr_end"])
                self.gl[par].inc(g.indirect_dma_start(
                    out=xstg[:, par * D:(par + 1) * D], out_offset=None,
                    in_=self.xn2tm,
                    in_offset=IndirectOffsetOnAxis(spm_i[:, ex * JT + j: ex * JT + j + 1], 0)))

        @blk.tensor
        def _(t):
            for j in range(JT):
                par = j % 2
                t.wait_ge(self.gl[par].sem, gl_b[par] + 16 * (j // 2 + 1))
                for dc in range(DC):
                    i = j * DC + dc
                    wge(t, self.dve.sem, dve0 + i - 1)
                    tb = self.psT if i % 2 == 0 else self.psT2
                    ins = t.transpose(tb[:, 0:128],
                                      xstg[:, par * D + dc * 128: par * D + (dc + 1) * 128],
                                      self.c_idb[:, :])
                    self.pe.inc(ins)
        pe_tr_end = self.pe.n

        @blk.vector
        def _(v):
            v = VV(self, v)
            if ex >= 2:
                v.wait_ge(self.pe.sem, prev2["pe_up_end"])
            for j in range(JT):
                for dc in range(DC):
                    i = j * DC + dc
                    v.wait_ge(self.pe.sem, pe0 + i + 1)
                    srcb = self.psT if i % 2 == 0 else self.psT2
                    v.tensor_copy(xg[:, dc * CAP + j * 128: dc * CAP + (j + 1) * 128],
                                  srcb[:, 0:128])
        dve_tr = self.dve.n

        @blk.sync
        def _(sp):
            if ex == 0:
                wge(sp, self.dve.sem, dve_spm)
            else:
                wge(sp, self.pe.sem, pe0)
            self.scl.inc(sp.dma_start(out=wgr[0:1, 0:CAP],
                                      in_=self.wgflat[ex * CAP:(ex + 1) * CAP, :]
                                      .rearrange("a x -> x (a)")))
        ldS_w = self.scl.n

        @blk.tensor
        def _(t):
            t.wait_ge(self.scl.sem, ldS_w)
            for si, (sc, sn) in enumerate(SCH):
                wge(t, self.dve.sem, dve_tr + si)
                ins = t.matmul(self.ps[4][0:128, 0:sn], self.c_orf[0:1, :],
                               wgr[0:1, sc:sc + sn], start=True, stop=True)
                self.pe.inc(ins)

        @blk.vector
        def _(v):
            v = VV(self, v)
            for si, (sc, sn) in enumerate(SCH):
                v.wait_ge(self.pe.sem, pe_tr_end + si + 1)
                v.tensor_copy(wgb[:, sc:sc + sn], self.ps[4][0:128, 0:sn])
        pe_wgb = self.pe.n
        dve_wgb = self.dve.n

        @blk.gpsimd
        def _(g):
            for icp in range(MIC):
                par = icp % 2
                if icp >= 2:
                    wge(g, self.pe.sem, pe_wgb + 2 * NS * (icp - 1))
                elif ex >= 1:
                    wge(g, self.pe.sem, prev["pe_up_end"])
                b0 = ex * 2 * MIC + icp
                b1 = ex * 2 * MIC + MIC + icp
                self.block_load(g, wgc[:, par * D:(par + 1) * D], self.inp["w13T"], b0, self.wl[par])
                self.block_load(g, wuc[:, par * D:(par + 1) * D], self.inp["w13T"], b1, self.wl[par])

        @blk.tensor
        def _(t):
            t.wait_ge(self.dve.sem, dve_wgb)
            for icp in range(MIC):
                par = icp % 2
                t.wait_ge(self.wl[par].sem, wl_b[par] + 32 * (icp // 2 + 1))
                for si, (sc, sn) in enumerate(SCH):
                    it = icp * NS + si
                    par2 = it % 2
                    gbank = self.ps[0 + 2 * par2]
                    ubank = self.ps[1 + 2 * par2]
                    if it >= 2:
                        wge(t, self.act.sem, act0 + (it - 1))
                        wge(t, self.dve.sem, dve_wgb + 2 * (it - 1))
                    ins = None
                    for dc in range(DC):
                        ins = t.matmul(gbank[:, 0:sn],
                                       wgc[:, par * D + dc * 128: par * D + (dc + 1) * 128],
                                       xg[:, dc * CAP + sc: dc * CAP + sc + sn],
                                       start=(dc == 0), stop=(dc == DC - 1))
                    self.pe.inc(ins)
                    ins = None
                    for dc in range(DC):
                        ins = t.matmul(ubank[:, 0:sn],
                                       wuc[:, par * D + dc * 128: par * D + (dc + 1) * 128],
                                       xg[:, dc * CAP + sc: dc * CAP + sc + sn],
                                       start=(dc == 0), stop=(dc == DC - 1))
                    self.pe.inc(ins)
        pe_up_end = self.pe.n

        @blk.scalar
        def _(s):
            if ex >= 1:
                s.wait_ge(self.dve.sem, prev["dve_up_end"])
            for icp in range(MIC):
                for si, (sc, sn) in enumerate(SCH):
                    it = icp * NS + si
                    par2 = it % 2
                    s.wait_ge(self.pe.sem, pe_wgb + 2 * it + 1)
                    if it >= 2:
                        wge(s, self.dve.sem, dve_wgb + 2 * (it - 1))
                    self.act.inc(s.activation(sgb[:, par2 * 512:par2 * 512 + sn],
                                              self.ps[0 + 2 * par2][:, 0:sn], AF.Silu))

        @blk.vector
        def _(v):
            v = VV(self, v)
            if ex >= 2:
                v.wait_ge(self.pe.sem, prev2["pe_dn_end"])
            for icp in range(MIC):
                for si, (sc, sn) in enumerate(SCH):
                    it = icp * NS + si
                    par2 = it % 2
                    v.wait_ge(self.pe.sem, pe_wgb + 2 * it + 2)
                    v.wait_ge(self.act.sem, act0 + it + 1)
                    v.tensor_tensor(tml[:, 0:sn], sgb[:, par2 * 512:par2 * 512 + sn],
                                    self.ps[1 + 2 * par2][:, 0:sn], ALU.mult)
                    v.tensor_tensor(hg[:, icp * CAP + sc: icp * CAP + sc + sn],
                                    tml[:, 0:sn], wgb[:, sc:sc + sn], ALU.mult)
        dve_up_end = self.dve.n

        @blk.gpsimd
        def _(g):
            if ex >= 1:
                wge(g, self.pe.sem, prev["pe_dn_end"])
            for icp in range(MIC):
                self.block_load(g, w2a[:, icp * D:(icp + 1) * D],
                                self.inp["w2T"], ex * MIC + icp, self.wlS)
        wlS_dn = self.wlS.n

        @blk.tensor
        def _(t):
            t.wait_ge(self.wlS.sem, wlS_dn)
            t.wait_ge(self.dve.sem, dve_up_end)
            for j in range(JT):
                for og in range(NOG):
                    i = j * NOG + og
                    bank = self.ps[5] if i % 2 == 0 else self.banks[7]
                    wge(t, self.dve.sem, dve_up_end + i - 1)
                    ins = None
                    for icp in range(MIC):
                        ins = t.matmul(bank[:, 0:512],
                                       hg[:, icp * CAP + j * 128: icp * CAP + (j + 1) * 128],
                                       w2a[:, icp * D + og * 512: icp * D + (og + 1) * 512],
                                       start=(icp == 0), stop=(icp == MIC - 1))
                    self.pe.inc(ins)
        pe_dn_end = self.pe.n

        @blk.vector
        def _(v):
            v = VV(self, v)
            for j in range(JT):
                for og in range(NOG):
                    i = j * NOG + og
                    par2 = i % 2
                    v.wait_ge(self.pe.sem, pe_up_end + i + 1)
                    wge(v, self.st4[2 + par2].sem, st_b2[par2] + 16 * (i // 2))
                    bank = self.ps[5] if i % 2 == 0 else self.banks[7]
                    v.tensor_copy(ev[:, par2 * 512:par2 * 512 + 512], bank[:, 0:512])
        dve_dn_end = self.dve.n

        @blk.sync
        def _(sp):
            for j in range(JT):
                for og in range(NOG):
                    i = j * NOG + og
                    par2 = i % 2
                    sp.wait_ge(self.dve.sem, dve_up_end + i + 1)
                    self.st4[2 + par2].inc(sp.dma_start(
                        out=self.contrib[ex * CAP + j * 128: ex * CAP + (j + 1) * 128,
                                         og * 512:(og + 1) * 512],
                        in_=ev[:, par2 * 512:par2 * 512 + 512]))

        return dict(pe_tr_end=pe_tr_end, dve_up_end=dve_up_end, pe_up_end=pe_up_end,
                    pe_dn_end=pe_dn_end, dve_dn_end=dve_dn_end)

    def p8_combine(self, cA, cB, x3r, yt):
        blk = self.block
        TC = self.TC
        pe0, dve0 = self.pe.n, self.dve.n
        ld_b = [self.ld[0].n, self.ld[1].n]
        wl_b = [self.wl[0].n, self.wl[1].n]
        st_b = [self.st4[0].n, self.st4[1].n]
        stC = [self.st4[2].n, self.st4[3].n]

        @blk.gpsimd
        def _(g):
            wge(g, self.st4[2].sem, stC[0])
            wge(g, self.st4[3].sem, stC[1])
            for tc in range(TC):
                par = tc % 2
                if tc >= 2:
                    wge(g, self.dve.sem, dve0 + 2 * (tc - 1))
                self.wl[par].inc(g.indirect_dma_start(
                    out=cA[:, par * D:(par + 1) * D], out_offset=None,
                    in_=self.contrib,
                    in_offset=IndirectOffsetOnAxis(self.slA_i[:, tc:tc + 1], 0)))
                self.wl[par].inc(g.indirect_dma_start(
                    out=cB[:, par * D:(par + 1) * D], out_offset=None,
                    in_=self.contrib,
                    in_offset=IndirectOffsetOnAxis(self.slB_i[:, tc:tc + 1], 0)))

        @blk.sync
        def _(sp):
            for tc in range(TC):
                par = tc % 2
                if tc >= 2:
                    wge(sp, self.dve.sem, dve0 + 2 * (tc - 1))
                self.ld[par].inc(sp.dma_start(
                    out=x3r[:, par * D:(par + 1) * D],
                    in_=self.x3d[tc * 128:(tc + 1) * 128, :]))
                if tc >= 2:
                    j = tc - 2
                    sp.wait_ge(self.dve.sem, dve0 + 2 * (j + 1))
                    self.st4[j % 2].inc(sp.dma_start(
                        out=self.y[j * 128:(j + 1) * 128, :],
                        in_=yt[:, (j % 2) * D:(j % 2 + 1) * D]))
            for j in (TC - 2, TC - 1):
                sp.wait_ge(self.dve.sem, dve0 + 2 * (j + 1))
                self.st4[j % 2].inc(sp.dma_start(
                    out=self.y[j * 128:(j + 1) * 128, :],
                    in_=yt[:, (j % 2) * D:(j % 2 + 1) * D]))

        @blk.vector
        def _(v):
            v = VV(self, v)
            for tc in range(TC):
                par = tc % 2
                v.wait_ge(self.wl[par].sem, wl_b[par] + 32 * (tc // 2 + 1))
                v.wait_ge(self.ld[par].sem, ld_b[par] + 16 * (tc // 2 + 1))
                wge(v, self.st4[par].sem, st_b[par] + 16 * (tc // 2))
                v.tensor_tensor(yt[:, par * D:(par + 1) * D], x3r[:, par * D:(par + 1) * D],
                                cA[:, par * D:(par + 1) * D], ALU.add)
                v.tensor_tensor(yt[:, par * D:(par + 1) * D], yt[:, par * D:(par + 1) * D],
                                cB[:, par * D:(par + 1) * D], ALU.add)

    def p7_shared_down(self, hsb, wd, sgbc, x2t, x3t, tml, x3s):
        blk = self.block
        T, TG = self.T, self.TG
        pe0, dve0 = self.pe.n, self.dve.n

        @blk.tensor
        def _(t):
            for tg in range(TG):
                if tg > 0:
                    t.wait_ge(self.dve.sem, dve0 + tg)
                ins = t.matmul(self.ps[4], self.c_orf[0:1, :],
                               self.sg_fm[0:1, tg * 512:tg * 512 + 512], start=True, stop=True)
                self.pe.inc(ins)

        @blk.vector
        def _(v):
            v = VV(self, v)
            for tg in range(TG):
                v.wait_ge(self.pe.sem, pe0 + tg + 1)
                v.tensor_copy(sgbc[:, tg * 512:tg * 512 + 512], self.ps[4])

        for tg in range(TG):
            tg_pe0, tg_dve0 = self.pe.n, self.dve.n
            ld_b = [self.ld[0].n, self.ld[1].n]
            st_b = [self.st4[0].n, self.st4[1].n]
            wl_b = [self.wl[0].n, self.wl[1].n]

            @blk.sync
            def _(sp):
                if tg > 0:
                    sp.wait_ge(self.pe.sem, tg_pe0)
                self.ldS.inc(sp.dma_start(
                    out=hsb[:, 0:SIC * 512].rearrange("p (ic t) -> p ic t", t=512),
                    in_=self.hshd.rearrange("(tgx ic p) t -> tgx p ic t", p=128, ic=SIC)[tg]))
                for oc in range(DC):
                    par2 = oc % 2
                    if oc >= 2:
                        wge(sp, self.dve.sem, tg_dve0 + 6 * (oc - 2) + 2)
                    self.ld[par2].inc(sp.dma_start(
                        out=x2t[:, par2 * 512:par2 * 512 + 512],
                        in_=self.x2d[oc * 128:(oc + 1) * 128, tg * 512:tg * 512 + 512]))
                for qs in range(4):
                    sp.wait_ge(self.dve.sem, tg_dve0 + 6 * DC)
                    self.st4[qs % 2].inc(sp.dma_start(
                        out=self.x3d[tg * 512 + qs * 128: tg * 512 + (qs + 1) * 128, :],
                        in_=x3s[:, qs * D:(qs + 1) * D]))
            ld_h = self.ldS.n

            @blk.gpsimd
            def _(gp):
                wge(gp, self.pe.sem, tg_pe0)
                for oc in range(DC):
                    par = oc % 2
                    if oc >= 2:
                        wge(gp, self.pe.sem, tg_pe0 + 5 * (oc - 2) + 1)
                    self.block_load(gp, wd[:, par * SIC * 128:(par + 1) * SIC * 128],
                                    self.inp["wdownT"], oc, self.wl[par])

            @blk.tensor
            def _(t):
                t.wait_ge(self.ldS.sem, ld_h)
                for oc in range(DC):
                    par = oc % 2
                    t.wait_ge(self.wl[par].sem, wl_b[par] + 16 * (oc // 2 + 1))
                    bank = self.ps[oc % 2]
                    if oc >= 2:
                        wge(t, self.dve.sem, tg_dve0 + 6 * (oc - 2) + 1)
                    ins = None
                    for ic in range(SIC):
                        ins = t.matmul(bank,
                                       wd[:, par * SIC * 128 + ic * 128: par * SIC * 128 + (ic + 1) * 128],
                                       hsb[:, ic * 512:(ic + 1) * 512],
                                       start=(ic == 0), stop=(ic == SIC - 1))
                    self.pe.inc(ins)
                    for qs in range(4):
                        tbank = self.psT[:, 0:128] if qs % 2 == 0 else self.psT[:, 128:256]
                        wge(t, self.dve.sem, tg_dve0 + 6 * oc + 2 + max(0, qs - 1))
                        ins = t.transpose(tbank.bitcast(F32)[:, 0:128] if False else
                                          (self.ps[5][:, 0:128] if qs % 2 == 0 else self.banks[7][:, 0:128]),
                                          x3t[:, (oc % 2) * 512 + qs * 128: (oc % 2) * 512 + (qs + 1) * 128],
                                          self.c_idf[:, :])
                        self.pe.inc(ins)

            @blk.vector
            def _(v):
                v = VV(self, v)
                for oc in range(DC):
                    par2 = oc % 2
                    v.wait_ge(self.pe.sem, tg_pe0 + 5 * oc + 1)
                    v.wait_ge(self.ld[par2].sem, ld_b[par2] + 16 * (oc // 2 + 1))
                    if tg > 0 and oc == 0:
                        for q2 in range(2):
                            wge(v, self.st4[q2].sem, st_b[q2])
                    bank = self.ps[oc % 2]
                    v.tensor_tensor(tml[:, par2 * 512:par2 * 512 + 512], bank,
                                    sgbc[:, tg * 512:tg * 512 + 512], ALU.mult)
                    v.tensor_tensor(x3t[:, par2 * 512:par2 * 512 + 512],
                                    tml[:, par2 * 512:par2 * 512 + 512],
                                    x2t[:, par2 * 512:par2 * 512 + 512], ALU.add)
                    for qs in range(4):
                        v.wait_ge(self.pe.sem, tg_pe0 + 5 * oc + 2 + qs)
                        src = self.ps[5][:, 0:128] if qs % 2 == 0 else self.banks[7][:, 0:128]
                        v.tensor_copy(x3s[:, qs * D + oc * 128: qs * D + (oc + 1) * 128], src)

        if "dbg_x3" in self.dbg:
            @blk.sync
            def _(sp):
                for c in self.st4:
                    c.wait(sp)
                self.st4[0].inc(sp.dma_start(out=self.dbg["dbg_x3"], in_=self.x3d))

    def final_wait(self):
        blk = self.block

        @blk.sync
        def _(sp):
            for c in self.st4:
                c.wait(sp)
            for c in self.ld:
                c.wait(sp)
            self.ldS.wait(sp)
            self.scl.wait(sp)

        @blk.gpsimd
        def _(g):
            for c in self.wl:
                c.wait(g)
            self.wlS.wait(g)
            self.sc.wait(g)
            for c in self.gl:
                c.wait(g)



def build_mk(L=8192, debug=()):
    nc = bass.Bass("TRN2", target_bir_lowering=False)
    mk = MK(nc, L, debug=debug)
    mk.build()
    return nc, mk


def host_prep(inputs, L=8192):
    f = np.float32
    T_ = L // 4
    TC_ = T_ // 128
    CAP_ = ((3 * T_ // 8 + 127) // 128) * 128
    d = {}
    def blockize(WT):
        rows, cols = WT.shape
        rc, nb = rows // 128, cols // 128
        w4 = WT.reshape(rc, 128, nb, 128)
        return np.ascontiguousarray(w4.transpose(2, 1, 0, 3).reshape(nb * 128, rc * 128))

    for p in "qkvo":
        W = np.asarray(inputs[f"L0_attn_W{p}"], f)
        d[f"w{p}T"] = blockize(np.ascontiguousarray(W.T))
        Bm = np.asarray(inputs[f"L0_attn_B{p}"], f)
        BT = np.ascontiguousarray(Bm.T)
        d[f"b{p}T_a"] = np.ascontiguousarray(
            BT.reshape(DC, 128, R).transpose(1, 0, 2).reshape(128, DC * R))
        Am = np.asarray(inputs[f"L0_attn_A{p}"], f)
        d[f"a{p}T"] = np.ascontiguousarray(Am.T)
    for p, key in [("q", "L0_attn_bq"), ("k", "L0_attn_bk"), ("o", "L0_attn_bo")]:
        b = np.asarray(inputs[key], f)
        d[f"b{p}_a"] = np.ascontiguousarray(b.reshape(DC, 128).T)
    d["bv_row"] = np.ascontiguousarray(np.asarray(inputs["L0_attn_bv"], f).reshape(1, D))
    gw9 = np.concatenate([np.asarray(inputs["L0_ffn_gate_w"], f),
                          np.asarray(inputs["L0_ffn_shared_gate_w"], f)], axis=0)
    gwT = gw9.T
    d["gw_arr"] = np.ascontiguousarray(gwT.reshape(DC, 128, 9).transpose(1, 0, 2).reshape(128, DC * 9))
    d["wguT"] = blockize(np.asarray(inputs["L0_ffn_shared_gate_up_w"], f).T)
    d["wdownT"] = blockize(np.asarray(inputs["L0_ffn_shared_down_w"], f).T)
    w13 = np.asarray(inputs["L0_ffn_w13"], f)
    d["w13T"] = blockize(np.ascontiguousarray(w13.transpose(2, 0, 1).reshape(D, NE * 2 * MI)))
    w2 = np.asarray(inputs["L0_ffn_w2"], f)
    d["w2T"] = np.ascontiguousarray(w2.transpose(0, 2, 1).reshape(NE * MI, D))
    d["ident_b"] = np.eye(128, dtype=BF)
    d["ident_f"] = np.eye(128, dtype=f)
    d["ones_col_b"] = np.ones((128, 1), dtype=BF)
    d["ones_row_b"] = np.ones((1, 128), dtype=BF)
    d["ones_row_f"] = np.ones((1, 128), dtype=f)
    tie = np.zeros((8, 8), f)
    for e_ in range(8):
        for j in range(8):
            tie[e_, j] = 1.0 if j < e_ else 0.0
    d["tie64"] = np.ascontiguousarray(np.tile(tie.reshape(1, 64), (128, 1)))
    sel = np.zeros((NE, NE * 128), f)
    for e_ in range(NE):
        sel[e_, e_ * 128:(e_ + 1) * 128] = 1.0
    d["sel8"] = sel
    tri = np.zeros((NE, NE), f)
    for j in range(NE):
        for e_ in range(NE):
            tri[j, e_] = 1.0 if j < e_ else 0.0
    d["tri8"] = tri
    d["ones8"] = np.ones((NE, 1), f)
    d["segb"] = (np.arange(NE, dtype=f) * CAP_).reshape(NE, 1)
    d["tokid"] = (np.arange(TC_, dtype=f)[None, :] * 128
                  + np.arange(128, dtype=f)[:, None]).astype(f)
    NSL_ = NE * CAP_
    tr2 = np.empty((NE, T_), f)
    for e_ in range(NE):
        tr2[e_, :] = NSL_ + (np.arange(T_) // 128) * NE + e_
    d["trashf"] = tr2
    return d


def make_xt(x, core, L=8192):
    T = L // 4
    b, q = core // 4, core % 4
    xb_ = np.asarray(x[b], np.float32)
    order = [q] + [i for i in range(4) if i != q]
    parts = [xb_[p * T:(p + 1) * T] for p in order]
    xr = np.concatenate(parts, axis=0)
    return np.ascontiguousarray(xr.T)


def kernel(**inputs):
    L = 8192
    T = L // 4
    nc, mk = build_mk(L=L)
    shared = host_prep(inputs, L=L)
    x = np.asarray(inputs["L0_attn_x"], np.float32)
    in_maps = []
    for c in range(8):
        m = dict(shared)
        m["xt_in"] = make_xt(x, c, L=L)
        in_maps.append(m)
    res = run_bass_kernel_spmd(nc, in_maps, core_ids=list(range(8)))
    out = np.empty((2, L, D), np.float32)
    for c in range(8):
        b, q = c // 4, c % 4
        out[b, q * T:(q + 1) * T, :] = res.results[c]["y"]
    return out


# version 15 (final)
# speedup vs baseline: 1.0942x; 1.0039x over previous
import numpy as np
import ml_dtypes
import concourse.bass as bass
import concourse.mybir as mybir
from concourse.bass_utils import run_bass_kernel_spmd
from concourse.bass_core_rust import IndirectOffsetOnAxis

F32 = mybir.dt.float32
BF16 = mybir.dt.bfloat16
I32 = mybir.dt.int32
AF = mybir.ActivationFunctionType
ALU = mybir.AluOpType
AX = mybir.AxisListType

D = 2048
DC = 16
H = 16
HD = 128
R = 16
NE = 8
MI = 1408
MIC = MI // 128
SI = 5632
SIC = SI // 128
SCALE = HD ** -0.5
EPS = 1e-5
BF = ml_dtypes.bfloat16


class Ctr:
    def __init__(self, sem, step=1):
        self.sem = sem
        self.step = step
        self.n = 0

    def inc(self, ins):
        ins.then_inc(self.sem, self.step)
        self.n += self.step
        return self.n

    def wait(self, eng, val=None):
        if val is None:
            val = self.n
        if val > 0:
            eng.wait_ge(self.sem, val)


def wge(eng, sem, val):
    if val > 0:
        eng.wait_ge(sem, val)


class VV:
    _PASS = ("wait_ge", "sem_inc", "drain", "nop", "engine_nop")

    def __init__(self, mk, v):
        self._mk = mk
        self._v = v

    def __getattr__(self, name):
        fn = getattr(self._v, name)
        if name in VV._PASS or name.startswith("_"):
            return fn

        def run(*a, **k):
            mk = self._mk
            if mk.dve.n:
                self._v.wait_ge(mk.dve.sem, mk.dve.n)
            ins = fn(*a, **k)
            mk.dve.inc(ins)
            return ins
        return run


class MK:
    def __init__(self, nc: bass.Bass, L: int, debug=()):
        assert L % 512 == 0
        self.nc = nc
        self.L = L
        self.T = L // 4
        self.TG = self.T // 512
        self.TC = self.T // 128
        self.KT = L // 128
        self.debug = set(debug)
        assert self.TG <= 4
        self.CAP = ((3 * self.T // 8 + 127) // 128) * 128
        self.NSL = NE * self.CAP
        self.TRASH = self.NSL
        self.PMW = self.NSL // 128 + 1
        self.SCH = []
        off = 0
        while off < self.CAP:
            n = min(512, self.CAP - off)
            self.SCH.append((off, n))
            off += n

        d = {}
        din = lambda name, shape, dt: nc.dram_tensor(name, shape, dt, kind="ExternalInput").ap()
        self.xt_in = din("xt_in", [D, L], F32)
        for p in "qkvo":
            d[f"w{p}T"] = din(f"w{p}T", [DC * 128, DC * 128], F32)
            d[f"b{p}T_a"] = din(f"b{p}T_a", [128, DC * R], F32)
            d[f"a{p}T"] = din(f"a{p}T", [R, D], F32)
        d["bq_a"] = din("bq_a", [128, DC], F32)
        d["bk_a"] = din("bk_a", [128, DC], F32)
        d["bo_a"] = din("bo_a", [128, DC], F32)
        d["bv_row"] = din("bv_row", [1, D], F32)
        d["gw_arr"] = din("gw_arr", [128, DC * 9], F32)
        d["wguT"] = din("wguT", [(2 * SI // 128) * 128, D], F32)
        d["wdownT"] = din("wdownT", [DC * 128, SIC * 128], F32)
        d["w13T"] = din("w13T", [(NE * 2 * MI // 128) * 128, D], F32)
        d["w2T"] = din("w2T", [NE * MI, D], F32)
        d["ident_b"] = din("ident_b", [128, 128], BF16)
        d["ident_f"] = din("ident_f", [128, 128], F32)
        d["ones_col_b"] = din("ones_col_b", [128, 1], BF16)
        d["ones_row_b"] = din("ones_row_b", [1, 128], BF16)
        d["ones_row_f"] = din("ones_row_f", [1, 128], F32)
        d["tie64"] = din("tie64", [128, 64], F32)
        d["sel8"] = din("sel8", [NE, NE * 128], F32)
        d["tri8"] = din("tri8", [NE, NE], F32)
        d["ones8"] = din("ones8", [NE, 1], F32)
        d["segb"] = din("segb", [NE, 1], F32)
        d["tokid"] = din("tokid", [128, self.TC], F32)
        d["trashf"] = din("trashf", [NE, self.T], F32)
        self.inp = d

        self.y = nc.dram_tensor("y", [self.T, D], F32, kind="ExternalOutput").ap()
        self.dbg = {}
        for name, shape, dt in [
            ("dbg_q", [D, self.T], BF16), ("dbg_k", [4 * D, self.T], BF16), ("dbg_v", [4 * D, self.T], BF16),
            ("dbg_o", [128, DC * self.T], BF16), ("dbg_x2", [D, self.T], F32),
            ("dbg_xn2", [128, DC * self.T], BF16),
            ("dbg_w", [NE, self.T], F32), ("dbg_sg", [1, self.T], F32),
            ("dbg_hsh", [self.TG * SI, 512], BF16),
            ("dbg_x3", [self.T, D], F32),
            ("dbg_slotmap", [128, self.PMW], F32), ("dbg_wg", [self.NSL + 128, 1], F32),
            ("dbg_contrib", [self.NSL + 128, D], F32),
        ]:
            if name in self.debug:
                self.dbg[name] = nc.dram_tensor(name, shape, dt, kind="ExternalOutput").ap()

        dsc = lambda name, shape, dt: nc.dram_tensor(name, shape, dt, kind="Internal").ap()
        self.kag_in = dsc("kag_in", [D, self.T], BF16)
        self.vag_in = dsc("vag_in", [D, self.T], BF16)
        self.kd = dsc("kd", [4 * D, self.T], BF16)
        self.vd = dsc("vd", [4 * D, self.T], BF16)
        self.qd = dsc("qd", [D, self.T], BF16)
        self.x2d = dsc("x2d", [D, self.T], F32)
        self.x3d = dsc("x3d", [self.T, D], F32)
        self.hshd = dsc("hshd", [self.TG * SI, 512], BF16)
        self.xn2tm = dsc("xn2tm", [self.T, D], BF16)
        self.slotpm = dsc("slotpm", [128 * self.PMW, 1], F32)
        self.wgflat = dsc("wgflat", [self.NSL + 128, 1], F32)
        self.contrib = dsc("contrib", [self.NSL + 128, D], F32)

    def build(self):
        nc = self.nc
        from contextlib import ExitStack
        with ExitStack() as ctx:
            e = ctx.enter_context
            self.psh = [e(nc.psum_tensor(f"ps{i}", [128, 512], F32)) for i in range(6)]
            self.psTh = e(nc.psum_tensor("psT", [128, 1024], BF16))
            self.ps7h = e(nc.psum_tensor("ps7", [128, 512], F32))
            self.ps = [p[:, :] for p in self.psh]
            self.psT = self.psTh[:, :]
            self.banks = self.ps + [self.psT.bitcast(F32), self.ps7h[:, :]]
            self.psT2 = self.ps7h[:, :].bitcast(BF16)

            self.ld = [Ctr(e(nc.semaphore(f"s_ld{i}")), 16) for i in range(2)]
            self.ldS = Ctr(e(nc.semaphore("s_ldS")), 16)
            self.st4 = [Ctr(e(nc.semaphore(f"s_st{i}")), 16) for i in range(4)]
            self.wl = [Ctr(e(nc.semaphore(f"s_wl{i}")), 16) for i in range(2)]
            self.wlS = Ctr(e(nc.semaphore("s_wlS")), 16)
            self.sc = Ctr(e(nc.semaphore("s_sc")), 16)
            self.scl = Ctr(e(nc.semaphore("s_scl")), 16)
            self.gl = [Ctr(e(nc.semaphore(f"s_gl{i}")), 16) for i in range(2)]
            self.cc = Ctr(e(nc.semaphore("s_cc")), 1)
            self.pe = Ctr(e(nc.semaphore("s_pe")))
            self.act = Ctr(e(nc.semaphore("s_act")))
            self.dve = Ctr(e(nc.semaphore("s_dve")))
            self.bar = e(nc.semaphore("s_bar"))
            self.nbar = 0

            sb = lambda name, shape, dt: e(nc.sbuf_tensor(name, shape, dt))
            self.c_bq = sb("c_bq", [128, DC], F32)
            self.c_bk = sb("c_bk", [128, DC], F32)
            self.c_bo = sb("c_bo", [128, DC], F32)
            self.c_bv = sb("c_bv", [1, D], BF16)
            self.c_BT = {p: sb(f"c_BT{p}", [128, DC * R], BF16) for p in "qkvo"}
            self.c_gw = sb("c_gw", [128, DC * 9], BF16)
            self.c_idb = sb("c_idb", [128, 128], BF16)
            self.c_idf = sb("c_idf", [128, 128], F32)
            self.c_ocb = sb("c_ocb", [128, 1], BF16)
            self.c_orb = sb("c_orb", [1, 128], BF16)
            self.c_orf = sb("c_orf", [1, 128], F32)
            self.c_tie = sb("c_tie", [128, 64], F32)
            self.c_sel = sb("c_sel", [NE, NE * 128], F32)
            self.sg_fm = sb("sg_fm", [1, self.T], F32)
            self.c_eps = sb("c_eps", [1, 1], F32)
            self.c_tri8 = sb("c_tri8", [NE, NE], F32)
            self.c_ones8 = sb("c_ones8", [NE, 1], F32)
            self.c_segb = sb("c_segb", [NE, 1], F32)
            self.c_tokid = sb("c_tokid", [128, self.TC], F32)
            self.slA_i = sb("slA_i", [128, self.TC], I32)
            self.slB_i = sb("slB_i", [128, self.TC], I32)
            self.c_trash = sb("c_trash", [NE, self.T], F32)

            self.block = e(nc.Block())
            self._emit()
        return nc

    _stats_n = 0

    def _alloc_stats(self, e):
        nc = self.nc
        MK._stats_n += 1
        sfx = f"_{MK._stats_n}"
        self.s1 = e(nc.sbuf_tensor("s1" + sfx, [1, self.T], F32))
        self.s2 = e(nc.sbuf_tensor("s2" + sfx, [1, self.T], F32))
        self.rstd = e(nc.sbuf_tensor("rstd" + sfx, [1, self.T], F32))
        self.Msb = e(nc.sbuf_tensor("Msb" + sfx, [128, self.T], BF16))
        self.Rsb = e(nc.sbuf_tensor("Rsb" + sfx, [128, self.T], BF16))

    def barrier(self):
        self.nbar += 1
        k = 5 * self.nbar
        blk = self.block

        @blk.tensor
        def _(t):
            t.drain()
            t.sem_inc(self.bar, 1)
            t.wait_ge(self.bar, k)

        @blk.scalar
        def _(s):
            s.drain()
            s.sem_inc(self.bar, 1)
            s.wait_ge(self.bar, k)

        @blk.vector
        def _(v):
            v = VV(self, v)
            v.drain()
            v.sem_inc(self.bar, 1)
            v.wait_ge(self.bar, k)

        @blk.sync
        def _(sp):
            for c in self.ld:
                c.wait(sp)
            self.ldS.wait(sp)
            for c in self.st4:
                c.wait(sp)
            sp.sem_inc(self.bar, 1)
            sp.wait_ge(self.bar, k)

        @blk.gpsimd
        def _(g):
            for c in self.wl:
                c.wait(g)
            self.wlS.wait(g)
            self.sc.wait(g)
            for c in self.gl:
                c.wait(g)
            g.sem_inc(self.bar, 1)
            g.wait_ge(self.bar, k)

    def block_load(self, g, dst_sb_ap, src, block, ctr):
        ctr.inc(g.dma_start(out=dst_sb_ap, in_=src[block * 128:(block + 1) * 128, :]))

    def _emit(self):
        from contextlib import ExitStack
        nc = self.nc
        self.p0_consts()
        self.barrier()
        with ExitStack() as s1:
            e = s1.enter_context
            xb = e(nc.sbuf_tensor("xb_g", [128, DC * self.T], BF16))
            xn = xb
            sq = e(nc.sbuf_tensor("sq_g", [128, 2 * self.T], BF16))
            t1b = e(nc.sbuf_tensor("t1b", [128, self.T], F32))
            wcol = e(nc.sbuf_tensor("wcol", [128, 4 * DC * 128], BF16))
            qcol = e(nc.sbuf_tensor("qcol", [128, 2 * DC * 128], BF16))
            ev = e(nc.sbuf_tensor("ev_p1", [128, 4 * 512], BF16))
            self._alloc_stats(e)
            self.c_AT = {p: e(nc.sbuf_tensor(f"c_AT{p}", [R, D], BF16)) for p in "qkv"}
            self.xBq = e(nc.sbuf_tensor("xBq", [R, self.T], BF16))
            self.xBg = {p: e(nc.sbuf_tensor(f"xBg{p}", [R, self.T], BF16)) for p in "kv"}
            blk = self.block

            @blk.gpsimd
            def _(g_):
                for p in "qkv":
                    self.wlS.inc(g_.dma_start(out=self.c_AT[p][:, :], in_=self.inp[f"a{p}T"]))
            self.p1_group(0, xb, xn, sq, t1b, wcol, qcol, ev)
            self.p1b_allgather()
            self.barrier()
        with ExitStack() as s2:
            e = s2.enter_context
            ofm = e(nc.sbuf_tensor("ofm", [128, DC * self.T], BF16))
            with ExitStack() as s2a:
                ee = s2a.enter_context
                kh = ee(nc.sbuf_tensor("kh", [128, 2 * self.L], BF16))
                vraw = ee(nc.sbuf_tensor("vraw", [128, self.L], BF16))
                vext = ee(nc.sbuf_tensor("vext", [128, 2 * (self.KT * 129)], BF16))
                qh = ee(nc.sbuf_tensor("qh", [128, 2 * self.T], BF16))
                pt = ee(nc.sbuf_tensor("pt", [128, 3 * 512], BF16))
                otmp = ee(nc.sbuf_tensor("otmp", [128, 128], BF16))
                rcp = ee(nc.sbuf_tensor("rcp", [128, 4], F32))
                self.p2_attention(ofm, kh, vraw, vext, qh, pt, otmp, rcp)
                self.barrier()
            with ExitStack() as s2b:
                ee = s2b.enter_context
                wcol = ee(nc.sbuf_tensor("wcol3", [128, 2 * DC * 128], BF16))
                xtile = ee(nc.sbuf_tensor("xtile3", [128, 2 * 512], F32))
                x2t = ee(nc.sbuf_tensor("x2t3", [128, 2 * 512], F32))
                self.c_ATo = ee(nc.sbuf_tensor("c_ATo", [R, D], BF16))
                self.xBo = ee(nc.sbuf_tensor("xBo", [R, self.T], BF16))
                self.p3_oproj(ofm, wcol, xtile, x2t)
                self.barrier()
        with ExitStack() as s3:
            e = s3.enter_context
            xn2 = e(nc.sbuf_tensor("xn2", [128, DC * self.T], BF16))
            self.w_fm = e(nc.sbuf_tensor("w_fm", [NE, self.T], F32))
            with ExitStack() as s3a:
                ee = s3a.enter_context
                sq = ee(nc.sbuf_tensor("sq2", [128, 2 * self.T], BF16))
                t1b = ee(nc.sbuf_tensor("t1b2", [128, self.T], F32))
                self._alloc_stats(ee)
                self.p4_ln2(xn2, sq, t1b)
                self.barrier()
            with ExitStack() as s3z:
                ee = s3z.enter_context
                xstg = ee(nc.sbuf_tensor("xstg", [128, 2 * D], BF16))
                self.p4b_xn2tm(xn2, xstg)
                self.barrier()
            with ExitStack() as s3b:
                ee = s3b.enter_context
                pr = ee(nc.sbuf_tensor("pr", [128, 8], F32))
                rsc = ee(nc.sbuf_tensor("rsc", [128, 24], F32))
                rs2 = ee(nc.sbuf_tensor("rs2", [128, 64], F32))
                rs3 = ee(nc.sbuf_tensor("rs3", [128, 64], F32))
                self.p5_router(xn2, pr, rsc, rs2, rs3)
                self.barrier()
            with ExitStack() as s3i:
                ee = s3i.enter_context
                self.p5b_index(ee)
                self.barrier()
            with ExitStack() as s3c:
                ee = s3c.enter_context
                wg = ee(nc.sbuf_tensor("wg6", [128, 2 * DC * 128], BF16))
                wu = ee(nc.sbuf_tensor("wu6", [128, 2 * DC * 128], BF16))
                sgb = ee(nc.sbuf_tensor("sgb6", [128, 2 * 512], BF16))
                tml = ee(nc.sbuf_tensor("tml6", [128, 512], F32))
                ht = ee(nc.sbuf_tensor("ht6", [128, 2 * 512], BF16))
                self.p6_up(xn2, wg, wu, sgb, tml, ht, None)
                self.barrier()
        with ExitStack() as s4b:
            e = s4b.enter_context
            self.p6b_sparse(e)
            self.barrier()
        with ExitStack() as s4:
            e = s4.enter_context
            hsb = e(nc.sbuf_tensor("hsb7", [128, SIC * 512], BF16))
            wd = e(nc.sbuf_tensor("wd7", [128, 2 * SIC * 128], BF16))
            sgbc = e(nc.sbuf_tensor("sgbc", [128, self.T], F32))
            x2t = e(nc.sbuf_tensor("x2t7", [128, 2 * 512], F32))
            x3t = e(nc.sbuf_tensor("x3t7", [128, 2 * 512], F32))
            tml = e(nc.sbuf_tensor("tml7", [128, 2 * 512], F32))
            x3s = e(nc.sbuf_tensor("x3s7", [128, 4 * D], F32))
            self.p7_shared_down(hsb, wd, sgbc, x2t, x3t, tml, x3s)
            self.barrier()
        with ExitStack() as s5:
            e = s5.enter_context
            cA = e(nc.sbuf_tensor("cA8", [128, 2 * D], F32))
            cB = e(nc.sbuf_tensor("cB8", [128, 2 * D], F32))
            x3r = e(nc.sbuf_tensor("x3r8", [128, 2 * D], F32))
            yt = e(nc.sbuf_tensor("yt8", [128, 2 * D], F32))
            self.p8_combine(cA, cB, x3r, yt)
        self.barrier()
        self.final_wait()

    def p0_consts(self):
        blk = self.block
        inp = self.inp

        @blk.vector
        def _(v):
            v = VV(self, v)
            v.memset(self.c_eps[0:1, 0:1], EPS)

        @blk.sync
        def _(sp):
            for dst, src in [
                (self.c_bq, inp["bq_a"]), (self.c_bk, inp["bk_a"]), (self.c_bo, inp["bo_a"]),
                (self.c_idb, inp["ident_b"]), (self.c_idf, inp["ident_f"]),
                (self.c_ocb, inp["ones_col_b"]), (self.c_orb, inp["ones_row_b"]),
                (self.c_orf, inp["ones_row_f"]), (self.c_tie, inp["tie64"]),
                (self.c_sel, inp["sel8"]), (self.c_tri8, inp["tri8"]),
                (self.c_ones8, inp["ones8"]), (self.c_segb, inp["segb"]),
                (self.c_tokid, inp["tokid"]), (self.c_trash, inp["trashf"]),
            ]:
                self.ldS.inc(sp.dma_start(out=dst[:, :], in_=src))

        @blk.gpsimd
        def _(g):
            self.wl[0].inc(g.dma_start(out=self.c_bv[:, :], in_=inp["bv_row"]))
            self.wl[0].inc(g.dma_start(out=self.c_gw[:, :], in_=inp["gw_arr"]))
            for p in "qkvo":
                self.wl[0].inc(g.dma_start(out=self.c_BT[p][:, :], in_=inp[f"b{p}T_a"]))

    def _ln_core(self, xb, sq, t1b, xn, src_sem, src_val):
        blk = self.block
        T, TG = self.T, self.TG
        pe0, act0, dve0 = self.pe.n, self.act.n, self.dve.n

        @blk.scalar
        def _(s):
            s.wait_ge(src_sem, src_val)
            for dc in range(DC):
                if dc >= 2:
                    wge(s, self.pe.sem, pe0 + (dc - 1))
                self.act.inc(s.activation(sq[:, (dc % 2) * T:(dc % 2) * T + T],
                                          xb[:, dc * T:(dc + 1) * T], AF.Square))

        @blk.tensor
        def _(t):
            for dc in range(DC):
                t.wait_ge(self.act.sem, act0 + dc + 1)
                first, last = dc == 0, dc == DC - 1
                ins = None
                for tg in range(TG):
                    t.matmul(self.banks[tg][0:1, 0:512], self.c_ocb[:, 0:1],
                             xb[:, dc * T + tg * 512: dc * T + tg * 512 + 512],
                             start=first, stop=last)
                    ins = t.matmul(self.banks[TG + tg][0:1, 0:512], self.c_ocb[:, 0:1],
                                   sq[:, (dc % 2) * T + tg * 512:(dc % 2) * T + tg * 512 + 512],
                                   start=first, stop=last)
                self.pe.inc(ins)

        @blk.vector
        def _(v):
            v = VV(self, v)
            v.wait_ge(self.pe.sem, pe0 + DC)
            for tg in range(TG):
                v.tensor_copy(self.s1[0:1, tg * 512:tg * 512 + 512], self.banks[tg][0:1, 0:512])
                v.tensor_copy(self.s2[0:1, tg * 512:tg * 512 + 512], self.banks[TG + tg][0:1, 0:512])
            v.tensor_scalar(self.s1[0:1, :], self.s1[0:1, :], 1.0 / D, None, ALU.mult)
            v.tensor_scalar(self.s2[0:1, :], self.s2[0:1, :], 1.0 / D, None, ALU.mult)
            v.tensor_tensor(self.rstd[0:1, :], self.s1[0:1, :], self.s1[0:1, :], ALU.mult)
            v.tensor_tensor(self.s2[0:1, :], self.s2[0:1, :], self.rstd[0:1, :],
                            ALU.subtract)

        @blk.scalar
        def _(s):
            s.wait_ge(self.dve.sem, dve0 + 2 * TG + 4)
            self.act.inc(s.activation(self.s2[0:1, :], self.s2[0:1, :], AF.Sqrt,
                                      bias=self.c_eps[0:1, 0:1]))

        @blk.vector
        def _(v):
            v = VV(self, v)
            v.wait_ge(self.act.sem, act0 + DC + 1)
            v.reciprocal(self.rstd[0:1, :], self.s2[0:1, :])

        @blk.tensor
        def _(t):
            t.wait_ge(self.dve.sem, dve0 + 2 * TG + 5)
            for tg in range(TG):
                if tg > 0:
                    t.wait_ge(self.dve.sem, dve0 + 2 * TG + 5 + 2 * tg)
                ins = t.matmul(self.ps[0], self.c_orf[0:1, :],
                               self.s1[0:1, tg * 512:tg * 512 + 512], start=True, stop=True)
                self.pe.inc(ins)
                ins = t.matmul(self.ps[1], self.c_orf[0:1, :],
                               self.rstd[0:1, tg * 512:tg * 512 + 512], start=True, stop=True)
                self.pe.inc(ins)

        @blk.vector
        def _(v):
            v = VV(self, v)
            for tg in range(TG):
                v.wait_ge(self.pe.sem, pe0 + DC + 2 * tg + 1)
                v.tensor_copy(self.Msb[:, tg * 512:tg * 512 + 512], self.ps[0])
                v.wait_ge(self.pe.sem, pe0 + DC + 2 * tg + 2)
                v.tensor_copy(self.Rsb[:, tg * 512:tg * 512 + 512], self.ps[1])
            for dc in range(DC):
                v.tensor_tensor(t1b[:, 0:T], xb[:, dc * T:(dc + 1) * T],
                                self.Msb[:, 0:T], ALU.subtract)
                v.tensor_tensor(xn[:, dc * T:(dc + 1) * T],
                                t1b[:, 0:T], self.Rsb[:, 0:T], ALU.mult)
        return self.dve.n

    def p1_group(self, g, xb, xn, sq, t1b, wcol, qcol, ev):
        blk = self.block
        T, TG, TC = self.T, self.TG, self.TC
        mine = True

        @blk.gpsimd
        def _(gp):
            self.wlS.inc(gp.dma_start(
                out=xb[:, 0:DC * T].rearrange("p (dc t) -> p dc t", t=T),
                in_=self.xt_in[:, 0:T].rearrange("(dc p) t -> p dc t", p=128)))
        wl_xb = self.wlS.n

        dve_norm = self._ln_core(xb, sq, t1b, xn, self.wlS.sem, wl_xb)

        pe_xb0 = self.pe.n
        act_xb0 = self.act.n
        projs = ["k", "v"] + (["q"] if mine else [])
        xBdst = {"k": self.xBg["k"], "v": self.xBg["v"], "q": self.xBq}

        @blk.tensor
        def _(t):
            t.wait_ge(self.dve.sem, dve_norm)
            t.wait_ge(self.wlS.sem, wl_xb)
            for i, p in enumerate(projs):
                for tg in range(TG):
                    wge(t, self.act.sem, act_xb0 + i * TG + tg)
                    ins = None
                    for dc in range(DC):
                        ins = t.matmul(self.ps[i][0:R, 0:512],
                                       self.c_BT[p][:, dc * R:(dc + 1) * R],
                                       xn[:, dc * T + tg * 512: dc * T + tg * 512 + 512],
                                       start=(dc == 0), stop=(dc == DC - 1))
                    self.pe.inc(ins)

        @blk.scalar
        def _(s):
            for i, p in enumerate(projs):
                for tg in range(TG):
                    s.wait_ge(self.pe.sem, pe_xb0 + i * TG + tg + 1)
                    self.act.inc(s.activation(xBdst[p][0:R, tg * 512:tg * 512 + 512],
                                              self.ps[i][0:R, 0:512], AF.Copy, scale=0.5))
        act_ev0 = self.act.n
        pe_proj0 = self.pe.n
        assert pe_proj0 == pe_xb0 + len(projs) * TG

        nproj = len(projs)
        per_oc = TG + TC + (TG if mine else 0)
        wl_b = [self.wl[0].n, self.wl[1].n]
        st_b = [c.n for c in self.st4]

        def gidx(oc, cls, j):
            off = {"k": 0, "v": TG, "q": TG + TC}[cls]
            return oc * per_oc + off + j

        @blk.gpsimd
        def _(gp):
            for oc in range(DC):
                par = oc % 2
                if oc >= 2:
                    wge(gp, self.pe.sem, pe_proj0 + (oc - 1) * per_oc)
                self.block_load(gp, wcol[:, (2 * par) * DC * 128:(2 * par + 1) * DC * 128],
                                self.inp["wkT"], oc, self.wl[par])
                self.block_load(gp, wcol[:, (2 * par + 1) * DC * 128:(2 * par + 2) * DC * 128],
                                self.inp["wvT"], oc, self.wl[par])
                if mine:
                    self.block_load(gp, qcol[:, par * DC * 128:(par + 1) * DC * 128],
                                    self.inp["wqT"], oc, self.wl[par])

        @blk.tensor
        def _(t):
            t.wait_ge(self.act.sem, act_ev0)
            for oc in range(DC):
                par = oc % 2
                t.wait_ge(self.wl[par].sem, wl_b[par] + 16 * nproj * (oc // 2 + 1))
                kw = wcol[:, (2 * par) * DC * 128:(2 * par) * DC * 128 + DC * 128]
                vw = wcol[:, (2 * par + 1) * DC * 128:(2 * par + 1) * DC * 128 + DC * 128]
                for tg in range(TG):
                    i = oc * TG + tg
                    bank = self.ps[i % 2]
                    if i >= 2:
                        oc2, tg2 = divmod(i - 2, TG)
                        wge(t, self.act.sem, act_ev0 + gidx(oc2, "k", tg2) + 1)
                    for dc in range(DC):
                        t.matmul(bank, kw[:, dc * 128:(dc + 1) * 128],
                                 xn[:, dc * T + tg * 512: dc * T + tg * 512 + 512],
                                 start=(dc == 0), stop=False)
                    ins = t.matmul(bank, self.c_AT["k"][0:R, oc * 128:(oc + 1) * 128],
                                   self.xBg["k"][0:R, tg * 512:tg * 512 + 512],
                                   start=False, stop=True)
                    self.pe.inc(ins)
                for tc in range(TC):
                    i = oc * TC + tc
                    bank = self.ps[2 + (i % 2)]
                    if i >= 2:
                        oc2, tc2 = divmod(i - 2, TC)
                        wge(t, self.act.sem, act_ev0 + gidx(oc2, "v", tc2) + 1)
                    for dc in range(DC):
                        t.matmul(bank[:, 0:128], xn[:, dc * T + tc * 128: dc * T + tc * 128 + 128],
                                 vw[:, dc * 128:(dc + 1) * 128], start=(dc == 0), stop=False)
                    t.matmul(bank[:, 0:128], self.xBg["v"][0:R, tc * 128:tc * 128 + 128],
                             self.c_AT["v"][0:R, oc * 128:(oc + 1) * 128], start=False, stop=False)
                    ins = t.matmul(bank[:, 0:128], self.c_orb[0:1, :],
                                   self.c_bv[0:1, oc * 128:(oc + 1) * 128], start=False, stop=True)
                    self.pe.inc(ins)
                if mine:
                    qw = qcol[:, par * DC * 128: par * DC * 128 + DC * 128]
                    for tg in range(TG):
                        i = oc * TG + tg
                        bank = self.ps[4 + (i % 2)]
                        if i >= 2:
                            oc2, tg2 = divmod(i - 2, TG)
                            wge(t, self.act.sem, act_ev0 + gidx(oc2, "q", tg2) + 1)
                        for dc in range(DC):
                            t.matmul(bank, qw[:, dc * 128:(dc + 1) * 128],
                                     xn[:, dc * T + tg * 512: dc * T + tg * 512 + 512],
                                     start=(dc == 0), stop=False)
                        ins = t.matmul(bank, self.c_AT["q"][0:R, oc * 128:(oc + 1) * 128],
                                       self.xBq[0:R, tg * 512:tg * 512 + 512],
                                       start=False, stop=True)
                        self.pe.inc(ins)

        @blk.scalar
        def _(s):
            for oc in range(DC):
                for tg in range(TG):
                    gi = gidx(oc, "k", tg)
                    s.wait_ge(self.pe.sem, pe_proj0 + gi + 1)
                    wge(s, self.st4[gi % 4].sem, st_b[gi % 4] + 16 * (gi // 4))
                    self.act.inc(s.activation(ev[:, (gi % 4) * 512:(gi % 4) * 512 + 512],
                                              self.ps[(oc * TG + tg) % 2], AF.Identity,
                                              bias=self.c_bk[:, oc:oc + 1]))
                for tc in range(TC):
                    gi = gidx(oc, "v", tc)
                    s.wait_ge(self.pe.sem, pe_proj0 + gi + 1)
                    wge(s, self.st4[gi % 4].sem, st_b[gi % 4] + 16 * (gi // 4))
                    self.act.inc(s.activation(ev[:, (gi % 4) * 512:(gi % 4) * 512 + 128],
                                              self.ps[2 + ((oc * TC + tc) % 2)][:, 0:128],
                                              AF.Copy))
                if mine:
                    for tg in range(TG):
                        gi = gidx(oc, "q", tg)
                        s.wait_ge(self.pe.sem, pe_proj0 + gi + 1)
                        wge(s, self.st4[gi % 4].sem, st_b[gi % 4] + 16 * (gi // 4))
                        self.act.inc(s.activation(ev[:, (gi % 4) * 512:(gi % 4) * 512 + 512],
                                                  self.ps[4 + ((oc * TG + tg) % 2)], AF.Identity,
                                                  bias=self.c_bq[:, oc:oc + 1]))

        @blk.sync
        def _(sp):
            for oc in range(DC):
                for tg in range(TG):
                    gi = gidx(oc, "k", tg)
                    sp.wait_ge(self.act.sem, act_ev0 + gi + 1)
                    self.st4[gi % 4].inc(sp.dma_start(
                        out=self.kag_in[oc * 128:(oc + 1) * 128,
                                        tg * 512: tg * 512 + 512],
                        in_=ev[:, (gi % 4) * 512:(gi % 4) * 512 + 512]))
                for tc in range(TC):
                    gi = gidx(oc, "v", tc)
                    sp.wait_ge(self.act.sem, act_ev0 + gi + 1)
                    self.st4[gi % 4].inc(sp.dma_start(
                        out=self.vag_in[oc * 128:(oc + 1) * 128, tc * 128:tc * 128 + 128],
                        in_=ev[:, (gi % 4) * 512:(gi % 4) * 512 + 128]))
                if mine:
                    for tg in range(TG):
                        gi = gidx(oc, "q", tg)
                        sp.wait_ge(self.act.sem, act_ev0 + gi + 1)
                        self.st4[gi % 4].inc(sp.dma_start(
                            out=self.qd[oc * 128:(oc + 1) * 128, tg * 512:tg * 512 + 512],
                            in_=ev[:, (gi % 4) * 512:(gi % 4) * 512 + 512]))

        if mine and "dbg_q" in self.dbg:
            @blk.sync
            def _(sp):
                for c in self.st4:
                    c.wait(sp)
                self.st4[0].inc(sp.dma_start(out=self.dbg["dbg_q"], in_=self.qd))

    NCH = 16

    def p1b_allgather(self):
        blk = self.block
        rg = [[0, 1, 2, 3], [4, 5, 6, 7]]
        NCH = self.NCH
        hb = H // NCH
        rows = hb * 128

        @blk.gpsimd
        def _(g):
            for c in self.st4:
                c.wait(g)
            for c in range(NCH):
                self.cc.inc(g.collective_compute(
                    "AllGather", ALU.bypass, replica_groups=rg,
                    ins=[self.kag_in[c * rows:(c + 1) * rows, :]],
                    outs=[self.kd[c * 4 * rows:(c + 1) * 4 * rows, :]]))
                self.cc.inc(g.collective_compute(
                    "AllGather", ALU.bypass, replica_groups=rg,
                    ins=[self.vag_in[c * rows:(c + 1) * rows, :]],
                    outs=[self.vd[c * 4 * rows:(c + 1) * 4 * rows, :]]))

        if "dbg_k" in self.dbg:
            @blk.sync
            def _(sp):
                sp.wait_ge(self.cc.sem, self.cc.n)
                self.st4[0].inc(sp.dma_start(out=self.dbg["dbg_k"], in_=self.kd))
                self.st4[0].inc(sp.dma_start(out=self.dbg["dbg_v"], in_=self.vd))

    def p2_attention(self, ofm, kh, vraw, vext, qh, pt, otmp, rcp):
        blk = self.block
        T, L, KT = self.T, self.L, self.KT

        @blk.vector
        def _(v):
            v = VV(self, v)
            for b in range(2):
                ve = vext[:, b * (KT * 129):(b + 1) * (KT * 129)]
                v.memset(
                    ve.rearrange("p (kt j) -> p kt j", j=129)[:, :, 128:129], 1.0)
        dve_pre = self.dve.n

        cc_base = self.cc.n - 2 * self.NCH

        def kd_row(h, r):
            hb = H // self.NCH
            c, hh = divmod(h, hb)
            return c * 4 * hb * 128 + r * hb * 128 + hh * 128

        @blk.sync
        def _(sp):
            sp.wait_ge(self.cc.sem, cc_base + 2)
            self.ld[0].inc(sp.dma_start(out=qh[:, 0:T], in_=self.qd[0:128, :]))
            for r in range(4):
                self.ld[0].inc(sp.dma_start(out=kh[:, r * T:(r + 1) * T],
                                            in_=self.kd[kd_row(0, r): kd_row(0, r) + 128, :]))
                self.ldS.inc(sp.dma_start(out=vraw[:, r * T:(r + 1) * T],
                                          in_=self.vd[kd_row(0, r): kd_row(0, r) + 128, :]))
        self._cc_base = cc_base
        self._kd_row = kd_row

        prev = dict(pe_end=self.pe.n, dve_end=self.dve.n, inter_done=dve_pre,
                    ld_qk=self.ld[0].n, ld_v=self.ldS.n)
        for h in range(H):
            prev = self._p2_head(h, ofm, kh, vraw, vext, qh, pt, otmp, rcp, prev)

        if "dbg_o" in self.dbg:
            @blk.sync
            def _(sp):
                sp.wait_ge(self.dve.sem, self.dve.n)
                self.st4[0].inc(sp.dma_start(out=self.dbg["dbg_o"], in_=ofm[:, 0:DC * T]))

    def _p2_head(self, h, ofm, kh, vraw, vext, qh, pt, otmp, rcp, prev):
        blk = self.block
        T, L, KT = self.T, self.L, self.KT
        QG = T // 512
        par = h % 2
        pe0, act0, dve0 = self.pe.n, self.act.n, self.dve.n
        PEQG = 2 * KT + 4
        DVQG = 12

        @blk.vector
        def _(v):
            v = VV(self, v)
            v.wait_ge(self.ldS.sem, prev["ld_v"])
            if h >= 1:
                v.wait_ge(self.pe.sem, prev["pe_end"])
            ve = vext[:, par * (KT * 129):par * (KT * 129) + KT * 129]
            v.tensor_copy(
                ve.rearrange("p (kt j) -> p kt j", j=129)[:, :, 0:128],
                vraw[:, :].rearrange("p (kt j) -> p kt j", j=128))
        inter_done = dve0 + 1

        ld_qk_next, ld_v_next = prev["ld_qk"], prev["ld_v"]
        if h + 1 < H:
            @blk.sync
            def _(sp):
                npar = (h + 1) % 2
                if h >= 1:
                    sp.wait_ge(self.pe.sem, prev["pe_end"])
                hb = H // self.NCH
                sp.wait_ge(self.cc.sem, self._cc_base + 2 * ((h + 1) // hb + 1))
                self.ld[npar].inc(sp.dma_start(out=qh[:, npar * T:npar * T + T],
                                               in_=self.qd[(h + 1) * 128:(h + 2) * 128, :]))
                for r in range(4):
                    row = self._kd_row(h + 1, r)
                    self.ld[npar].inc(sp.dma_start(
                        out=kh[:, npar * L + r * T: npar * L + (r + 1) * T],
                        in_=self.kd[row: row + 128, :]))
                sp.wait_ge(self.dve.sem, inter_done)
                for r in range(4):
                    row = self._kd_row(h + 1, r)
                    self.ldS.inc(sp.dma_start(
                        out=vraw[:, r * T:(r + 1) * T],
                        in_=self.vd[row: row + 128, :]))
            ld_qk_next = self.ld[(h + 1) % 2].n
            ld_v_next = self.ldS.n

        @blk.tensor
        def _(t):
            t.wait_ge(self.ld[par].sem, prev["ld_qk"])
            t.wait_ge(self.dve.sem, inter_done)
            kb = kh[:, par * L:par * L + L]
            qb = qh[:, par * T:par * T + T]
            ve = vext[:, par * (KT * 129):par * (KT * 129) + KT * 129]
            def emit_scores(qg, kt):
                g = qg * KT + kt
                sb_ = [self.ps[0], self.ps[1], self.banks[7]][g % 3]
                wge(t, self.act.sem, act0 + g - 2)
                ins = t.matmul(sb_, kb[:, kt * 128:(kt + 1) * 128],
                               qb[:, qg * 512:qg * 512 + 512], start=True, stop=True)
                self.pe.inc(ins)

            for qg in range(QG):
                if qg == 0:
                    t.wait_ge(self.dve.sem, prev["dve_end"])
                else:
                    t.wait_ge(self.dve.sem, dve0 + 1 + DVQG * qg)
                emit_scores(qg, 0)
                emit_scores(qg, 1)
                for kt in range(KT):
                    if kt + 2 < KT:
                        emit_scores(qg, kt + 2)
                    t.wait_ge(self.act.sem, act0 + qg * KT + kt + 1)
                    pb = pt[:, ((qg * KT + kt) % 3) * 512:((qg * KT + kt) % 3) * 512 + 512]
                    ins = None
                    for qs in range(4):
                        ins = t.matmul(self.ps[2 + qs][:, 0:129],
                                       pb[:, qs * 128:(qs + 1) * 128],
                                       ve[:, kt * 129:(kt + 1) * 129],
                                       start=(kt == 0), stop=(kt == KT - 1))
                    self.pe.inc(ins)
                for qs in range(4):
                    t.wait_ge(self.dve.sem, dve0 + 1 + DVQG * qg + 3 * qs + 2)
                    ins = t.transpose(self.psT[:, 0:128], otmp[:, :], self.c_idb[:, :])
                    self.pe.inc(ins)
        pe_end = self.pe.n

        @blk.scalar
        def _(s):
            for qg in range(QG):
                for kt in range(KT):
                    off = 1 if kt == 0 else (2 if kt == 1 else 2 * kt - 1)
                    s.wait_ge(self.pe.sem, pe0 + qg * PEQG + off)
                    g = qg * KT + kt
                    sb_ = [self.ps[0], self.ps[1], self.banks[7]][g % 3]
                    pb = pt[:, (g % 3) * 512:(g % 3) * 512 + 512]
                    self.act.inc(s.activation(pb, sb_, AF.Exp, scale=SCALE))

        @blk.vector
        def _(v):
            v = VV(self, v)
            for qg in range(QG):
                for qs in range(4):
                    v.wait_ge(self.pe.sem, pe0 + qg * PEQG + 2 * KT)
                    ob = self.ps[2 + qs]
                    v.reciprocal(rcp[:, qs:qs + 1], ob[:, 128:129])
                    if qg > 0 or qs > 0:
                        pqg, pqs = (qg, qs - 1) if qs > 0 else (qg - 1, 3)
                        v.wait_ge(self.pe.sem, pe0 + pqg * PEQG + 2 * KT + pqs + 1)
                    v.tensor_scalar(otmp[:, :], ob[:, 0:128],
                                    rcp[:, qs:qs + 1], None, ALU.mult)
                    v.wait_ge(self.pe.sem, pe0 + qg * PEQG + 2 * KT + qs + 1)
                    v.tensor_copy(
                        ofm[:, h * T + qg * 512 + qs * 128: h * T + qg * 512 + qs * 128 + 128],
                        self.psT[:, 0:128])
        dve_end = self.dve.n

        return dict(pe_end=pe_end, dve_end=dve_end, inter_done=inter_done,
                    ld_qk=ld_qk_next, ld_v=ld_v_next)

    def p3_oproj(self, ofm, wcol, xtile, x2t):
        blk = self.block
        T, TG = self.T, self.TG

        @blk.gpsimd
        def _(gp):
            self.wlS.inc(gp.dma_start(out=self.c_ATo[:, :], in_=self.inp["aoT"]))
        wl_at = self.wlS.n

        pe0, act0, dve0 = self.pe.n, self.act.n, self.dve.n
        ld_b = [self.ld[0].n, self.ld[1].n]
        st_b = [self.st4[0].n, self.st4[1].n]
        wl_b = [self.wl[0].n, self.wl[1].n]

        @blk.tensor
        def _(t):
            for tg in range(TG):
                wge(t, self.act.sem, act0 + tg)
                ins = None
                for dc in range(DC):
                    ins = t.matmul(self.ps[2][0:R, 0:512],
                                   self.c_BT["o"][:, dc * R:(dc + 1) * R],
                                   ofm[:, dc * T + tg * 512: dc * T + tg * 512 + 512],
                                   start=(dc == 0), stop=(dc == DC - 1))
                self.pe.inc(ins)

        @blk.scalar
        def _(s):
            for tg in range(TG):
                s.wait_ge(self.pe.sem, pe0 + tg + 1)
                self.act.inc(s.activation(self.xBo[0:R, tg * 512:tg * 512 + 512],
                                          self.ps[2][0:R, 0:512], AF.Copy, scale=0.5))
        act_xbo = self.act.n

        @blk.gpsimd
        def _(gp):
            for oc in range(DC):
                par = oc % 2
                if oc >= 2:
                    wge(gp, self.pe.sem, pe0 + TG + (oc - 1) * TG)
                self.block_load(gp, wcol[:, par * DC * 128:(par + 1) * DC * 128],
                                self.inp["woT"], oc, self.wl[par])

        @blk.sync
        def _(sp):
            NI = DC * TG
            for i in range(NI + 2):
                if i >= 2:
                    j = i - 2
                    sp.wait_ge(self.dve.sem, dve0 + j + 1)
                    self.st4[j % 2].inc(sp.dma_start(
                        out=self.x2d[(j // TG) * 128:(j // TG + 1) * 128,
                                     (j % TG) * 512:(j % TG) * 512 + 512],
                        in_=x2t[:, (j % 2) * 512:(j % 2) * 512 + 512]))
                if i < NI:
                    par2 = i % 2
                    self.ld[par2].inc(sp.dma_start(
                        out=xtile[:, par2 * 512:par2 * 512 + 512],
                        in_=self.xt_in[(i // TG) * 128:(i // TG + 1) * 128,
                                       (i % TG) * 512:(i % TG) * 512 + 512]))

        @blk.tensor
        def _(t):
            t.wait_ge(self.act.sem, act_xbo)
            t.wait_ge(self.wlS.sem, wl_at)
            for oc in range(DC):
                par = oc % 2
                t.wait_ge(self.wl[par].sem, wl_b[par] + 16 * (oc // 2 + 1))
                ow = wcol[:, par * DC * 128:par * DC * 128 + DC * 128]
                for qg in range(TG):
                    i = oc * TG + qg
                    bank = self.ps[i % 2]
                    if i >= 2:
                        wge(t, self.dve.sem, dve0 + i - 1)
                    for dc in range(DC):
                        t.matmul(bank, ow[:, dc * 128:(dc + 1) * 128],
                                 ofm[:, dc * T + qg * 512: dc * T + qg * 512 + 512],
                                 start=(dc == 0), stop=False)
                    ins = t.matmul(bank, self.c_ATo[0:R, oc * 128:(oc + 1) * 128],
                                   self.xBo[0:R, qg * 512:qg * 512 + 512],
                                   start=False, stop=True)
                    self.pe.inc(ins)
        pe_op0 = pe0 + TG

        @blk.vector
        def _(v):
            v = VV(self, v)
            for oc in range(DC):
                for qg in range(TG):
                    i = oc * TG + qg
                    par2 = i % 2
                    v.wait_ge(self.pe.sem, pe_op0 + i + 1)
                    v.wait_ge(self.ld[par2].sem, ld_b[par2] + 16 * (i // 2 + 1))
                    wge(v, self.st4[par2].sem, st_b[par2] + 16 * (i // 2))
                    bank = self.ps[i % 2]
                    v.scalar_tensor_tensor(
                        x2t[:, par2 * 512:par2 * 512 + 512], bank,
                        self.c_bo[:, oc:oc + 1], xtile[:, par2 * 512:par2 * 512 + 512],
                        ALU.add, ALU.add)

        if "dbg_x2" in self.dbg:
            @blk.sync
            def _(sp):
                for c in self.st4:
                    c.wait(sp)
                self.st4[0].inc(sp.dma_start(out=self.dbg["dbg_x2"], in_=self.x2d))

    def p4_ln2(self, xn2, sq, t1b):
        blk = self.block

        @blk.gpsimd
        def _(gp):
            self.wlS.inc(gp.dma_start(
                out=xn2[:, 0:DC * self.T].rearrange("p (dc t) -> p dc t", t=self.T),
                in_=self.x2d.rearrange("(dc p) t -> p dc t", p=128)))
        wl_x = self.wlS.n

        self._ln_core(xn2, sq, t1b, xn2, self.wlS.sem, wl_x)

        if "dbg_xn2" in self.dbg:
            @blk.sync
            def _(sp):
                sp.wait_ge(self.dve.sem, self.dve.n)
                self.st4[0].inc(sp.dma_start(out=self.dbg["dbg_xn2"], in_=xn2[:, 0:DC * self.T]))

    def p5_router(self, xn2, pr, rsc, rs2, rs3):
        blk = self.block
        T, TC = self.T, self.TC
        for tc in range(TC):
            pe_tc0, act_tc0, dve_tc0 = self.pe.n, self.act.n, self.dve.n

            @blk.tensor
            def _(t):
                wge(t, self.dve.sem, dve_tc0)
                wge(t, self.act.sem, act_tc0)
                ins = None
                for dc in range(DC):
                    ins = t.matmul(self.ps[0][:, 0:9],
                                   xn2[:, dc * T + tc * 128: dc * T + tc * 128 + 128],
                                   self.c_gw[:, dc * 9:(dc + 1) * 9],
                                   start=(dc == 0), stop=(dc == DC - 1))
                self.pe.inc(ins)

            @blk.vector
            def _(v):
                v = VV(self, v)
                v.wait_ge(self.pe.sem, pe_tc0 + 1)
                v.reduce_max(rsc[:, 0:1], self.ps[0][:, 0:8], axis=AX.X)
                v.tensor_scalar(rsc[:, 1:2], rsc[:, 0:1], -1.0, None, ALU.mult)

            @blk.scalar
            def _(s):
                s.wait_ge(self.dve.sem, dve_tc0 + 2)
                self.act.inc(s.activation(pr[:, 0:8], self.ps[0][:, 0:8], AF.Exp,
                                          bias=rsc[:, 1:2]))
                self.act.inc(s.activation(rsc[:, 2:3], self.ps[0][:, 8:9], AF.Sigmoid))

            @blk.vector
            def _(v):
                v = VV(self, v)
                v.wait_ge(self.act.sem, act_tc0 + 1)
                v.reduce_sum(rsc[:, 3:4], pr[:, 0:8], axis=AX.X)
                v.reciprocal(rsc[:, 4:5], rsc[:, 3:4])
                v.tensor_scalar(pr[:, 0:8], pr[:, 0:8], rsc[:, 4:5], None, ALU.mult)
                prA = pr[:, 0:8].unsqueeze(2).broadcast_to([128, 8, 8])
                prB = pr[:, 0:8].unsqueeze(1).broadcast_to([128, 8, 8])
                v.tensor_tensor(rs2[:, 0:64], prA, prB, ALU.is_lt)
                v.tensor_tensor(rs3[:, 0:64], prA, prB, ALU.is_equal)
                v.tensor_tensor(rs3[:, 0:64], rs3[:, 0:64], self.c_tie[:, 0:64], ALU.mult)
                v.tensor_tensor(rs2[:, 0:64], rs2[:, 0:64], rs3[:, 0:64], ALU.add)
                v.tensor_reduce(rsc[:, 8:16], rs2[:, 0:64].rearrange("p (e j) -> p e j", j=8),
                                AX.X, ALU.add)
                v.tensor_scalar(rsc[:, 16:24], rsc[:, 8:16], 2.0, None, ALU.is_lt)
                v.tensor_tensor(pr[:, 0:8], pr[:, 0:8], rsc[:, 16:24], ALU.mult)

            @blk.tensor
            def _(t):
                t.wait_ge(self.dve.sem, dve_tc0 + 12)
                ins = t.transpose(self.ps[1][0:8, 0:128], pr[:, 0:8], self.c_idf[:, :])
                self.pe.inc(ins)
                t.wait_ge(self.act.sem, act_tc0 + 2)
                ins = t.transpose(self.ps[2][0:1, 0:128], rsc[:, 2:3], self.c_idf[:, :])
                self.pe.inc(ins)

            @blk.vector
            def _(v):
                v = VV(self, v)
                v.wait_ge(self.pe.sem, pe_tc0 + 3)
                v.tensor_copy(self.w_fm[0:NE, tc * 128:(tc + 1) * 128], self.ps[1][0:8, 0:128])
                v.tensor_copy(self.sg_fm[0:1, tc * 128:(tc + 1) * 128],
                              self.ps[2][0:1, 0:128])

        if "dbg_w" in self.dbg:
            @blk.sync
            def _(sp):
                sp.wait_ge(self.dve.sem, self.dve.n)
                self.st4[0].inc(sp.dma_start(out=self.dbg["dbg_w"], in_=self.w_fm[0:NE, 0:T]))
                self.st4[0].inc(sp.dma_start(out=self.dbg["dbg_sg"], in_=self.sg_fm[0:1, 0:T]))

    def p6_up(self, xn2, wg, wu, sgb, tml, ht, web):
        blk = self.block
        T, TG = self.T, self.TG
        segs = [("sh", SIC, None)]
        prev_dve_end = self.dve.n
        for name, nic, eidx in segs:
            pe0, act0, dve0 = self.pe.n, self.act.n, self.dve.n
            wl_b = [self.wl[0].n, self.wl[1].n]
            st_b = [self.st4[0].n, self.st4[1].n]
            expert = eidx is not None
            pe_pre = TG if expert else 0
            dve_pre = TG if expert else 0
            dve_per = 2 if expert else 1

            @blk.gpsimd
            def _(gp):
                gp.wait_ge(self.pe.sem, pe0)
                for ic in range(nic):
                    par = ic % 2
                    if ic >= 2:
                        wge(gp, self.pe.sem, pe0 + pe_pre + 2 * TG * (ic - 1))
                    if expert:
                        b0 = eidx * 2 * MIC + ic
                        b1 = eidx * 2 * MIC + MIC + ic
                        src = self.inp["w13T"]
                    else:
                        b0 = ic
                        b1 = SIC + ic
                        src = self.inp["wguT"]
                    self.block_load(gp, wg[:, par * DC * 128:(par + 1) * DC * 128],
                                    src, b0, self.wl[par])
                    self.block_load(gp, wu[:, par * DC * 128:(par + 1) * DC * 128],
                                    src, b1, self.wl[par])

            if expert:
                @blk.tensor
                def _(t):
                    t.wait_ge(self.dve.sem, prev_dve_end)
                    for tg in range(TG):
                        if tg > 0:
                            t.wait_ge(self.dve.sem, dve0 + tg)
                        ins = t.matmul(self.ps[4], self.c_sel[0:NE, eidx * 128:(eidx + 1) * 128],
                                       self.w_fm[0:NE, tg * 512:tg * 512 + 512],
                                       start=True, stop=True)
                        self.pe.inc(ins)

                @blk.vector
                def _(v):
                    v = VV(self, v)
                    for tg in range(TG):
                        v.wait_ge(self.pe.sem, pe0 + tg + 1)
                        v.tensor_copy(web[:, tg * 512:tg * 512 + 512], self.ps[4])

            pe1 = pe0 + pe_pre

            @blk.tensor
            def _(t):
                if not expert:
                    t.wait_ge(self.dve.sem, prev_dve_end)
                for ic in range(nic):
                    par = ic % 2
                    t.wait_ge(self.wl[par].sem, wl_b[par] + 32 * (ic // 2 + 1))
                    for tg in range(TG):
                        it = ic * TG + tg
                        par2 = it % 2
                        gbank = self.ps[0 + 2 * par2]
                        ubank = self.ps[1 + 2 * par2]
                        if it >= 2:
                            wge(t, self.act.sem, act0 + (it - 1))
                            wge(t, self.dve.sem, dve0 + dve_pre + dve_per * (it - 1))
                        ins = None
                        for dc in range(DC):
                            ins = t.matmul(gbank, wg[:, par * DC * 128 + dc * 128: par * DC * 128 + (dc + 1) * 128],
                                           xn2[:, dc * T + tg * 512: dc * T + tg * 512 + 512],
                                           start=(dc == 0), stop=(dc == DC - 1))
                        self.pe.inc(ins)
                        ins = None
                        for dc in range(DC):
                            ins = t.matmul(ubank, wu[:, par * DC * 128 + dc * 128: par * DC * 128 + (dc + 1) * 128],
                                           xn2[:, dc * T + tg * 512: dc * T + tg * 512 + 512],
                                           start=(dc == 0), stop=(dc == DC - 1))
                        self.pe.inc(ins)

            @blk.scalar
            def _(s):
                for ic in range(nic):
                    for tg in range(TG):
                        it = ic * TG + tg
                        par2 = it % 2
                        s.wait_ge(self.pe.sem, pe1 + 2 * it + 1)
                        if it >= 2:
                            wge(s, self.dve.sem, dve0 + dve_pre + dve_per * (it - 1))
                        gbank = self.ps[0 + 2 * par2]
                        self.act.inc(s.activation(sgb[:, par2 * 512:par2 * 512 + 512],
                                                  gbank, AF.Silu))

            @blk.vector
            def _(v):
                v = VV(self, v)
                for ic in range(nic):
                    for tg in range(TG):
                        it = ic * TG + tg
                        par2 = it % 2
                        v.wait_ge(self.pe.sem, pe1 + 2 * it + 2)
                        v.wait_ge(self.act.sem, act0 + it + 1)
                        wge(v, self.st4[par2].sem, st_b[par2] + 16 * (it // 2))
                        ubank = self.ps[1 + 2 * par2]
                        if expert:
                            v.tensor_tensor(tml[:, 0:512],
                                            sgb[:, par2 * 512:par2 * 512 + 512],
                                            ubank, ALU.mult)
                            v.tensor_tensor(ht[:, par2 * 512:par2 * 512 + 512],
                                            tml[:, 0:512],
                                            web[:, tg * 512:tg * 512 + 512], ALU.mult)
                        else:
                            v.tensor_tensor(ht[:, par2 * 512:par2 * 512 + 512],
                                            sgb[:, par2 * 512:par2 * 512 + 512],
                                            ubank, ALU.mult)

            @blk.sync
            def _(sp):
                for ic in range(nic):
                    for tg in range(TG):
                        it = ic * TG + tg
                        par2 = it % 2
                        sp.wait_ge(self.dve.sem, dve0 + dve_pre + dve_per * (it + 1))
                        if expert:
                            rows0 = tg * NE * MI + eidx * MI + ic * 128
                            dst = self.hexd[rows0:rows0 + 128, 0:512]
                        else:
                            rows0 = tg * SI + ic * 128
                            dst = self.hshd[rows0:rows0 + 128, 0:512]
                        self.st4[par2].inc(sp.dma_start(out=dst,
                                                        in_=ht[:, par2 * 512:par2 * 512 + 512]))
            prev_dve_end = self.dve.n

        if "dbg_hsh" in self.dbg:
            @blk.sync
            def _(sp):
                for c in self.st4:
                    c.wait(sp)
                self.st4[0].inc(sp.dma_start(out=self.dbg["dbg_hsh"], in_=self.hshd))


    def p4b_xn2tm(self, xn2, xstg):
        blk = self.block
        T, TC = self.T, self.TC
        pe0, dve0 = self.pe.n, self.dve.n
        st_b = [self.st4[2].n, self.st4[3].n]

        @blk.tensor
        def _(t):
            for tc in range(TC):
                for dc in range(DC):
                    i = tc * DC + dc
                    wge(t, self.dve.sem, dve0 + i - 1)
                    tb = self.psT if i % 2 == 0 else self.psT2
                    ins = t.transpose(tb[:, 0:128],
                                      xn2[:, dc * T + tc * 128: dc * T + tc * 128 + 128],
                                      self.c_idb[:, :])
                    self.pe.inc(ins)

        @blk.vector
        def _(v):
            v = VV(self, v)
            for tc in range(TC):
                par = tc % 2
                for dc in range(DC):
                    i = tc * DC + dc
                    v.wait_ge(self.pe.sem, pe0 + i + 1)
                    if dc == 0:
                        wge(v, self.st4[2 + par].sem, st_b[par] + 16 * (tc // 2))
                    srcb = self.psT if i % 2 == 0 else self.psT2
                    v.tensor_copy(xstg[:, par * D + dc * 128: par * D + (dc + 1) * 128],
                                  srcb[:, 0:128])

        @blk.sync
        def _(sp):
            for tc in range(TC):
                par = tc % 2
                sp.wait_ge(self.dve.sem, dve0 + (tc + 1) * DC)
                self.st4[2 + par].inc(sp.dma_start(
                    out=self.xn2tm[tc * 128:(tc + 1) * 128, :],
                    in_=xstg[:, par * D:(par + 1) * D]))
        self.xn2tm_st = [self.st4[2].n, self.st4[3].n]

    def p5b_index(self, ee):
        nc = self.nc
        blk = self.block
        T, TC, CAP, NSL, PMW = self.T, self.TC, self.CAP, self.NSL, self.PMW
        NCHK = T // 512
        mfm = ee(nc.sbuf_tensor("i_mfm", [NE, T], F32))
        sfm = ee(nc.sbuf_tensor("i_sfm", [NE, T], F32))
        gfm = ee(nc.sbuf_tensor("i_gfm", [NE, T], F32))
        cum = ee(nc.sbuf_tensor("i_cum", [NE, T], F32))
        m1 = ee(nc.sbuf_tensor("i_m1", [NE, T], F32))
        m2 = ee(nc.sbuf_tensor("i_m2", [NE, T], F32))
        amk = ee(nc.sbuf_tensor("i_amk", [NE, T], F32))
        bmk = ee(nc.sbuf_tensor("i_bmk", [NE, T], F32))
        rowA = ee(nc.sbuf_tensor("i_rowA", [1, T], F32))
        rowB = ee(nc.sbuf_tensor("i_rowB", [1, T], F32))
        wrA = ee(nc.sbuf_tensor("i_wrA", [1, T], F32))
        wrB = ee(nc.sbuf_tensor("i_wrB", [1, T], F32))
        saf = ee(nc.sbuf_tensor("i_saf", [128, TC], F32))
        sbf = ee(nc.sbuf_tensor("i_sbf", [128, TC], F32))
        paf = ee(nc.sbuf_tensor("i_paf", [128, TC], F32))
        pbf = ee(nc.sbuf_tensor("i_pbf", [128, TC], F32))
        paf_i = ee(nc.sbuf_tensor("i_paf_i", [128, TC], I32))
        pbf_i = ee(nc.sbuf_tensor("i_pbf_i", [128, TC], I32))
        waf = ee(nc.sbuf_tensor("i_waf", [128, TC], F32))
        wbf = ee(nc.sbuf_tensor("i_wbf", [128, TC], F32))
        zrow = ee(nc.sbuf_tensor("i_zrow", [128, 512], F32))
        pe0, dve0 = self.pe.n, self.dve.n
        wl0 = self.wl[0].n

        @blk.vector
        def _(v):
            v = VV(self, v)
            v.memset(zrow[:, :], 0.0)
            v.tensor_scalar(mfm[0:NE, :], self.w_fm[0:NE, :], 0.0, None, ALU.is_gt)
            v.tensor_tensor_scan(sfm[0:NE, :], mfm[0:NE, :], mfm[0:NE, :], 0.0,
                                 ALU.add, ALU.bypass)
            v.tensor_tensor(sfm[0:NE, :], sfm[0:NE, :], mfm[0:NE, :], ALU.subtract)
        dve_v1 = self.dve.n

        @blk.tensor
        def _(t):
            t.wait_ge(self.dve.sem, dve_v1)
            for ch in range(NCHK):
                if ch >= 2:
                    t.wait_ge(self.dve.sem, dve_v1 + ch - 1)
                ins = t.matmul(self.ps[4 + ch % 2][0:NE, 0:512], self.c_tri8[0:NE, 0:NE],
                               mfm[0:NE, ch * 512:(ch + 1) * 512], start=True, stop=True)
                self.pe.inc(ins)

        @blk.vector
        def _(v):
            v = VV(self, v)
            for ch in range(NCHK):
                v.wait_ge(self.pe.sem, pe0 + ch + 1)
                v.tensor_copy(cum[0:NE, ch * 512:(ch + 1) * 512],
                              self.ps[4 + ch % 2][0:NE, 0:512])
        pe_cum = self.pe.n

        @blk.vector
        def _(v):
            v = VV(self, v)
            v.tensor_scalar(m1[0:NE, :], sfm[0:NE, :], float(CAP), None, ALU.is_lt)
            v.tensor_tensor(m1[0:NE, :], m1[0:NE, :], mfm[0:NE, :], ALU.mult)
            v.tensor_scalar(gfm[0:NE, :], sfm[0:NE, :], self.c_segb[0:NE, 0:1], None, ALU.add)
            v.tensor_tensor(gfm[0:NE, :], gfm[0:NE, :], m1[0:NE, :], ALU.mult)
            v.tensor_tensor(m2[0:NE, :], self.c_trash[0:NE, :], m1[0:NE, :], ALU.mult)
            v.tensor_tensor(m2[0:NE, :], self.c_trash[0:NE, :], m2[0:NE, :], ALU.subtract)
            v.tensor_tensor(gfm[0:NE, :], gfm[0:NE, :], m2[0:NE, :], ALU.add)
            v.tensor_scalar(amk[0:NE, :], cum[0:NE, :], 0.0, None, ALU.is_equal)
            v.tensor_tensor(amk[0:NE, :], amk[0:NE, :], mfm[0:NE, :], ALU.mult)
            v.tensor_scalar(bmk[0:NE, :], cum[0:NE, :], 1.0, None, ALU.is_equal)
            v.tensor_tensor(bmk[0:NE, :], bmk[0:NE, :], mfm[0:NE, :], ALU.mult)
            v.tensor_tensor(m1[0:NE, :], amk[0:NE, :], gfm[0:NE, :], ALU.mult)
            v.tensor_tensor(m2[0:NE, :], bmk[0:NE, :], gfm[0:NE, :], ALU.mult)
            v.tensor_tensor(amk[0:NE, :], amk[0:NE, :], self.w_fm[0:NE, :], ALU.mult)
            v.tensor_tensor(bmk[0:NE, :], bmk[0:NE, :], self.w_fm[0:NE, :], ALU.mult)
        dve_v2 = self.dve.n

        @blk.tensor
        def _(t):
            t.wait_ge(self.dve.sem, dve_v2)
            for pp, (sA, sB) in enumerate([(m1, m2), (amk, bmk)]):
                for ch in range(NCHK):
                    it = pp * NCHK + ch
                    if it >= 2:
                        t.wait_ge(self.dve.sem, dve_v2 + 2 * (it - 1))
                    ins = t.matmul(self.ps[4 + ch % 2][0:1, 0:512], self.c_ones8[0:NE, 0:1],
                                   sA[0:NE, ch * 512:(ch + 1) * 512], start=True, stop=True)
                    self.pe.inc(ins)
                    ins = t.matmul(self.banks[6][0:1, 0:512], self.c_ones8[0:NE, 0:1],
                                   sB[0:NE, ch * 512:(ch + 1) * 512], start=True, stop=True)
                    self.pe.inc(ins)

        @blk.vector
        def _(v):
            v = VV(self, v)
            for pp, (dA, dB) in enumerate([(rowA, rowB), (wrA, wrB)]):
                for ch in range(NCHK):
                    it = pp * NCHK + ch
                    v.wait_ge(self.pe.sem, pe_cum + 2 * it + 2)
                    v.tensor_copy(dA[0:1, ch * 512:(ch + 1) * 512], self.ps[4 + ch % 2][0:1, 0:512])
                    v.tensor_copy(dB[0:1, ch * 512:(ch + 1) * 512], self.banks[6][0:1, 0:512])
        pmA, pmB, pmt = amk, bmk, m1
        it1 = m2[0:NE, :].bitcast(I32)
        it2 = gfm[0:NE, :].bitcast(I32)

        @blk.vector
        def _(v):
            v = VV(self, v)
            for src, dst in ((rowA, pmA), (rowB, pmB)):
                v.tensor_copy(it1[0:1, :], src[0:1, :])
                v.tensor_scalar(it2[0:1, :], it1[0:1, :], 127, None, ALU.bitwise_and)
                v.tensor_scalar(it1[0:1, :], it1[0:1, :], 7, None, ALU.arith_shift_right)
                v.tensor_copy(dst[0:1, :], it2[0:1, :])
                v.tensor_scalar(dst[0:1, :], dst[0:1, :], float(PMW), None, ALU.mult)
                v.tensor_copy(pmt[0:1, :], it1[0:1, :])
                v.tensor_tensor(dst[0:1, :], dst[0:1, :], pmt[0:1, :], ALU.add)
        pe_sab = self.pe.n
        dve_sab = self.dve.n

        @blk.tensor
        def _(t):
            t.wait_ge(self.dve.sem, dve_sab)
            for tc in range(TC):
                for k, src in enumerate([rowA, rowB, wrA, wrB, pmA, pmB]):
                    i = tc * 6 + k
                    wge(t, self.dve.sem, dve_sab + i - 1)
                    ins = t.transpose(self.ps[4 + i % 2][0:128, 0:1],
                                      src[0:1, tc * 128:(tc + 1) * 128],
                                      self.c_idf[0:1, 0:1])
                    self.pe.inc(ins)

        @blk.vector
        def _(v):
            v = VV(self, v)
            for tc in range(TC):
                for k, dst in enumerate([saf, sbf, waf, wbf, paf, pbf]):
                    i = tc * 6 + k
                    v.wait_ge(self.pe.sem, pe_sab + i + 1)
                    v.tensor_copy(dst[:, tc:tc + 1], self.ps[4 + i % 2][0:128, 0:1])
            v.tensor_copy(self.slA_i[:, 0:TC], saf[:, 0:TC])
            v.tensor_copy(self.slB_i[:, 0:TC], sbf[:, 0:TC])
            v.tensor_copy(paf_i[:, 0:TC], paf[:, 0:TC])
            v.tensor_copy(pbf_i[:, 0:TC], pbf[:, 0:TC])
        dve_all = self.dve.n

        @blk.gpsimd
        def _(g):
            g.wait_ge(self.dve.sem, dve_all)
            self.sc.inc(g.dma_start(
                out=self.slotpm.rearrange("(p c) x -> p (c x)", p=128),
                in_=zrow[:, 0:PMW]))
            self.sc.inc(g.dma_start(
                out=self.wgflat.rearrange("(p c) x -> p (c x)", p=128),
                in_=zrow[:, 0:PMW]))
            for q in range(D // 512):
                self.sc.inc(g.dma_start(
                    out=self.contrib[self.NSL:self.NSL + 128, q * 512:(q + 1) * 512],
                    in_=zrow[:, 0:512]))
            g.wait_ge(self.sc.sem, self.sc.n)
            for tc in range(TC):
                for out, idx, val in ((self.slotpm, paf_i, self.c_tokid),
                                      (self.slotpm, pbf_i, self.c_tokid),
                                      (self.wgflat, self.slA_i, waf),
                                      (self.wgflat, self.slB_i, wbf)):
                    self.sc.inc(g.indirect_dma_start(
                        out=out,
                        out_offset=IndirectOffsetOnAxis(idx[:, tc:tc + 1], 0),
                        in_=val[:, tc:tc + 1], in_offset=None))
        self.sc_scatter_end = self.sc.n

    def p6b_sparse(self, e):
        nc = self.nc
        blk = self.block
        T, CAP, NSL, PMW = self.T, self.CAP, self.NSL, self.PMW
        JT = CAP // 128
        SCH = self.SCH
        spm_f = e(nc.sbuf_tensor("b_spm_f", [128, PMW], F32))
        spm_i = e(nc.sbuf_tensor("b_spm_i", [128, PMW], I32))
        xstg = e(nc.sbuf_tensor("b_xstg", [128, 2 * D], BF16))
        xg = [e(nc.sbuf_tensor(f"b_xg{i}", [128, DC * CAP], BF16)) for i in range(2)]
        hg = [e(nc.sbuf_tensor(f"b_hg{i}", [128, MIC * CAP], BF16)) for i in range(2)]
        w2a = e(nc.sbuf_tensor("b_w2a", [128, MIC * D], BF16))
        wgb = e(nc.sbuf_tensor("b_wgb", [128, CAP], F32))
        wgr = e(nc.sbuf_tensor("b_wgr", [1, CAP], F32))
        wgc = e(nc.sbuf_tensor("b_wgc", [128, 2 * D], BF16))
        wuc = e(nc.sbuf_tensor("b_wuc", [128, 2 * D], BF16))
        sgb = e(nc.sbuf_tensor("b_sgb", [128, 2 * 512], BF16))
        tml = e(nc.sbuf_tensor("b_tml", [128, 512], F32))
        ev = e(nc.sbuf_tensor("b_ev", [128, 2 * 512], F32))

        @blk.sync
        def _(sp):
            sp.wait_ge(self.sc.sem, self.sc_scatter_end)
            self.scl.inc(sp.dma_start(
                out=spm_f[:, :],
                in_=self.slotpm.rearrange("(p c) x -> p (c x)", p=128)))
        ld_spm = self.scl.n

        @blk.vector
        def _(v):
            v = VV(self, v)
            v.wait_ge(self.scl.sem, ld_spm)
            v.tensor_copy(spm_i[:, :], spm_f[:, :])
        dve_spm = self.dve.n

        init = dict(pe_tr_end=self.pe.n, dve_up_end=self.dve.n, pe_up_end=self.pe.n,
                    pe_dn_end=self.pe.n, dve_dn_end=self.dve.n)
        hist = [init, init]
        for ex in range(NE):
            r = self._p6b_expert(ex, spm_i, spm_f, xstg, xg[ex % 2], hg[ex % 2], w2a,
                                 wgb, wgr, wgc, wuc, sgb, tml, ev, dve_spm,
                                 hist[-1], hist[-2])
            hist.append(r)

        if "dbg_contrib" in self.dbg:
            @blk.sync
            def _(sp):
                for c in self.st4:
                    c.wait(sp)
                self.st4[0].inc(sp.dma_start(out=self.dbg["dbg_contrib"], in_=self.contrib))

    def _p6b_expert(self, ex, spm_i, spm_f, xstg, xg, hg, w2a, wgb, wgr, wgc, wuc,
                    sgb, tml, ev, dve_spm, prev, prev2):
        blk = self.block
        CAP, SCH = self.CAP, self.SCH
        JT = CAP // 128
        NS = len(SCH)
        NOG = D // 512
        pe0, act0, dve0 = self.pe.n, self.act.n, self.dve.n
        gl_b = [self.gl[0].n, self.gl[1].n]
        wl_b = [self.wl[0].n, self.wl[1].n]
        st_b2 = [self.st4[2].n, self.st4[3].n]

        @blk.gpsimd
        def _(g):
            g.wait_ge(self.dve.sem, dve_spm)
            if ex == 0:
                wge(g, self.st4[2].sem, self.xn2tm_st[0])
                wge(g, self.st4[3].sem, self.xn2tm_st[1])
            for j in range(JT):
                par = j % 2
                if j >= 2:
                    wge(g, self.pe.sem, pe0 + 16 * (j - 1))
                elif ex >= 1:
                    wge(g, self.pe.sem, prev["pe_tr_end"])
                self.gl[par].inc(g.indirect_dma_start(
                    out=xstg[:, par * D:(par + 1) * D], out_offset=None,
                    in_=self.xn2tm,
                    in_offset=IndirectOffsetOnAxis(spm_i[:, ex * JT + j: ex * JT + j + 1], 0)))

        @blk.tensor
        def _(t):
            for j in range(JT):
                par = j % 2
                t.wait_ge(self.gl[par].sem, gl_b[par] + 16 * (j // 2 + 1))
                for dc in range(DC):
                    i = j * DC + dc
                    wge(t, self.dve.sem, dve0 + i - 1)
                    tb = self.psT if i % 2 == 0 else self.psT2
                    ins = t.transpose(tb[:, 0:128],
                                      xstg[:, par * D + dc * 128: par * D + (dc + 1) * 128],
                                      self.c_idb[:, :])
                    self.pe.inc(ins)
        pe_tr_end = self.pe.n

        @blk.vector
        def _(v):
            v = VV(self, v)
            if ex >= 2:
                v.wait_ge(self.pe.sem, prev2["pe_up_end"])
            for j in range(JT):
                for dc in range(DC):
                    i = j * DC + dc
                    v.wait_ge(self.pe.sem, pe0 + i + 1)
                    srcb = self.psT if i % 2 == 0 else self.psT2
                    v.tensor_copy(xg[:, dc * CAP + j * 128: dc * CAP + (j + 1) * 128],
                                  srcb[:, 0:128])
        dve_tr = self.dve.n

        @blk.sync
        def _(sp):
            if ex == 0:
                wge(sp, self.dve.sem, dve_spm)
            else:
                wge(sp, self.pe.sem, pe0)
            self.scl.inc(sp.dma_start(out=wgr[0:1, 0:CAP],
                                      in_=self.wgflat[ex * CAP:(ex + 1) * CAP, :]
                                      .rearrange("a x -> x (a)")))
        ldS_w = self.scl.n

        @blk.tensor
        def _(t):
            t.wait_ge(self.scl.sem, ldS_w)
            for si, (sc, sn) in enumerate(SCH):
                wge(t, self.dve.sem, dve_tr + si)
                ins = t.matmul(self.ps[4][0:128, 0:sn], self.c_orf[0:1, :],
                               wgr[0:1, sc:sc + sn], start=True, stop=True)
                self.pe.inc(ins)

        @blk.vector
        def _(v):
            v = VV(self, v)
            for si, (sc, sn) in enumerate(SCH):
                v.wait_ge(self.pe.sem, pe_tr_end + si + 1)
                v.tensor_copy(wgb[:, sc:sc + sn], self.ps[4][0:128, 0:sn])
        pe_wgb = self.pe.n
        dve_wgb = self.dve.n

        @blk.gpsimd
        def _(g):
            for icp in range(MIC):
                par = icp % 2
                if icp >= 2:
                    wge(g, self.pe.sem, pe_wgb + 2 * NS * (icp - 1))
                elif ex >= 1:
                    wge(g, self.pe.sem, prev["pe_up_end"])
                b0 = ex * 2 * MIC + icp
                b1 = ex * 2 * MIC + MIC + icp
                self.block_load(g, wgc[:, par * D:(par + 1) * D], self.inp["w13T"], b0, self.wl[par])
                self.block_load(g, wuc[:, par * D:(par + 1) * D], self.inp["w13T"], b1, self.wl[par])

        @blk.tensor
        def _(t):
            t.wait_ge(self.dve.sem, dve_wgb)
            for icp in range(MIC):
                par = icp % 2
                t.wait_ge(self.wl[par].sem, wl_b[par] + 32 * (icp // 2 + 1))
                for si, (sc, sn) in enumerate(SCH):
                    it = icp * NS + si
                    par2 = it % 2
                    gbank = self.ps[0 + 2 * par2]
                    ubank = self.ps[1 + 2 * par2]
                    if it >= 2:
                        wge(t, self.act.sem, act0 + (it - 1))
                        wge(t, self.dve.sem, dve_wgb + 2 * (it - 1))
                    ins = None
                    for dc in range(DC):
                        ins = t.matmul(gbank[:, 0:sn],
                                       wgc[:, par * D + dc * 128: par * D + (dc + 1) * 128],
                                       xg[:, dc * CAP + sc: dc * CAP + sc + sn],
                                       start=(dc == 0), stop=(dc == DC - 1))
                    self.pe.inc(ins)
                    ins = None
                    for dc in range(DC):
                        ins = t.matmul(ubank[:, 0:sn],
                                       wuc[:, par * D + dc * 128: par * D + (dc + 1) * 128],
                                       xg[:, dc * CAP + sc: dc * CAP + sc + sn],
                                       start=(dc == 0), stop=(dc == DC - 1))
                    self.pe.inc(ins)
        pe_up_end = self.pe.n

        @blk.scalar
        def _(s):
            if ex >= 1:
                s.wait_ge(self.dve.sem, prev["dve_up_end"])
            for icp in range(MIC):
                for si, (sc, sn) in enumerate(SCH):
                    it = icp * NS + si
                    par2 = it % 2
                    s.wait_ge(self.pe.sem, pe_wgb + 2 * it + 1)
                    if it >= 2:
                        wge(s, self.dve.sem, dve_wgb + 2 * (it - 1))
                    self.act.inc(s.activation(sgb[:, par2 * 512:par2 * 512 + sn],
                                              self.ps[0 + 2 * par2][:, 0:sn], AF.Silu))

        @blk.vector
        def _(v):
            v = VV(self, v)
            if ex >= 2:
                v.wait_ge(self.pe.sem, prev2["pe_dn_end"])
            for icp in range(MIC):
                for si, (sc, sn) in enumerate(SCH):
                    it = icp * NS + si
                    par2 = it % 2
                    v.wait_ge(self.pe.sem, pe_wgb + 2 * it + 2)
                    v.wait_ge(self.act.sem, act0 + it + 1)
                    v.tensor_tensor(tml[:, 0:sn], sgb[:, par2 * 512:par2 * 512 + sn],
                                    self.ps[1 + 2 * par2][:, 0:sn], ALU.mult)
                    v.tensor_tensor(hg[:, icp * CAP + sc: icp * CAP + sc + sn],
                                    tml[:, 0:sn], wgb[:, sc:sc + sn], ALU.mult)
        dve_up_end = self.dve.n

        @blk.gpsimd
        def _(g):
            if ex >= 1:
                wge(g, self.pe.sem, prev["pe_dn_end"])
            for icp in range(MIC):
                self.block_load(g, w2a[:, icp * D:(icp + 1) * D],
                                self.inp["w2T"], ex * MIC + icp, self.wlS)
        wlS_dn = self.wlS.n

        @blk.tensor
        def _(t):
            t.wait_ge(self.wlS.sem, wlS_dn)
            t.wait_ge(self.dve.sem, dve_up_end)
            for j in range(JT):
                for og in range(NOG):
                    i = j * NOG + og
                    bank = self.ps[5] if i % 2 == 0 else self.banks[7]
                    wge(t, self.dve.sem, dve_up_end + i - 1)
                    ins = None
                    for icp in range(MIC):
                        ins = t.matmul(bank[:, 0:512],
                                       hg[:, icp * CAP + j * 128: icp * CAP + (j + 1) * 128],
                                       w2a[:, icp * D + og * 512: icp * D + (og + 1) * 512],
                                       start=(icp == 0), stop=(icp == MIC - 1))
                    self.pe.inc(ins)
        pe_dn_end = self.pe.n

        @blk.vector
        def _(v):
            v = VV(self, v)
            for j in range(JT):
                for og in range(NOG):
                    i = j * NOG + og
                    par2 = i % 2
                    v.wait_ge(self.pe.sem, pe_up_end + i + 1)
                    wge(v, self.st4[2 + par2].sem, st_b2[par2] + 16 * (i // 2))
                    bank = self.ps[5] if i % 2 == 0 else self.banks[7]
                    v.tensor_copy(ev[:, par2 * 512:par2 * 512 + 512], bank[:, 0:512])
        dve_dn_end = self.dve.n

        @blk.sync
        def _(sp):
            for j in range(JT):
                for og in range(NOG):
                    i = j * NOG + og
                    par2 = i % 2
                    sp.wait_ge(self.dve.sem, dve_up_end + i + 1)
                    self.st4[2 + par2].inc(sp.dma_start(
                        out=self.contrib[ex * CAP + j * 128: ex * CAP + (j + 1) * 128,
                                         og * 512:(og + 1) * 512],
                        in_=ev[:, par2 * 512:par2 * 512 + 512]))

        return dict(pe_tr_end=pe_tr_end, dve_up_end=dve_up_end, pe_up_end=pe_up_end,
                    pe_dn_end=pe_dn_end, dve_dn_end=dve_dn_end)

    def p8_combine(self, cA, cB, x3r, yt):
        blk = self.block
        TC = self.TC
        pe0, dve0 = self.pe.n, self.dve.n
        ld_b = [self.ld[0].n, self.ld[1].n]
        wl_b = [self.wl[0].n, self.wl[1].n]
        st_b = [self.st4[0].n, self.st4[1].n]
        stC = [self.st4[2].n, self.st4[3].n]

        @blk.gpsimd
        def _(g):
            wge(g, self.st4[2].sem, stC[0])
            wge(g, self.st4[3].sem, stC[1])
            for tc in range(TC):
                par = tc % 2
                if tc >= 2:
                    wge(g, self.dve.sem, dve0 + 2 * (tc - 1))
                self.wl[par].inc(g.indirect_dma_start(
                    out=cA[:, par * D:(par + 1) * D], out_offset=None,
                    in_=self.contrib,
                    in_offset=IndirectOffsetOnAxis(self.slA_i[:, tc:tc + 1], 0)))
                self.wl[par].inc(g.indirect_dma_start(
                    out=cB[:, par * D:(par + 1) * D], out_offset=None,
                    in_=self.contrib,
                    in_offset=IndirectOffsetOnAxis(self.slB_i[:, tc:tc + 1], 0)))

        @blk.sync
        def _(sp):
            for tc in range(TC):
                par = tc % 2
                if tc >= 2:
                    wge(sp, self.dve.sem, dve0 + 2 * (tc - 1))
                self.ld[par].inc(sp.dma_start(
                    out=x3r[:, par * D:(par + 1) * D],
                    in_=self.x3d[tc * 128:(tc + 1) * 128, :]))
                if tc >= 2:
                    j = tc - 2
                    sp.wait_ge(self.dve.sem, dve0 + 2 * (j + 1))
                    self.st4[j % 2].inc(sp.dma_start(
                        out=self.y[j * 128:(j + 1) * 128, :],
                        in_=yt[:, (j % 2) * D:(j % 2 + 1) * D]))
            for j in (TC - 2, TC - 1):
                sp.wait_ge(self.dve.sem, dve0 + 2 * (j + 1))
                self.st4[j % 2].inc(sp.dma_start(
                    out=self.y[j * 128:(j + 1) * 128, :],
                    in_=yt[:, (j % 2) * D:(j % 2 + 1) * D]))

        @blk.vector
        def _(v):
            v = VV(self, v)
            for tc in range(TC):
                par = tc % 2
                v.wait_ge(self.wl[par].sem, wl_b[par] + 32 * (tc // 2 + 1))
                v.wait_ge(self.ld[par].sem, ld_b[par] + 16 * (tc // 2 + 1))
                wge(v, self.st4[par].sem, st_b[par] + 16 * (tc // 2))
                v.tensor_tensor(yt[:, par * D:(par + 1) * D], x3r[:, par * D:(par + 1) * D],
                                cA[:, par * D:(par + 1) * D], ALU.add)
                v.tensor_tensor(yt[:, par * D:(par + 1) * D], yt[:, par * D:(par + 1) * D],
                                cB[:, par * D:(par + 1) * D], ALU.add)

    def p7_shared_down(self, hsb, wd, sgbc, x2t, x3t, tml, x3s):
        blk = self.block
        T, TG = self.T, self.TG
        pe0, dve0 = self.pe.n, self.dve.n

        @blk.tensor
        def _(t):
            for tg in range(TG):
                if tg > 0:
                    t.wait_ge(self.dve.sem, dve0 + tg)
                ins = t.matmul(self.ps[4], self.c_orf[0:1, :],
                               self.sg_fm[0:1, tg * 512:tg * 512 + 512], start=True, stop=True)
                self.pe.inc(ins)

        @blk.vector
        def _(v):
            v = VV(self, v)
            for tg in range(TG):
                v.wait_ge(self.pe.sem, pe0 + tg + 1)
                v.tensor_copy(sgbc[:, tg * 512:tg * 512 + 512], self.ps[4])

        for tg in range(TG):
            tg_pe0, tg_dve0 = self.pe.n, self.dve.n
            ld_b = [self.ld[0].n, self.ld[1].n]
            st_b = [self.st4[0].n, self.st4[1].n]
            wl_b = [self.wl[0].n, self.wl[1].n]

            @blk.sync
            def _(sp):
                if tg > 0:
                    sp.wait_ge(self.pe.sem, tg_pe0)
                self.ldS.inc(sp.dma_start(
                    out=hsb[:, 0:SIC * 512].rearrange("p (ic t) -> p ic t", t=512),
                    in_=self.hshd.rearrange("(tgx ic p) t -> tgx p ic t", p=128, ic=SIC)[tg]))
                for oc in range(DC):
                    par2 = oc % 2
                    if oc >= 2:
                        wge(sp, self.dve.sem, tg_dve0 + 6 * (oc - 2) + 2)
                    self.ld[par2].inc(sp.dma_start(
                        out=x2t[:, par2 * 512:par2 * 512 + 512],
                        in_=self.x2d[oc * 128:(oc + 1) * 128, tg * 512:tg * 512 + 512]))
                for qs in range(4):
                    sp.wait_ge(self.dve.sem, tg_dve0 + 6 * DC)
                    self.st4[qs % 2].inc(sp.dma_start(
                        out=self.x3d[tg * 512 + qs * 128: tg * 512 + (qs + 1) * 128, :],
                        in_=x3s[:, qs * D:(qs + 1) * D]))
            ld_h = self.ldS.n

            @blk.gpsimd
            def _(gp):
                wge(gp, self.pe.sem, tg_pe0)
                for oc in range(DC):
                    par = oc % 2
                    if oc >= 2:
                        wge(gp, self.pe.sem, tg_pe0 + 5 * (oc - 2) + 1)
                    self.block_load(gp, wd[:, par * SIC * 128:(par + 1) * SIC * 128],
                                    self.inp["wdownT"], oc, self.wl[par])

            @blk.tensor
            def _(t):
                t.wait_ge(self.ldS.sem, ld_h)
                for oc in range(DC):
                    par = oc % 2
                    t.wait_ge(self.wl[par].sem, wl_b[par] + 16 * (oc // 2 + 1))
                    bank = self.ps[oc % 2]
                    if oc >= 2:
                        wge(t, self.dve.sem, tg_dve0 + 6 * (oc - 2) + 1)
                    ins = None
                    for ic in range(SIC):
                        ins = t.matmul(bank,
                                       wd[:, par * SIC * 128 + ic * 128: par * SIC * 128 + (ic + 1) * 128],
                                       hsb[:, ic * 512:(ic + 1) * 512],
                                       start=(ic == 0), stop=(ic == SIC - 1))
                    self.pe.inc(ins)
                    for qs in range(4):
                        tbank = self.psT[:, 0:128] if qs % 2 == 0 else self.psT[:, 128:256]
                        wge(t, self.dve.sem, tg_dve0 + 6 * oc + 2 + max(0, qs - 1))
                        ins = t.transpose(tbank.bitcast(F32)[:, 0:128] if False else
                                          (self.ps[5][:, 0:128] if qs % 2 == 0 else self.banks[7][:, 0:128]),
                                          x3t[:, (oc % 2) * 512 + qs * 128: (oc % 2) * 512 + (qs + 1) * 128],
                                          self.c_idf[:, :])
                        self.pe.inc(ins)

            @blk.vector
            def _(v):
                v = VV(self, v)
                for oc in range(DC):
                    par2 = oc % 2
                    v.wait_ge(self.pe.sem, tg_pe0 + 5 * oc + 1)
                    v.wait_ge(self.ld[par2].sem, ld_b[par2] + 16 * (oc // 2 + 1))
                    if tg > 0 and oc == 0:
                        for q2 in range(2):
                            wge(v, self.st4[q2].sem, st_b[q2])
                    bank = self.ps[oc % 2]
                    v.tensor_tensor(tml[:, par2 * 512:par2 * 512 + 512], bank,
                                    sgbc[:, tg * 512:tg * 512 + 512], ALU.mult)
                    v.tensor_tensor(x3t[:, par2 * 512:par2 * 512 + 512],
                                    tml[:, par2 * 512:par2 * 512 + 512],
                                    x2t[:, par2 * 512:par2 * 512 + 512], ALU.add)
                    for qs in range(4):
                        v.wait_ge(self.pe.sem, tg_pe0 + 5 * oc + 2 + qs)
                        src = self.ps[5][:, 0:128] if qs % 2 == 0 else self.banks[7][:, 0:128]
                        v.tensor_copy(x3s[:, qs * D + oc * 128: qs * D + (oc + 1) * 128], src)

        if "dbg_x3" in self.dbg:
            @blk.sync
            def _(sp):
                for c in self.st4:
                    c.wait(sp)
                self.st4[0].inc(sp.dma_start(out=self.dbg["dbg_x3"], in_=self.x3d))

    def final_wait(self):
        blk = self.block

        @blk.sync
        def _(sp):
            for c in self.st4:
                c.wait(sp)
            for c in self.ld:
                c.wait(sp)
            self.ldS.wait(sp)
            self.scl.wait(sp)

        @blk.gpsimd
        def _(g):
            for c in self.wl:
                c.wait(g)
            self.wlS.wait(g)
            self.sc.wait(g)
            for c in self.gl:
                c.wait(g)



def build_mk(L=8192, debug=()):
    nc = bass.Bass("TRN2", target_bir_lowering=False)
    mk = MK(nc, L, debug=debug)
    mk.build()
    return nc, mk


def host_prep(inputs, L=8192):
    f = np.float32
    T_ = L // 4
    TC_ = T_ // 128
    CAP_ = ((3 * T_ // 8 + 127) // 128) * 128
    d = {}
    def blockize(WT):
        rows, cols = WT.shape
        rc, nb = rows // 128, cols // 128
        w4 = WT.reshape(rc, 128, nb, 128)
        return np.ascontiguousarray(w4.transpose(2, 1, 0, 3).reshape(nb * 128, rc * 128))

    for p in "qkvo":
        W = np.asarray(inputs[f"L0_attn_W{p}"], f)
        d[f"w{p}T"] = blockize(np.ascontiguousarray(W.T))
        Bm = np.asarray(inputs[f"L0_attn_B{p}"], f)
        BT = np.ascontiguousarray(Bm.T)
        d[f"b{p}T_a"] = np.ascontiguousarray(
            BT.reshape(DC, 128, R).transpose(1, 0, 2).reshape(128, DC * R))
        Am = np.asarray(inputs[f"L0_attn_A{p}"], f)
        d[f"a{p}T"] = np.ascontiguousarray(Am.T)
    for p, key in [("q", "L0_attn_bq"), ("k", "L0_attn_bk"), ("o", "L0_attn_bo")]:
        b = np.asarray(inputs[key], f)
        d[f"b{p}_a"] = np.ascontiguousarray(b.reshape(DC, 128).T)
    d["bv_row"] = np.ascontiguousarray(np.asarray(inputs["L0_attn_bv"], f).reshape(1, D))
    gw9 = np.concatenate([np.asarray(inputs["L0_ffn_gate_w"], f),
                          np.asarray(inputs["L0_ffn_shared_gate_w"], f)], axis=0)
    gwT = gw9.T
    d["gw_arr"] = np.ascontiguousarray(gwT.reshape(DC, 128, 9).transpose(1, 0, 2).reshape(128, DC * 9))
    d["wguT"] = blockize(np.asarray(inputs["L0_ffn_shared_gate_up_w"], f).T)
    d["wdownT"] = blockize(np.asarray(inputs["L0_ffn_shared_down_w"], f).T)
    w13 = np.asarray(inputs["L0_ffn_w13"], f)
    d["w13T"] = blockize(np.ascontiguousarray(w13.transpose(2, 0, 1).reshape(D, NE * 2 * MI)))
    w2 = np.asarray(inputs["L0_ffn_w2"], f)
    d["w2T"] = np.ascontiguousarray(w2.transpose(0, 2, 1).reshape(NE * MI, D))
    d["ident_b"] = np.eye(128, dtype=BF)
    d["ident_f"] = np.eye(128, dtype=f)
    d["ones_col_b"] = np.ones((128, 1), dtype=BF)
    d["ones_row_b"] = np.ones((1, 128), dtype=BF)
    d["ones_row_f"] = np.ones((1, 128), dtype=f)
    tie = np.zeros((8, 8), f)
    for e_ in range(8):
        for j in range(8):
            tie[e_, j] = 1.0 if j < e_ else 0.0
    d["tie64"] = np.ascontiguousarray(np.tile(tie.reshape(1, 64), (128, 1)))
    sel = np.zeros((NE, NE * 128), f)
    for e_ in range(NE):
        sel[e_, e_ * 128:(e_ + 1) * 128] = 1.0
    d["sel8"] = sel
    tri = np.zeros((NE, NE), f)
    for j in range(NE):
        for e_ in range(NE):
            tri[j, e_] = 1.0 if j < e_ else 0.0
    d["tri8"] = tri
    d["ones8"] = np.ones((NE, 1), f)
    d["segb"] = (np.arange(NE, dtype=f) * CAP_).reshape(NE, 1)
    d["tokid"] = (np.arange(TC_, dtype=f)[None, :] * 128
                  + np.arange(128, dtype=f)[:, None]).astype(f)
    NSL_ = NE * CAP_
    tr2 = np.empty((NE, T_), f)
    for e_ in range(NE):
        tr2[e_, :] = NSL_ + (np.arange(T_) // 128) * NE + e_
    d["trashf"] = tr2
    return d


def make_xt(x, core, L=8192):
    T = L // 4
    b, q = core // 4, core % 4
    xb_ = np.asarray(x[b], np.float32)
    order = [q] + [i for i in range(4) if i != q]
    parts = [xb_[p * T:(p + 1) * T] for p in order]
    xr = np.concatenate(parts, axis=0)
    return np.ascontiguousarray(xr.T)


def kernel(**inputs):
    L = 8192
    T = L // 4
    nc, mk = build_mk(L=L)
    shared = host_prep(inputs, L=L)
    x = np.asarray(inputs["L0_attn_x"], np.float32)
    in_maps = []
    for c in range(8):
        m = dict(shared)
        m["xt_in"] = make_xt(x, c, L=L)
        in_maps.append(m)
    res = run_bass_kernel_spmd(nc, in_maps, core_ids=list(range(8)))
    out = np.empty((2, L, D), np.float32)
    for c in range(8):
        b, q = c // 4, c % 4
        out[b, q * T:(q + 1) * T, :] = res.results[c]["y"]
    return out
